# version 06
# baseline (speedup 1.0000x reference)
import numpy as np
import concourse.bass as bass
import concourse.mybir as mybir
from concourse.bass_utils import run_bass_kernel_spmd
from contextlib import ExitStack

F32 = mybir.dt.float32
BF16 = mybir.dt.bfloat16
I32 = mybir.dt.int32
AF = mybir.ActivationFunctionType
ALU = mybir.AluOpType
AX = mybir.AxisListType

P = 128
T = 2048
D = 1024
NT = T // P
H = 16
DH = 64
IDIM = 4096
LN2 = 0.6931471805599453
INV_LN2 = 1.4426950408889634
KB = 1024

ENGS = ("sync", "vector", "scalar", "tensor", "gpsimd")


class Sem:
    def __init__(self, nc, st, name):
        self.h = st.enter_context(nc.semaphore(name))
        self.v = 0


class Prog:
    def __init__(self, nc, st, paranoid=False):
        self.nc = nc
        self.ops = []
        self.paranoid = paranoid
        self.sem = {e: Sem(nc, st, "sem_" + e) for e in ENGS}
        self.dma = {q: Sem(nc, st, "dma_" + q) for q in ("sync", "gpsimd", "scalar")}

    def _par_waits(self, eng):
        w = [(s, s.v) for e, s in self.sem.items() if e != eng]
        w += [(s, s.v) for s in self.dma.values()]
        return w

    def op(self, eng, fn, waits=(), inc=True):
        if self.paranoid:
            waits = list(waits) + self._par_waits(eng)
        incs = []
        if inc or self.paranoid:
            s = self.sem[eng]
            s.v += 1
            incs.append((s, 1))
        self.ops.append((eng, tuple(waits), tuple(incs), fn))
        return self.sem[eng].v

    def dma_op(self, q, fn, waits=(), n=1):
        if self.paranoid:
            waits = list(waits) + self._par_waits(None)
        s = self.dma[q]

        def wrapped(eng, fn=fn, s=s):
            insts = fn(eng)
            for inst in insts:
                inst.then_inc(s.h, 16)
            return None

        s.v += 16 * n
        self.ops.append((q, tuple(waits), (), wrapped))
        return s.v

    def emit(self, block):
        for ename in ENGS:
            ops = [o for o in self.ops if o[0] == ename]

            def mk(ops=ops):
                def f(eng):
                    for (_, waits, incs, fn) in ops:
                        best = {}
                        for (s, val) in waits:
                            if val > best.get(id(s), (None, -1))[1]:
                                best[id(s)] = (s, val)
                        for (s, val) in best.values():
                            eng.wait_ge(s.h, val)
                        inst = fn(eng)
                        for (s, n) in incs:
                            inst.then_inc(s.h, n)
                return f
            getattr(block, ename)(mk())


class Union:
    def __init__(self, nc, st, name, bytes_per_partition):
        self.nc = nc
        self.backing = st.enter_context(
            nc.sbuf_tensor(name, [P, bytes_per_partition], mybir.dt.uint8))
        self.base = nc.lookup_mloc(self.backing).addr
        self.size = bytes_per_partition

    def at(self, off, name, shape, dtype):
        per_part = int(np.prod(shape[1:])) * mybir.dt.size(dtype)
        assert off % 32 == 0 and off + per_part <= self.size, (name, off, per_part)
        return self.nc.alloc_sbuf_tensor_at(name, shape, dtype,
                                            offset=self.base + off)


def build_program(paranoid=False, debug_taps=False, stop_after='E'):
    nc = bass.Bass("TRN2", target_bir_lowering=False, debug=False,
                   detect_race_conditions=False)
    st = ExitStack()
    E = st.enter_context

    def din(name, shape, dt=F32):
        return nc.dram_tensor(name, shape, dt, kind="ExternalInput").ap()

    x_h = din("x", [T, D])
    wq_h = din("wq", [D, D]); wk_h = din("wk", [D, D]); wv_h = din("wv", [D, D])
    wg_h = din("wg", [D, D]); wout_h = din("wout", [D, D])
    wgk1_h = din("wgk1", [D, 16])
    w2b_h = din("w2b", [17, D])
    gatew_h = din("gatew", [IDIM, D]); upw_h = din("upw", [IDIM, D])
    downw_h = din("downw", [D, IDIM])
    gamma_h = din("gamma_bc", [P, D]); beta_h = din("beta_bc", [P, D])
    sx_h = din("sx_col", [P, 1]); sd_h = din("sd_col", [P, 1])
    tri16_h = din("tri16", [P, P]); cmask_h = din("cmask", [P, P])
    oneh128_h = din("oneh128", [P, P]); onecol_h = din("onecol", [1, P])
    identbf_h = din("identbf", [P, P])

    y_h = nc.dram_tensor("y", [T, D], F32, kind="ExternalOutput").ap()
    scr_sig = nc.dram_tensor("scr_sig", [T, D], F32, kind="Internal").ap()
    scr_g = nc.dram_tensor("scr_g", [T, D], F32, kind="Internal").ap()
    scr_sg = nc.dram_tensor("scr_sg", [T, D], F32, kind="Internal").ap()
    scr_sout = nc.dram_tensor("scr_sout", [T, D], F32, kind="Internal").ap()
    scr_gp = nc.dram_tensor("scr_gp", [T, D], F32, kind="Internal").ap()

    TMP = [E(nc.sbuf_tensor(f"tmp{i}", [P, D], F32)) for i in range(5)]
    TMPB = [E(nc.sbuf_tensor(f"tmpb{i}", [P, D], BF16)) for i in range(2)]
    TMPX = E(nc.sbuf_tensor("tmpx", [P, D], F32))
    xq_fm = E(nc.sbuf_tensor("xq_fm", [P, 8 * T], BF16))
    tri16 = E(nc.sbuf_tensor("tri16_sb", [P, P], F32))
    cmask = E(nc.sbuf_tensor("cmask_sb", [P, P], F32))
    oneh128 = E(nc.sbuf_tensor("oneh128_sb", [P, P], F32))
    identbf = E(nc.sbuf_tensor("identbf_sb", [P, P], F32))
    onecol = E(nc.sbuf_tensor("onecol_sb", [1, P], F32))
    gamma_bc = E(nc.sbuf_tensor("gamma_bc_sb", [P, D], F32))
    beta_bc = E(nc.sbuf_tensor("beta_bc_sb", [P, D], F32))
    wgk1_bf = E(nc.sbuf_tensor("wgk1_bf", [P, 8 * 16], BF16))
    w2b_bf = E(nc.sbuf_tensor("w2b_bf", [17, D], BF16))
    rbf = E(nc.sbuf_tensor("rbf", [32, T], BF16))
    stateT = E(nc.sbuf_tensor("stateT", [P, H * DH], BF16))
    xnbf = E(nc.sbuf_tensor("xnbf", [P, D], BF16))

    onerow = E(nc.sbuf_tensor("onerow", [1, P + 4], F32))
    SC8 = E(nc.sbuf_tensor("sc8t", [P, 256], F32))
    AMX = E(nc.sbuf_tensor("amxt", [P, 256], F32))
    G8 = E(nc.sbuf_tensor("g8t", [P, 256], F32))
    SCT = [E(nc.sbuf_tensor(f"sct{i}", [P, 256], F32)) for i in range(4)]
    SCI = E(nc.sbuf_tensor("sci", [P, D], I32))
    TI = SCI
    COLS = {}
    for nm in ["sx", "sd", "k_x", "k_h", "lngs_x", "lngs_h",
               "gm_g", "gm_u", "gm_d", "gs_g", "gs_u", "gs_d",
               "k_g", "k_u", "k_d", "lngs_g", "lngs_u", "lngs_d",
               "a_gate", "a_gu", "a_d", "c0", "c1", "c2", "c3", "c4", "m16"]:
        COLS[nm] = E(nc.sbuf_tensor("col_" + nm, [P, 1], F32))

    U = Union(nc, st, "union", 114 * KB)
    xn_fm = U.at(0, "xn_fm", [P, 8 * T], BF16)
    WQ = U.at(32 * KB, "WQ", [P, 8 * D], BF16)
    WK = U.at(48 * KB, "WK", [P, 8 * D], BF16)
    WV = U.at(64 * KB, "WV", [P, 8 * D], BF16)
    WO = U.at(80 * KB, "WO", [P, 8 * D], BF16)
    WG = U.at(80 * KB, "WG", [P, 8 * D], BF16)
    qfm = U.at(96 * KB, "qfm", [P, 8 * P], BF16)
    kfm = U.at(98 * KB, "kfm", [P, 8 * P], BF16)
    ktm = U.at(100 * KB, "ktm", [P, D], BF16)
    vtm = U.at(102 * KB, "vtm", [P, D], BF16)
    vt_t = U.at(104 * KB, "vt_t", [P, D], BF16)
    A_c = U.at(106 * KB, "A_c", [P, D], F32)
    gch = U.at(110 * KB, "gch", [P, D], F32)
    Ainv = U.at(110 * KB, "Ainv", [P, D], F32)
    sgch = U.at(100 * KB, "sgch", [P, D], F32)
    mskt = U.at(113 * KB, "mskt", [P, 4 * P], BF16)
    hq_fm = U.at(0, "hq_fm", [P, 16 * T], BF16)
    dwq = U.at(64 * KB, "dwq", [P, 16 * D], BF16)
    hid_c = U.at(64 * KB, "hid_c", [P, NT * 256], F32)
    gwq = U.at(80 * KB, "gwq", [P, 8 * 256], BF16)
    uwq = U.at(84 * KB, "uwq", [P, 8 * 256], BF16)
    wstr = [U.at((96 + 4 * i) * KB, f"wstr{i}", [P, D], F32) for i in range(4)]

    PS = [E(nc.psum_tensor(f"ps{i}", [P, 512], F32)) for i in range(8)]

    p = Prog(nc, st, paranoid=paranoid)
    V = lambda: p.sem["vector"].v
    S = lambda: p.sem["scalar"].v
    TE = lambda: p.sem["tensor"].v
    DS = lambda: p.dma["sync"].v
    DG = lambda: p.dma["gpsimd"].v

    def vop(fn, w=()):
        return p.op("vector", fn, waits=w)

    def sop(fn, w=()):
        return p.op("scalar", fn, waits=w)

    def vopc(fn, w=()):
        return p.op("vector", fn, waits=list(w) + [(p.sem["vector"], V())])

    def sopc(fn, w=()):
        return p.op("scalar", fn, waits=list(w) + [(p.sem["scalar"], S())])

    def top(fn, w=()):
        return p.op("tensor", fn, waits=w)

    def load(dst, src, w=(), q="sync"):
        return p.dma_op(q, lambda e: [e.dma_start(out=dst, in_=src)], waits=w)

    def store(dst, src, w=(), q="sync"):
        return p.dma_op(q, lambda e: [e.dma_start(out=dst, in_=src)], waits=w)

    def loads(pairs, w=(), q="sync"):
        return p.dma_op(q, lambda e: [e.dma_start(out=d, in_=s) for d, s in pairs],
                        waits=w, n=len(pairs))

    pexp_state = {"ev": 0}

    def pe_xpose(pairs, ps, w=()):
        first = True
        for g0 in range(0, len(pairs), 4):
            grp = pairs[g0:g0 + 4]
            for j, (dst, srcp) in enumerate(grp):
                ww = []
                if first:
                    ww = list(w)
                    first = False
                if pexp_state["ev"]:
                    ww.append((p.sem["scalar"], pexp_state["ev"]))
                    pexp_state["ev"] = 0
                top(lambda e, j=j, srcp=srcp, ps=ps: e.transpose(
                    ps[:, j * P:(j + 1) * P], srcp, identbf[:]), w=ww)
            tv = TE()
            for j, (dst, srcp) in enumerate(grp):
                sop(lambda e, j=j, dst=dst, ps=ps: e.copy(dst, ps[:, j * P:(j + 1) * P]),
                    w=[(p.sem["tensor"], tv)] if j == 0 else ())
            pexp_state["ev"] = S()
        return TE(), S()

    def xbars(pairs, w=(), q="sync"):
        return p.dma_op(q, lambda e: [e.dma_start_transpose(d, s)
                                      for d, s in pairs], waits=w, n=len(pairs))

    def scale_pipe(amax_ap, nb, kcol, lncol, sc8_ap, g8_ap, wv):
        W = 256
        s0, s1, s2, s3 = (t[:, 0:W] for t in SCT)
        si = SCI[:, 0:W]
        amx = AMX[:, 0:W]
        vop(lambda e: e.tensor_scalar(s0, amx, COLS[kcol][:, 0:1], 448.0,
                                      ALU.mult, ALU.min), w=wv)
        vop(lambda e: e.tensor_scalar(s0, s0, 2.0 ** -20, None, ALU.max))
        sop(lambda e: e.activation(s1, s0, AF.Ln), w=[(p.sem["vector"], V())])
        vop(lambda e: e.tensor_scalar(s2, s1, INV_LN2, None, ALU.mult),
            w=[(p.sem["scalar"], S())])
        vop(lambda e: e.tensor_scalar(si, s2, 32.0, None, ALU.add))
        vop(lambda e: e.tensor_scalar(s3, si, -32.0, None, ALU.add))
        vop(lambda e: e.tensor_tensor(s1, s3, s2, ALU.is_gt))
        vop(lambda e: e.tensor_tensor(s3, s3, s1, ALU.subtract))
        sop(lambda e: e.activation(s2, s3, AF.Exp, scale=-LN2),
            w=[(p.sem["vector"], V())])
        vop(lambda e: e.tensor_tensor(s2, s0, s2, ALU.mult),
            w=[(p.sem["scalar"], S())])
        vop(lambda e: e.tensor_scalar(s0, s2, 8.0, 0.5, ALU.mult, ALU.add))
        vop(lambda e: e.tensor_copy(si, s0))
        vop(lambda e: e.tensor_copy(s2, si))
        vop(lambda e: e.tensor_tensor(s1, s2, s0, ALU.is_gt))
        vop(lambda e: e.tensor_tensor(s2, s2, s1, ALU.subtract))
        sop(lambda e: e.activation(s1, s3, AF.Exp, scale=LN2),
            w=[(p.sem["vector"], V())])
        vop(lambda e: e.scalar_tensor_tensor(SC8[:, 0:W], s2, 0.125, s1, ALU.mult,
                                             ALU.mult), w=[(p.sem["scalar"], S())])
        sop(lambda e: e.activation(s0, SC8[:, 0:W], AF.Ln),
            w=[(p.sem["vector"], V())])
        sop(lambda e: e.activation(G8[:, 0:W], s0, AF.Exp, scale=-1.0,
                                   bias=COLS[lncol][:, 0:1]))

    GQ = {"prev": 0}

    def quant_hot(src_ap, out_ap, nb, sc8_ap, g8_ap, ws, clip, out_ws=None):
        if out_ws is None:
            out_ws = [(p.dma["sync"], DS()),
                      (p.dma["scalar"], p.dma["scalar"].v)]
        F = 16 * nb
        t0, t1, t2, t3 = (t[:, 0:F] for t in TMP[0:4])
        b0, b1 = (t[:, 0:F] for t in TMPB)
        b2 = TMP[4][:, 0:F]
        ti = TI[:, 0:F]
        src3 = src_ap.rearrange("p (b k) -> p b k", k=16)
        g83 = g8_ap.broadcast_to([P, nb, 16])
        sc83 = sc8_ap.broadcast_to([P, nb, 16])
        vop(lambda e: e.tensor_tensor(t0.rearrange("p (b k) -> p b k", k=16),
                                      src3, g83, ALU.mult),
            w=list(ws) + [(p.sem["scalar"], S())])
        if clip:
            vop(lambda e: e.tensor_scalar(t0, t0, 6.0, -6.0, ALU.min, ALU.max))
        sop(lambda e: e.activation(b0, t0, AF.Square), w=[(p.sem["vector"], V())])
        vop(lambda e: e.tensor_scalar(b1, b0, 4.0, 0.5, ALU.is_ge, ALU.mult),
            w=[(p.sem["scalar"], S())])
        vop(lambda e: e.scalar_tensor_tensor(b2, b0, 16.0, b1, ALU.is_ge, ALU.add))
        vop(lambda e: e.tensor_scalar(b0, b2, -0.5, 2.0, ALU.mult, ALU.add))
        vop(lambda e: e.scalar_tensor_tensor(b0, b1, -1.5, b0, ALU.mult, ALU.add))
        vop(lambda e: e.tensor_tensor(t1, t0, b0, ALU.mult))
        vop(lambda e: e.tensor_scalar(ti, t1, 16.5, None, ALU.add))
        vop(lambda e: e.tensor_scalar(t2, ti, -16.0, None, ALU.add))
        vop(lambda e: e.scalar_tensor_tensor(t3, t1, 0.5, t2, ALU.add, ALU.is_lt),
            w=[(p.sem["gpsimd"], GQ["prev"])] if GQ["prev"] else ())
        vop(lambda e: e.tensor_tensor(t2, t2, t3, ALU.subtract))
        vop(lambda e: e.scalar_tensor_tensor(t3, b2, 0.5, t2, ALU.add, ALU.mult))
        vv = V()
        GQ["prev"] = p.op("gpsimd", lambda e: e.tensor_tensor(
            out_ap.rearrange("p (b k) -> p b k", k=16),
            t3.rearrange("p (b k) -> p b k", k=16),
            sc83, ALU.mult),
            waits=[(p.sem["vector"], vv)] + list(out_ws))
        return GQ["prev"]


    def amax_to(src_ap, nb, off, ws=()):
        return vop(lambda e: e.tensor_reduce(
            AMX[:, off:off + nb], src_ap.rearrange("p (b k) -> p b k", k=16),
            AX.X, ALU.max, apply_absolute_value=True), w=ws)

    def quantize_tile(src_ap, out_ap, nb, kcol, lncol, ws, clip):
        amax_to(src_ap, nb, 0, ws=ws)
        scale_pipe(None, nb, kcol, lncol, None, None,
                   wv=[(p.sem["vector"], V())])
        return quant_hot(src_ap, out_ap, nb, SC8[:, 0:nb], G8[:, 0:nb],
                         ws=[(p.sem["vector"], V())], clip=clip)

    loads([(tri16[:], tri16_h), (cmask[:], cmask_h), (oneh128[:], oneh128_h),
           (onecol[:], onecol_h), (gamma_bc[:], gamma_h), (beta_bc[:], beta_h),
           (COLS["sx"][:], sx_h), (COLS["sd"][:], sd_h), (identbf[:], identbf_h)])
    d_consts = DS()
    for W_, src in ((WQ, wq_h), (WK, wk_h), (WV, wv_h)):
        for kd in range(8):
            load(W_[:, kd * D:(kd + 1) * D], src[kd * P:(kd + 1) * P, :], q="gpsimd")
    for kd in range(8):
        load(wgk1_bf[:, kd * 16:(kd + 1) * 16], wgk1_h[kd * P:(kd + 1) * P, :],
             q="gpsimd")
    load(w2b_bf[:], w2b_h, q="gpsimd")
    d_attnw = DG()

    vop(lambda e: e.memset(AMX[:, :], 1.0))
    vop(lambda e: e.memset(COLS["m16"][:, 0:1], -16.0))
    vopc(lambda e: e.tensor_scalar(COLS["k_x"][:, 0:1], COLS["sx"][:, 0:1],
                                   1.0 / 6.0, None, ALU.mult),
         w=[(p.dma["sync"], d_consts)])
    vopc(lambda e: e.tensor_scalar(COLS["k_h"][:, 0:1], COLS["sd"][:, 0:1],
                                   1.0 / 6.0, None, ALU.mult))
    sop(lambda e: e.activation(COLS["lngs_x"][:, 0:1], COLS["sx"][:, 0:1], AF.Ln),
        w=[(p.sem["vector"], V())])
    sop(lambda e: e.activation(COLS["lngs_h"][:, 0:1], COLS["sd"][:, 0:1], AF.Ln))

    def gmax_stream(tiles, gm):
        vop(lambda e: e.memset(COLS[gm][:, 0:1], 0.0))
        for i, src in enumerate(tiles):
            buf = wstr[i % 2]
            w_ = src.shape[1]
            dv = load(buf[:, 0:w_], src, w=[(p.sem["vector"], V())])
            vop(lambda e, b=buf, w2=w_: e.tensor_reduce(
                COLS["c4"][:, 0:1], b[:, 0:w2], AX.X, ALU.max,
                apply_absolute_value=True), w=[(p.dma["sync"], dv)])
            vopc(lambda e, g=gm: e.tensor_tensor(COLS[g][:, 0:1], COLS[g][:, 0:1],
                                                 COLS["c4"][:, 0:1], ALU.max))
        vopc(lambda e, g=gm: e.tensor_copy(
            TMP[1][:, 0:32].rearrange("p (one k) -> p one k", k=32),
            COLS[g][:, 0:1].broadcast_to([P, 1, 32])))
        for j in range(4):
            vopc(lambda e, j=j: e.transpose(TMP[0][0:32, 32 * j:32 * (j + 1)],
                                            TMP[1][32 * j:32 * (j + 1), 0:32]))
        vopc(lambda e: e.tensor_reduce(onerow[0:1, P:P + 1], TMP[0][0:1, 0:P],
                                      AX.X, ALU.max))
        vopc(lambda e: e.reciprocal(onerow[0:1, P + 1:P + 2], onerow[0:1, P:P + 1]))
        vopc(lambda e: e.tensor_scalar(onerow[0:1, P + 2:P + 3],
                                      onerow[0:1, P + 1:P + 2], 2688.0, None,
                                      ALU.mult))

    def gs_bcast(gs, kc, lnc):
        tv = top(lambda e: e.matmul(PS[0][:, 0:1], onecol[0:1, 0:P],
                                    onerow[0:1, P + 2:P + 3], start=True, stop=True),
                 w=[(p.sem["vector"], V())])
        vopc(lambda e: e.tensor_copy(COLS[gs][:, 0:1], PS[0][:, 0:1]),
             w=[(p.sem["tensor"], tv)])
        vopc(lambda e: e.tensor_scalar(COLS[kc][:, 0:1], COLS[gs][:, 0:1],
                                       1.0 / 6.0, None, ALU.mult))
        sopc(lambda e: e.activation(COLS[lnc][:, 0:1], COLS[gs][:, 0:1], AF.Ln),
             w=[(p.sem["vector"], V())])

    gmax_stream([gatew_h[i * P:(i + 1) * P, :] for i in range(32)], "gm_g")
    gs_bcast("gs_g", "k_g", "lngs_g")
    gmax_stream([upw_h[i * P:(i + 1) * P, :] for i in range(32)], "gm_u")
    gs_bcast("gs_u", "k_u", "lngs_u")
    gmax_stream([downw_h[i * P:(i + 1) * P, j * D:(j + 1) * D]
                 for i in range(8) for j in range(4)], "gm_d")
    gs_bcast("gs_d", "k_d", "lngs_d")

    def recip_into(dst, a, b):
        vopc(lambda e: e.tensor_tensor(COLS["c4"][:, 0:1], COLS[a][:, 0:1],
                                       COLS[b][:, 0:1], ALU.mult))
        vopc(lambda e: e.reciprocal(COLS[dst][:, 0:1], COLS["c4"][:, 0:1]))

    recip_into("a_gate", "sx", "gs_g")
    recip_into("c0", "sx", "gs_u")
    vopc(lambda e: e.tensor_tensor(COLS["a_gu"][:, 0:1], COLS["a_gate"][:, 0:1],
                                   COLS["c0"][:, 0:1], ALU.mult))
    recip_into("a_d", "sd", "gs_d")

    xq_rd = [0, 0]
    xn_rd = 0
    for tt in range(NT):
        xb = TMP[1]
        dv = load(xb[:], x_h[tt * P:(tt + 1) * P, :],
                  w=[(p.sem["vector"], V()), (p.sem["scalar"], S()),
                     (p.sem["tensor"], max(xq_rd[0], xq_rd[1])),
                     (p.sem["gpsimd"], p.sem["gpsimd"].v)])
        vopc(lambda e, xb=xb: e.reduce_sum(COLS["c0"][:, 0:1], xb[:], AX.X),
             w=[(p.dma["sync"], dv)])
        sop(lambda e, xb=xb: e.activation(TMP[0][:], xb[:], AF.Square),
            w=[(p.sem["vector"], V())])
        vopc(lambda e: e.reduce_sum(COLS["c1"][:, 0:1], TMP[0][:], AX.X),
             w=[(p.sem["scalar"], S())])
        vopc(lambda e: e.tensor_scalar(COLS["c0"][:, 0:1], COLS["c0"][:, 0:1],
                                       1.0 / D, None, ALU.mult))
        vopc(lambda e: e.tensor_tensor(COLS["c2"][:, 0:1], COLS["c0"][:, 0:1],
                                       COLS["c0"][:, 0:1], ALU.mult))
        vopc(lambda e: e.scalar_tensor_tensor(COLS["c1"][:, 0:1], COLS["c1"][:, 0:1],
                                              1.0 / D, COLS["c2"][:, 0:1],
                                              ALU.mult, ALU.subtract))
        vopc(lambda e: e.tensor_scalar(COLS["c1"][:, 0:1], COLS["c1"][:, 0:1],
                                       1e-5, None, ALU.add))
        sopc(lambda e: e.activation(COLS["c2"][:, 0:1], COLS["c1"][:, 0:1], AF.Ln),
             w=[(p.sem["vector"], V())])
        sopc(lambda e: e.activation(COLS["c2"][:, 0:1], COLS["c2"][:, 0:1],
                                    AF.Exp, scale=-0.5))
        vopc(lambda e: e.tensor_tensor(COLS["c3"][:, 0:1], COLS["c0"][:, 0:1],
                                       COLS["c2"][:, 0:1], ALU.mult),
             w=[(p.sem["scalar"], S())])
        vopc(lambda e: e.tensor_scalar(COLS["c3"][:, 0:1], COLS["c3"][:, 0:1],
                                       -1.0, None, ALU.mult))
        sopc(lambda e, xb=xb: e.activation(TMP[4][:], xb[:], AF.Identity,
                                           scale=COLS["c2"][:, 0:1],
                                           bias=COLS["c3"][:, 0:1]),
             w=[(p.sem["vector"], V())])
        sv_xn = S()
        xn_rd, _ = pe_xpose([(xn_fm[:, kd * T + tt * P: kd * T + (tt + 1) * P],
                              TMP[4][:, kd * P:(kd + 1) * P]) for kd in range(8)],
                            PS[0], w=[(p.sem["scalar"], sv_xn),
                                      (p.dma["gpsimd"], d_attnw)])
        amax_to(TMP[4][:], 64, 0, ws=[(p.sem["scalar"], sv_xn)])
        scale_pipe(None, 64, "k_x", "lngs_x", None, None,
                   wv=[(p.sem["vector"], V())])
        qb = (TMP[1], TMPX)[tt % 2]
        gq = quant_hot(TMP[4][:], qb[:], 64, SC8[:, 0:64], G8[:, 0:64],
                       ws=[(p.sem["vector"], V()), (p.sem["tensor"], xn_rd)],
                       clip=True,
                       out_ws=([(p.sem["tensor"], xq_rd[tt % 2])]
                               if xq_rd[tt % 2] else []))
        xq_rd[tt % 2], _ = pe_xpose(
            [(xq_fm[:, kd * T + tt * P: kd * T + (tt + 1) * P],
              qb[:, kd * P:(kd + 1) * P]) for kd in range(8)],
            PS[1], w=[(p.sem["gpsimd"], gq)])
    s_xnfm = S()

    if stop_after == 'A':
        return _finish(nc, p, st, debug_taps, scr_sig, scr_g, scr_sg, scr_sout,
                       scr_gp, xq_fm, xn_fm, store, V, DS, DG, T, D)
    vop(lambda e: e.memset(rbf[0:32, :], 1.0))
    v_rones = V()
    for q4 in range(4):
        for kd in range(8):
            top(lambda e, q4=q4, kd=kd: e.matmul(
                PS[0][0:16, 0:512], wgk1_bf[:, kd * 16:(kd + 1) * 16],
                xn_fm[:, kd * T + q4 * 512: kd * T + (q4 + 1) * 512],
                start=(kd == 0), stop=(kd == 7)),
                w=[(p.sem["scalar"], s_xnfm), (p.sem["vector"], V())]
                if kd == 0 else ())
        vop(lambda e, q4=q4: e.tensor_copy(rbf[0:16, q4 * 512:(q4 + 1) * 512],
                                           PS[0][0:16, 0:512]),
            w=[(p.sem["tensor"], TE())])
    v_rbf = V()
    for kd in range(8):
        load(WG[:, kd * D:(kd + 1) * D], wg_h[kd * P:(kd + 1) * P, :], q="gpsimd")
    d_wg = DG()

    for tt in range(NT):
        tv0 = TE()
        for half in range(2):
            top(lambda e, tt=tt, half=half: e.matmul(
                PS[half][:, 0:512], rbf[0:17, tt * P:(tt + 1) * P],
                w2b_bf[0:17, half * 512:(half + 1) * 512], start=True, stop=True),
                w=[(p.sem["vector"], max(v_rbf, V())), (p.sem["scalar"], S())]
                if half == 0 else ())
        for half in range(2):
            for kd in range(8):
                top(lambda e, tt=tt, half=half, kd=kd: e.matmul(
                    PS[2 + half][:, 0:512], xn_fm[:, kd * T + tt * P:
                                                  kd * T + (tt + 1) * P],
                    WG[:, kd * D + half * 512: kd * D + (half + 1) * 512],
                    start=(kd == 0), stop=(kd == 7)),
                    w=[(p.dma["gpsimd"], d_wg)] if (tt == 0 and half == 0
                                                    and kd == 0) else ())
        tv = TE()
        sop(lambda e: e.activation(TMP[0][:, 0:512], PS[0][:, 0:512], AF.Sigmoid),
            w=[(p.sem["tensor"], tv), (p.dma["sync"], DS())])
        sop(lambda e: e.activation(TMP[0][:, 512:D], PS[1][:, 0:512], AF.Sigmoid))
        sop(lambda e: e.activation(TMP[1][:, 0:512], PS[2][:, 0:512], AF.Sigmoid))
        sop(lambda e: e.activation(TMP[1][:, 512:D], PS[3][:, 0:512], AF.Sigmoid))
        sv = S()
        store(scr_sig[tt * P:(tt + 1) * P, :], TMP[0][:],
              w=[(p.sem["scalar"], sv)])
        vop(lambda e: e.tensor_tensor(TMP[2][:, 0:512], PS[2][:, 0:512],
                                      TMP[1][:, 0:512], ALU.mult),
            w=[(p.sem["scalar"], sv), (p.dma["sync"], DS())])
        vop(lambda e: e.tensor_tensor(TMP[2][:, 512:D], PS[3][:, 0:512],
                                      TMP[1][:, 512:D], ALU.mult))
        vv = V()
        store(scr_sg[tt * P:(tt + 1) * P, :], TMP[2][:], w=[(p.sem["vector"], vv)])
    for tt in range(NT):
        dv = load(TMP[3][:], scr_sig[tt * P:(tt + 1) * P, :],
                  w=[(p.sem["scalar"], S())])
        sop(lambda e: e.activation(TMP[0][:], TMP[3][:], AF.Ln),
            w=[(p.dma["sync"], max(dv, DS()))])
        sv = S()
        store(scr_g[tt * P:(tt + 1) * P, :], TMP[0][:], w=[(p.sem["scalar"], sv)])
    d_gates = DS()

    if stop_after == 'B':
        return _finish(nc, p, st, debug_taps, scr_sig, scr_g, scr_sg, scr_sout,
                       scr_gp, xq_fm, xn_fm, store, V, DS, DG, T, D)
    for kd in range(8):
        load(WO[:, kd * D:(kd + 1) * D], wout_h[kd * P:(kd + 1) * P, :], q="gpsimd",
             w=[(p.sem["tensor"], TE())])
    d_wo = DG()
    vop(lambda e: e.memset(stateT[:, :], 0.0))
    v_state = V()
    d_dup = DS()
    v_wout_prev = 0
    v_oevac_prev = 0
    v_vt_prev = 0

    for c in range(NT):
        tc = c * P
        dv_g = load(gch[:], scr_g[tc:tc + P, :],
                    w=[(p.dma["sync"], d_gates), (p.sem["tensor"], TE()),
                       (p.sem["vector"], v_vt_prev if c > 0 else 0)])
        for half in range(2):
            for kd in range(8):
                top(lambda e, half=half, kd=kd, tc=tc: e.matmul(
                    PS[half][:, 0:512],
                    xn_fm[:, kd * T + tc: kd * T + tc + P],
                    WV[:, kd * D + half * 512: kd * D + (half + 1) * 512],
                    start=(kd == 0), stop=(kd == 7)),
                    w=[(p.sem["vector"], v_state), (p.dma["sync"], d_dup)]
                    if (half == 0 and kd == 0) else ())
        tv = TE()
        vop(lambda e: e.tensor_copy(vtm[:, 0:512], PS[0][:, 0:512]),
            w=[(p.sem["tensor"], tv)])
        vop(lambda e: e.tensor_copy(vtm[:, 512:D], PS[1][:, 0:512]))
        v_vtm = V()
        for oc in range(8):
            for kd in range(8):
                top(lambda e, oc=oc, kd=kd, tc=tc: e.matmul(
                    PS[2 + oc // 4][:, (oc % 4) * P:(oc % 4 + 1) * P],
                    WQ[:, kd * D + oc * P: kd * D + (oc + 1) * P],
                    xn_fm[:, kd * T + tc: kd * T + tc + P],
                    start=(kd == 0), stop=(kd == 7)),
                    w=[(p.sem["vector"], v_wout_prev)]
                    if (oc == 0 and kd == 0 and v_wout_prev) else ())
        tv = TE()
        vop(lambda e: e.tensor_copy(qfm[:, 0:512], PS[2][:, 0:512]),
            w=[(p.sem["tensor"], tv)])
        vop(lambda e: e.tensor_copy(qfm[:, 512:D], PS[3][:, 0:512]))
        v_qfm = V()
        for oc in range(8):
            for kd in range(8):
                top(lambda e, oc=oc, kd=kd, tc=tc: e.matmul(
                    PS[4 + oc // 4][:, (oc % 4) * P:(oc % 4 + 1) * P],
                    WK[:, kd * D + oc * P: kd * D + (oc + 1) * P],
                    xn_fm[:, kd * T + tc: kd * T + tc + P],
                    start=(kd == 0), stop=(kd == 7)),
                    w=[(p.sem["vector"], max(v_qfm, v_oevac_prev))]
                    if (oc == 0 and kd == 0) else ())
        tv = TE()
        vop(lambda e: e.tensor_copy(kfm[:, 0:512], PS[4][:, 0:512]),
            w=[(p.sem["tensor"], tv)])
        vop(lambda e: e.tensor_copy(kfm[:, 512:D], PS[5][:, 0:512]))
        v_kfm = V()
        d_ktm = xbars([(ktm[:, oc * P:(oc + 1) * P], kfm[:, oc * P:(oc + 1) * P])
                       for oc in range(8)], w=[(p.sem["vector"], v_kfm)])
        for half in range(2):
            top(lambda e, half=half: e.matmul(
                PS[6 + half][:, 0:512], tri16[:, :],
                gch[:, half * 512:(half + 1) * 512], start=True, stop=True),
                w=[(p.dma["sync"], dv_g)] if half == 0 else ())
        tv = TE()
        sop(lambda e: e.activation(A_c[:, 0:512], PS[6][:, 0:512], AF.Exp),
            w=[(p.sem["tensor"], tv)])
        sop(lambda e: e.activation(A_c[:, 512:D], PS[7][:, 0:512], AF.Exp))
        sop(lambda e: e.activation(Ainv[:, 0:512], PS[6][:, 0:512], AF.Exp,
                                   scale=-1.0))
        sop(lambda e: e.activation(Ainv[:, 512:D], PS[7][:, 0:512], AF.Exp,
                                   scale=-1.0))
        sv_gates = S()
        for half in range(2):
            top(lambda e, half=half: e.matmul(
                PS[half][:, 0:512], oneh128[:, :],
                A_c[:, half * 512:(half + 1) * 512], start=True, stop=True),
                w=[(p.sem["scalar"], sv_gates), (p.sem["vector"], v_vtm)]
                if half == 0 else ())
        tv_acb = TE()
        vop(lambda e: e.tensor_tensor(vt_t[:, 0:512], vtm[:, 0:512],
                                      Ainv[:, 0:512], ALU.mult),
            w=[(p.sem["scalar"], sv_gates)])
        vop(lambda e: e.tensor_tensor(vt_t[:, 512:D], vtm[:, 512:D],
                                      Ainv[:, 512:D], ALU.mult))
        v_vt = V()
        v_vt_prev = v_vt
        msk_done = [0] * 4
        intra_tv = [0] * 4
        for h in range(H):
            sl = h % 4
            rb = 64 * (h % 2)
            cb = (h // 2) * P
            w_s = []
            if h == 0:
                w_s.append((p.sem["vector"], v_kfm))
            if msk_done[sl]:
                w_s.append((p.sem["vector"], msk_done[sl]))
            tv_s = top(lambda e, sl=sl, rb=rb, cb=cb: e.matmul(
                PS[2][:, sl * P:(sl + 1) * P],
                kfm[rb:rb + 64, cb:cb + P], qfm[rb:rb + 64, cb:cb + P],
                start=True, stop=True), w=w_s)
            w_m = [(p.sem["tensor"], tv_s), (p.sem["vector"], v_vt)]
            if intra_tv[sl]:
                w_m = [(p.sem["tensor"], max(tv_s, intra_tv[sl]))]
            v_m = vop(lambda e, sl=sl: e.tensor_tensor(
                mskt[:, sl * P:(sl + 1) * P], PS[2][:, sl * P:(sl + 1) * P],
                cmask[:, :], ALU.mult), w=w_m)
            msk_done[sl] = v_m
            ob = PS[4] if h < 8 else PS[5]
            oc0 = 64 * (h % 8)
            w_o = ([(p.sem["vector"], v_state)] + (
                [(p.dma["sync"], d_dup)] if d_dup else [])) if h == 0 else []
            top(lambda e, ob=ob, oc0=oc0, rb=rb, cb=cb, h=h: e.matmul(
                ob[:, oc0:oc0 + 64], qfm[rb:rb + 64, cb:cb + P],
                stateT[rb:rb + 64, 64 * h:64 * h + 64],
                start=True, stop=False), w=w_o)
            intra_tv[sl] = top(
                lambda e, ob=ob, oc0=oc0, h=h, sl=sl: e.matmul(
                    ob[:, oc0:oc0 + 64], mskt[:, sl * P:(sl + 1) * P],
                    vt_t[:, 64 * h:64 * h + 64],
                    start=False, stop=True),
                w=[(p.sem["vector"], max(v_m, v_vt))])
            sb_ = PS[6] if h < 8 else PS[7]
            top(lambda e, sb_=sb_, oc0=oc0, h=h: e.matmul(
                sb_[0:64, oc0:oc0 + 64], ktm[:, 64 * h:64 * h + 64],
                vt_t[:, 64 * h:64 * h + 64], start=True, stop=True),
                w=[(p.dma["sync"], d_ktm), (p.sem["scalar"], sv_gates)]
                if h == 0 else ())
        tv_heads = TE()
        vop(lambda e: e.tensor_tensor(TMP[0][0:64, 0:512], PS[6][0:64, 0:512],
                                      stateT[0:64, 0:512], ALU.add),
            w=[(p.sem["tensor"], tv_heads)])
        vop(lambda e: e.tensor_tensor(TMP[0][0:64, 512:D], PS[7][0:64, 0:512],
                                      stateT[0:64, 512:D], ALU.add))
        vop(lambda e: e.tensor_tensor(stateT[0:64, 0:512], TMP[0][0:64, 0:512],
                                      PS[0][0:64, 0:512], ALU.mult))
        vop(lambda e: e.tensor_tensor(stateT[0:64, 512:D], TMP[0][0:64, 512:D],
                                      PS[1][0:64, 0:512], ALU.mult))
        v_state = V()
        d_dup = p.dma_op("sync", lambda e: [e.dma_start(
            out=stateT[64:128, :], in_=stateT[0:64, :])],
            waits=[(p.sem["vector"], v_state)])
        vop(lambda e: e.tensor_tensor(TMP[0][:, 0:512], PS[4][:, 0:512],
                                      A_c[:, 0:512], ALU.mult))
        vop(lambda e: e.tensor_tensor(TMP[0][:, 512:D], PS[5][:, 0:512],
                                      A_c[:, 512:D], ALU.mult))
        v_oevac_prev = V()
        dv_sg = load(sgch[:], scr_sg[tc:tc + P, :],
                     w=[(p.sem["tensor"], tv_heads)])
        vopc(lambda e: e.reduce_sum(COLS["c0"][:, 0:1], TMP[0][:], AX.X))
        sopc(lambda e: e.activation(TMP[1][:], TMP[0][:], AF.Square),
            w=[(p.sem["vector"], V())])
        vopc(lambda e: e.reduce_sum(COLS["c1"][:, 0:1], TMP[1][:], AX.X),
            w=[(p.sem["scalar"], S())])
        vopc(lambda e: e.tensor_scalar(COLS["c0"][:, 0:1], COLS["c0"][:, 0:1],
                                      1.0 / D, None, ALU.mult))
        vopc(lambda e: e.tensor_tensor(COLS["c2"][:, 0:1], COLS["c0"][:, 0:1],
                                      COLS["c0"][:, 0:1], ALU.mult))
        vopc(lambda e: e.scalar_tensor_tensor(COLS["c1"][:, 0:1], COLS["c1"][:, 0:1],
                                             1.0 / D, COLS["c2"][:, 0:1],
                                             ALU.mult, ALU.subtract))
        vopc(lambda e: e.tensor_scalar(COLS["c1"][:, 0:1], COLS["c1"][:, 0:1],
                                      1e-5, None, ALU.add))
        sopc(lambda e: e.activation(COLS["c2"][:, 0:1], COLS["c1"][:, 0:1], AF.Ln),
            w=[(p.sem["vector"], V())])
        sopc(lambda e: e.activation(COLS["c2"][:, 0:1], COLS["c2"][:, 0:1],
                                   AF.Exp, scale=-0.5))
        vopc(lambda e: e.tensor_tensor(COLS["c3"][:, 0:1], COLS["c0"][:, 0:1],
                                      COLS["c2"][:, 0:1], ALU.mult),
            w=[(p.sem["scalar"], S())])
        vopc(lambda e: e.tensor_scalar(COLS["c3"][:, 0:1], COLS["c3"][:, 0:1],
                                      -1.0, None, ALU.mult))
        sopc(lambda e: e.activation(TMP[1][:], TMP[0][:], AF.Identity,
                                   scale=COLS["c2"][:, 0:1],
                                   bias=COLS["c3"][:, 0:1]),
            w=[(p.sem["vector"], V())])
        vop(lambda e: e.tensor_tensor(TMP[2][:], TMP[1][:], gamma_bc[:], ALU.mult),
            w=[(p.sem["scalar"], S())])
        vop(lambda e: e.tensor_tensor(TMP[1][:], TMP[2][:], beta_bc[:], ALU.add))
        vop(lambda e: e.tensor_tensor(TMP[2][:], TMP[1][:], sgch[:], ALU.mult),
            w=[(p.dma["sync"], dv_sg)])
        vop(lambda e: e.tensor_copy(xnbf[:], TMP[2][:]),
            w=[(p.dma["sync"], DS())])
        v_spre = V()
        d_sfm = xbars([(qfm[:, oc * P:(oc + 1) * P], xnbf[:, oc * P:(oc + 1) * P])
                       for oc in range(8)], w=[(p.sem["vector"], v_spre)])
        for half in range(2):
            for kd in range(8):
                top(lambda e, half=half, kd=kd: e.matmul(
                    PS[2 + half][:, 0:512], qfm[:, kd * P:(kd + 1) * P],
                    WO[:, kd * D + half * 512: kd * D + (half + 1) * 512],
                    start=(kd == 0), stop=(kd == 7)),
                    w=[(p.dma["sync"], d_sfm), (p.dma["gpsimd"], d_wo)]
                    if (half == 0 and kd == 0) else ())
        tv = TE()
        vop(lambda e: e.tensor_copy(TMP[3][:, 0:512], PS[2][:, 0:512]),
            w=[(p.sem["tensor"], tv), (p.dma["sync"], DS())])
        vop(lambda e: e.tensor_copy(TMP[3][:, 512:D], PS[3][:, 0:512]))
        v_wout_prev = V()
        vv = V()
        store(scr_sout[tc:tc + P, :], TMP[3][:], w=[(p.sem["vector"], vv)])
    d_sout_all = DS()

    if stop_after == 'C':
        return _finish(nc, p, st, debug_taps, scr_sig, scr_g, scr_sg, scr_sout,
                       scr_gp, xq_fm, xn_fm, store, V, DS, DG, T, D)
    dq_rd = [0, 0]
    for half in range(2):
        for ic in range(8):
            I0 = half * 2048 + ic * 256
            for wi, (src_h, dstq, kc, lnc) in enumerate(
                    ((gatew_h, gwq, "k_g", "lngs_g"), (upw_h, uwq, "k_u", "lngs_u"))):
                dvs = []
                for sub in range(2):
                    buf = wstr[2 * wi + sub]
                    dvs.append(load(buf[:], src_h[I0 + sub * P: I0 + (sub + 1) * P, :],
                                    w=[(p.sem["vector"], V())]))
                for sub in range(2):
                    amax_to(wstr[2 * wi + sub][:], 64, 64 * sub,
                            ws=[(p.dma["sync"], dvs[sub])])
                scale_pipe(None, 128, kc, lnc, None, None,
                           wv=[(p.sem["vector"], V())])
                for sub in range(2):
                    qb = (TMP[1], TMPX)[sub]
                    gq = quant_hot(wstr[2 * wi + sub][:], qb[:], 64,
                                   SC8[:, 64 * sub:64 * sub + 64],
                                   G8[:, 64 * sub:64 * sub + 64],
                                   ws=[(p.sem["vector"], V())], clip=False,
                                   out_ws=([(p.sem["tensor"], dq_rd[sub])]
                                           if dq_rd[sub] else []))
                    dq_rd[sub], _ = pe_xpose(
                        [(dstq[:, kd * 256 + sub * P: kd * 256 + (sub + 1) * P],
                          qb[:, kd * P:(kd + 1) * P]) for kd in range(8)],
                        PS[2], w=[(p.sem["gpsimd"], gq)])
            s_wslice = S()
            ev_done = [0, 0]
            for tt in range(NT):
                par = tt % 2
                pg, pu = (PS[0], PS[1]) if par == 0 else (PS[4], PS[5])
                w0 = [(p.sem["scalar"], s_wslice)]
                if ev_done[par]:
                    w0.append((p.sem["vector"], ev_done[par]))
                elif tt == 0:
                    w0.append((p.sem["vector"], V()))
                for kd in range(8):
                    top(lambda e, kd=kd, tt=tt, pg=pg: e.matmul(
                        pg[:, 0:256], xq_fm[:, kd * T + tt * P:
                                            kd * T + (tt + 1) * P],
                        gwq[:, kd * 256:(kd + 1) * 256],
                        start=(kd == 0), stop=(kd == 7)),
                        w=w0 if kd == 0 else ())
                for kd in range(8):
                    top(lambda e, kd=kd, tt=tt, pu=pu: e.matmul(
                        pu[:, 0:256], xq_fm[:, kd * T + tt * P:
                                            kd * T + (tt + 1) * P],
                        uwq[:, kd * 256:(kd + 1) * 256],
                        start=(kd == 0), stop=(kd == 7)))
                tv = TE()
                t0s, t1s = (TMP[0], TMP[1]) if par == 0 else (TMP[2], TMP[3])
                sop(lambda e, pg=pg, t0s=t0s: e.activation(
                    t0s[:, 0:256], pg[:, 0:256],
                    AF.Sigmoid, scale=COLS["a_gate"][:, 0:1]),
                    w=[(p.sem["tensor"], tv)])
                sv = S()
                vop(lambda e, pg=pg, t0s=t0s, t1s=t1s: e.tensor_tensor(
                    t1s[:, 0:256], pg[:, 0:256], t0s[:, 0:256], ALU.mult),
                    w=[(p.sem["scalar"], sv),
                       (p.sem["tensor"], max(dq_rd[0], dq_rd[1]))])
                vop(lambda e, tt=tt, pu=pu, t1s=t1s: e.scalar_tensor_tensor(
                    hid_c[:, tt * 256:(tt + 1) * 256], pu[:, 0:256],
                    COLS["a_gu"][:, 0:1], t1s[:, 0:256], ALU.mult, ALU.mult))
                ev_done[par] = V()
            for slab in range(4):
                amax_to(hid_c[:, slab * D:(slab + 1) * D], 64, 64 * slab,
                        ws=[(p.sem["vector"], V())] if slab == 0 else ())
            scale_pipe(None, 256, "k_h", "lngs_h", None, None,
                       wv=[(p.sem["vector"], V())])
            for slab in range(4):
                qb = (TMP[1], TMPX)[slab % 2]
                src = hid_c[:, slab * D:(slab + 1) * D]
                gq = quant_hot(src, qb[:], 64,
                               SC8[:, 64 * slab:64 * slab + 64],
                               G8[:, 64 * slab:64 * slab + 64],
                               ws=[(p.sem["vector"], V())], clip=True,
                               out_ws=([(p.sem["tensor"], dq_rd[slab % 2])]
                                       if dq_rd[slab % 2] else []))
                prs = []
                for j in range(4):
                    tt = slab * 4 + j
                    for ib in range(2):
                        ki = ic * 2 + ib
                        prs.append((hq_fm[:, ki * T + tt * P: ki * T + (tt + 1) * P],
                                    qb[:, j * 256 + ib * P:
                                        j * 256 + (ib + 1) * P]))
                dq_rd[slab % 2], _ = pe_xpose(prs, PS[2],
                                              w=[(p.sem["gpsimd"], gq)])
        s_hq = S()
        for i in range(8):
            dvs = []
            for sub in range(2):
                dvs.append(load(wstr[sub][:], downw_h[i * P:(i + 1) * P,
                                                      half * 2048 + sub * D:
                                                      half * 2048 + (sub + 1) * D],
                                w=[(p.sem["vector"], V())]))
            for sub in range(2):
                amax_to(wstr[sub][:], 64, 64 * sub, ws=[(p.dma["sync"], dvs[sub])])
            scale_pipe(None, 128, "k_d", "lngs_d", None, None,
                       wv=[(p.sem["vector"], V())])
            for sub in range(2):
                qb = (TMP[1], TMPX)[sub]
                gq = quant_hot(wstr[sub][:], qb[:], 64,
                               SC8[:, 64 * sub:64 * sub + 64],
                               G8[:, 64 * sub:64 * sub + 64],
                               ws=[(p.sem["vector"], V())], clip=False,
                               out_ws=([(p.sem["tensor"], dq_rd[sub])]
                                       if dq_rd[sub] else []))
                dq_rd[sub], _ = pe_xpose(
                    [(dwq[:, (sub * 8 + kk) * D + i * P:
                          (sub * 8 + kk) * D + (i + 1) * P],
                      qb[:, kk * P:(kk + 1) * P]) for kk in range(8)],
                    PS[2], w=[(p.sem["gpsimd"], gq)])
        s_dwq = S()
        ev2 = [0, 0]
        for tt in range(NT):
            par = tt % 2
            pa, pb = (PS[2], PS[3]) if par == 0 else (PS[6], PS[7])
            w0 = [(p.sem["scalar"], max(s_dwq, s_hq))]
            w0.append((p.sem["vector"], ev2[par] if ev2[par] else V()))
            for ki in range(16):
                for half2 in range(2):
                    top(lambda e, ki=ki, half2=half2, tt=tt, pa=pa, pb=pb: e.matmul(
                        (pa, pb)[half2][:, 0:512],
                        hq_fm[:, ki * T + tt * P: ki * T + (tt + 1) * P],
                        dwq[:, ki * D + half2 * 512: ki * D + (half2 + 1) * 512],
                        start=(ki == 0), stop=(ki == 15)),
                        w=w0 if (ki == 0 and half2 == 0) else ())
            tv = TE()
            t0s = TMP[0] if par == 0 else TMP[2]
            t1s = TMP[1] if par == 0 else TMP[3]
            if half == 0:
                vop(lambda e, pa=pa, t0s=t0s: e.tensor_copy(
                    t0s[:, 0:512], pa[:, 0:512]),
                    w=[(p.sem["tensor"], tv), (p.dma["sync"], DS())])
                vop(lambda e, pb=pb, t0s=t0s: e.tensor_copy(
                    t0s[:, 512:D], pb[:, 0:512]))
                vv = V()
                store(scr_gp[tt * P:(tt + 1) * P, :], t0s[:],
                      w=[(p.sem["vector"], vv)])
            else:
                dv = load(t1s[:], scr_gp[tt * P:(tt + 1) * P, :],
                          w=[(p.sem["vector"], ev2[par] if ev2[par] else V()),
                             (p.sem["tensor"], max(dq_rd[0], dq_rd[1]))])
                vop(lambda e, pa=pa, t0s=t0s, t1s=t1s: e.tensor_tensor(
                    t0s[:, 0:512], pa[:, 0:512], t1s[:, 0:512], ALU.add),
                    w=[(p.sem["tensor"], tv), (p.dma["sync"], max(dv, DS()))])
                vop(lambda e, pb=pb, t0s=t0s, t1s=t1s: e.tensor_tensor(
                    t0s[:, 512:D], pb[:, 0:512], t1s[:, 512:D], ALU.add))
                vv = V()
                store(scr_gp[tt * P:(tt + 1) * P, :], t0s[:],
                      w=[(p.sem["vector"], vv)])
            ev2[par] = V()
    d_gf = DS()

    for tt in range(NT):
        xb = wstr[0]
        dv1 = load(xb[:], x_h[tt * P:(tt + 1) * P, :],
                   w=[(p.sem["vector"], V()), (p.dma["sync"], d_gf)])
        dv2 = load(TMP[3][:], scr_gp[tt * P:(tt + 1) * P, :])
        dv3 = p.dma_op("gpsimd", lambda e, tt=tt: [e.dma_start(
            out=TMP[4][:], in_=scr_sout[tt * P:(tt + 1) * P, :])],
            waits=[(p.sem["vector"], V())])
        sop(lambda e: e.activation(TMP[0][:], TMP[3][:], AF.Sigmoid,
                                   scale=COLS["a_d"][:, 0:1]),
            w=[(p.dma["sync"], dv2)])
        sv = S()
        vop(lambda e: e.tensor_tensor(TMP[1][:], TMP[3][:], TMP[0][:], ALU.mult),
            w=[(p.sem["scalar"], sv)])
        vop(lambda e: e.scalar_tensor_tensor(TMP[2][:], TMP[1][:],
                                             COLS["a_d"][:, 0:1], TMP[4][:],
                                             ALU.mult, ALU.mult),
            w=[(p.dma["gpsimd"], dv3)])
        vop(lambda e, xb=xb: e.tensor_tensor(TMP[1][:], TMP[2][:], xb[:], ALU.add),
            w=[(p.dma["sync"], dv1)])
        vv = V()
        store(y_h[tt * P:(tt + 1) * P, :], TMP[1][:], w=[(p.sem["vector"], vv)])
    return _finish(nc, p, st, debug_taps, scr_sig, scr_g, scr_sg, scr_sout,
                   scr_gp, xq_fm, xn_fm, store, V, DS, DG, T, D)


def _finish(nc, p, st, debug_taps, scr_sig, scr_g, scr_sg, scr_sout, scr_gp,
            xq_fm, xn_fm, store, V, DS, DG, T, D):
    if debug_taps:
        for nm, scr in (("dbg_sig", scr_sig), ("dbg_g", scr_g), ("dbg_sg", scr_sg),
                        ("dbg_sout", scr_sout), ("dbg_gp", scr_gp)):
            dst = nc.dram_tensor(nm, [T, D], F32, kind="ExternalOutput").ap()
            store(dst, scr, w=[(p.sem["vector"], V())])
        dbx = nc.dram_tensor("dbg_xqfm", [P, 8 * T], BF16, kind="ExternalOutput").ap()
        store(dbx, xq_fm[:], w=[(p.sem["vector"], V())])
        dbn = nc.dram_tensor("dbg_xnfm", [P, 8 * T], BF16, kind="ExternalOutput").ap()
        store(dbn, xn_fm[:], w=[(p.sem["vector"], V())])
    d_final = DS()
    p.op("sync", lambda e: e.nop(), waits=[(p.dma["sync"], d_final),
                                           (p.dma["gpsimd"], DG()),
                                           (p.dma["scalar"], p.dma["scalar"].v)],
         inc=False)
    with nc.Block() as block:
        p.emit(block)
    st.close()
    return nc


_NC_CACHE = None


def _get_nc():
    global _NC_CACHE
    if _NC_CACHE is None:
        _NC_CACHE = build_program()
    return _NC_CACHE


def kernel(L0_attn_x, L0_attn_Wq, L0_attn_Wk, L0_attn_Wv, L0_attn_Wg,
           L0_attn_Wgk1, L0_attn_Wgk2, L0_attn_bgk2, L0_attn_gamma,
           L0_attn_beta, L0_attn_Wout, L0_ffn_gate_w, L0_ffn_up_w,
           L0_ffn_down_w, L0_ffn_s_in, L0_ffn_s_in_down):
    f32 = np.float32
    x = np.ascontiguousarray(L0_attn_x, dtype=f32)
    B = x.shape[0]
    w2b = np.ascontiguousarray(
        np.concatenate([np.asarray(L0_attn_Wgk2, f32),
                        np.asarray(L0_attn_bgk2, f32)[None, :]], axis=0))
    gamma_bc = np.ascontiguousarray(
        np.broadcast_to(np.asarray(L0_attn_gamma, f32)[None, :], (P, D)))
    beta_bc = np.ascontiguousarray(
        np.broadcast_to(np.asarray(L0_attn_beta, f32)[None, :], (P, D)))
    sx_col = np.full((P, 1), np.asarray(L0_ffn_s_in, f32)[0], f32)
    sd_col = np.full((P, 1), np.asarray(L0_ffn_s_in_down, f32)[0], f32)
    sidx = np.arange(P)
    tri16 = ((sidx[:, None] <= sidx[None, :]).astype(f32) / f32(16.0))
    cmask = (sidx[:, None] <= sidx[None, :]).astype(f32)
    oneh128 = np.zeros((P, P), f32); oneh128[P - 1, :] = 1.0
    onecol = np.ones((1, P), f32)
    identbf = np.eye(P, dtype=f32)

    shared = dict(
        wq=np.ascontiguousarray(L0_attn_Wq, f32),
        wk=np.ascontiguousarray(L0_attn_Wk, f32),
        wv=np.ascontiguousarray(L0_attn_Wv, f32),
        wg=np.ascontiguousarray(L0_attn_Wg, f32),
        wout=np.ascontiguousarray(L0_attn_Wout, f32),
        wgk1=np.ascontiguousarray(L0_attn_Wgk1, f32),
        w2b=w2b, gamma_bc=gamma_bc, beta_bc=beta_bc,
        sx_col=sx_col, sd_col=sd_col,
        gatew=np.ascontiguousarray(L0_ffn_gate_w, f32),
        upw=np.ascontiguousarray(L0_ffn_up_w, f32),
        downw=np.ascontiguousarray(L0_ffn_down_w, f32),
        tri16=tri16, cmask=cmask, oneh128=oneh128, onecol=onecol,
        identbf=identbf,
    )
    in_maps = [dict(shared, x=np.ascontiguousarray(x[b])) for b in range(B)]
    nc = _get_nc()
    res = run_bass_kernel_spmd(nc, in_maps, core_ids=list(range(B)))
    out = np.stack([np.asarray(res.results[b]["y"]) for b in range(B)], axis=0)
    return out.astype(np.float32)


# version 07
# speedup vs baseline: 1.0088x; 1.0088x over previous
import numpy as np
import concourse.bass as bass
import concourse.mybir as mybir
from concourse.bass_utils import run_bass_kernel_spmd
from contextlib import ExitStack

F32 = mybir.dt.float32
BF16 = mybir.dt.bfloat16
I32 = mybir.dt.int32
AF = mybir.ActivationFunctionType
ALU = mybir.AluOpType
AX = mybir.AxisListType

P = 128
T = 2048
D = 1024
NT = T // P
H = 16
DH = 64
IDIM = 4096
LN2 = 0.6931471805599453
INV_LN2 = 1.4426950408889634
KB = 1024

ENGS = ("sync", "vector", "scalar", "tensor", "gpsimd")


class Sem:
    def __init__(self, nc, st, name):
        self.h = st.enter_context(nc.semaphore(name))
        self.v = 0


class Prog:
    def __init__(self, nc, st, paranoid=False):
        self.nc = nc
        self.ops = []
        self.paranoid = paranoid
        self.sem = {e: Sem(nc, st, "sem_" + e) for e in ENGS}
        self.dma = {q: Sem(nc, st, "dma_" + q) for q in ("sync", "gpsimd", "scalar")}

    def _par_waits(self, eng):
        w = [(s, s.v) for e, s in self.sem.items() if e != eng]
        w += [(s, s.v) for s in self.dma.values()]
        return w

    def op(self, eng, fn, waits=(), inc=True):
        if self.paranoid:
            waits = list(waits) + self._par_waits(eng)
        incs = []
        if inc or self.paranoid:
            s = self.sem[eng]
            s.v += 1
            incs.append((s, 1))
        self.ops.append((eng, tuple(waits), tuple(incs), fn))
        return self.sem[eng].v

    def dma_op(self, q, fn, waits=(), n=1):
        if self.paranoid:
            waits = list(waits) + self._par_waits(None)
        s = self.dma[q]

        def wrapped(eng, fn=fn, s=s):
            insts = fn(eng)
            for inst in insts:
                inst.then_inc(s.h, 16)
            return None

        s.v += 16 * n
        self.ops.append((q, tuple(waits), (), wrapped))
        return s.v

    def emit(self, block):
        for ename in ENGS:
            ops = [o for o in self.ops if o[0] == ename]

            def mk(ops=ops):
                def f(eng):
                    for (_, waits, incs, fn) in ops:
                        best = {}
                        for (s, val) in waits:
                            if val > best.get(id(s), (None, -1))[1]:
                                best[id(s)] = (s, val)
                        for (s, val) in best.values():
                            eng.wait_ge(s.h, val)
                        inst = fn(eng)
                        for (s, n) in incs:
                            inst.then_inc(s.h, n)
                return f
            getattr(block, ename)(mk())


class Union:
    def __init__(self, nc, st, name, bytes_per_partition):
        self.nc = nc
        self.backing = st.enter_context(
            nc.sbuf_tensor(name, [P, bytes_per_partition], mybir.dt.uint8))
        self.base = nc.lookup_mloc(self.backing).addr
        self.size = bytes_per_partition

    def at(self, off, name, shape, dtype):
        per_part = int(np.prod(shape[1:])) * mybir.dt.size(dtype)
        assert off % 32 == 0 and off + per_part <= self.size, (name, off, per_part)
        return self.nc.alloc_sbuf_tensor_at(name, shape, dtype,
                                            offset=self.base + off)


def build_program(paranoid=False, debug_taps=False, stop_after='E'):
    nc = bass.Bass("TRN2", target_bir_lowering=False, debug=False,
                   detect_race_conditions=False)
    st = ExitStack()
    E = st.enter_context

    def din(name, shape, dt=F32):
        return nc.dram_tensor(name, shape, dt, kind="ExternalInput").ap()

    x_h = din("x", [T, D])
    wq_h = din("wq", [D, D]); wk_h = din("wk", [D, D]); wv_h = din("wv", [D, D])
    wg_h = din("wg", [D, D]); wout_h = din("wout", [D, D])
    wgk1_h = din("wgk1", [D, 16])
    w2b_h = din("w2b", [17, D])
    gatew_h = din("gatew", [IDIM, D]); upw_h = din("upw", [IDIM, D])
    downw_h = din("downw", [D, IDIM])
    gamma_h = din("gamma_bc", [P, D]); beta_h = din("beta_bc", [P, D])
    sx_h = din("sx_col", [P, 1]); sd_h = din("sd_col", [P, 1])
    tri16_h = din("tri16", [P, P]); cmask_h = din("cmask", [P, P])
    oneh128_h = din("oneh128", [P, P]); onecol_h = din("onecol", [1, P])
    identbf_h = din("identbf", [P, P])

    y_h = nc.dram_tensor("y", [T, D], F32, kind="ExternalOutput").ap()
    scr_sig = nc.dram_tensor("scr_sig", [T, D], F32, kind="Internal").ap()
    scr_g = nc.dram_tensor("scr_g", [T, D], F32, kind="Internal").ap()
    scr_sg = nc.dram_tensor("scr_sg", [T, D], F32, kind="Internal").ap()
    scr_sout = nc.dram_tensor("scr_sout", [T, D], F32, kind="Internal").ap()
    scr_gp = nc.dram_tensor("scr_gp", [T, D], F32, kind="Internal").ap()

    TMP = [E(nc.sbuf_tensor(f"tmp{i}", [P, D], F32)) for i in range(5)]
    TMPB = [E(nc.sbuf_tensor(f"tmpb{i}", [P, D], BF16)) for i in range(2)]
    TMPX = E(nc.sbuf_tensor("tmpx", [P, D], F32))
    xq_fm = E(nc.sbuf_tensor("xq_fm", [P, 8 * T], BF16))
    tri16 = E(nc.sbuf_tensor("tri16_sb", [P, P], F32))
    cmask = E(nc.sbuf_tensor("cmask_sb", [P, P], F32))
    oneh128 = E(nc.sbuf_tensor("oneh128_sb", [P, P], F32))
    identbf = E(nc.sbuf_tensor("identbf_sb", [P, P], F32))
    onecol = E(nc.sbuf_tensor("onecol_sb", [1, P], F32))
    gamma_bc = E(nc.sbuf_tensor("gamma_bc_sb", [P, D], F32))
    beta_bc = E(nc.sbuf_tensor("beta_bc_sb", [P, D], F32))
    wgk1_bf = E(nc.sbuf_tensor("wgk1_bf", [P, 8 * 16], BF16))
    w2b_bf = E(nc.sbuf_tensor("w2b_bf", [17, D], BF16))
    rbf = E(nc.sbuf_tensor("rbf", [32, T], BF16))
    stateT = E(nc.sbuf_tensor("stateT", [P, H * DH], BF16))
    xnbf = E(nc.sbuf_tensor("xnbf", [P, D], BF16))

    onerow = E(nc.sbuf_tensor("onerow", [1, P + 4], F32))
    SC8 = E(nc.sbuf_tensor("sc8t", [P, 256], F32))
    AMX = E(nc.sbuf_tensor("amxt", [P, 256], F32))
    G8 = E(nc.sbuf_tensor("g8t", [P, 256], F32))
    SCT = [E(nc.sbuf_tensor(f"sct{i}", [P, 256], F32)) for i in range(4)]
    SCI = E(nc.sbuf_tensor("sci", [P, D], I32))
    TI = SCI
    COLS = {}
    for nm in ["sx", "sd", "k_x", "k_h", "lngs_x", "lngs_h",
               "gm_g", "gm_u", "gm_d", "gs_g", "gs_u", "gs_d",
               "k_g", "k_u", "k_d", "lngs_g", "lngs_u", "lngs_d",
               "a_gate", "a_gu", "a_d", "c0", "c1", "c2", "c3", "c4", "m16"]:
        COLS[nm] = E(nc.sbuf_tensor("col_" + nm, [P, 1], F32))

    U = Union(nc, st, "union", 114 * KB)
    xn_fm = U.at(0, "xn_fm", [P, 8 * T], BF16)
    WQ = U.at(32 * KB, "WQ", [P, 8 * D], BF16)
    WK = U.at(48 * KB, "WK", [P, 8 * D], BF16)
    WV = U.at(64 * KB, "WV", [P, 8 * D], BF16)
    WO = U.at(80 * KB, "WO", [P, 8 * D], BF16)
    WG = U.at(80 * KB, "WG", [P, 8 * D], BF16)
    qfm = U.at(96 * KB, "qfm", [P, 8 * P], BF16)
    kfm = U.at(98 * KB, "kfm", [P, 8 * P], BF16)
    ktm = U.at(100 * KB, "ktm", [P, D], BF16)
    vtm = U.at(102 * KB, "vtm", [P, D], BF16)
    vt_t = U.at(104 * KB, "vt_t", [P, D], BF16)
    A_c = U.at(106 * KB, "A_c", [P, D], F32)
    gch = U.at(110 * KB, "gch", [P, D], F32)
    Ainv = U.at(110 * KB, "Ainv", [P, D], F32)
    sgch = U.at(100 * KB, "sgch", [P, D], F32)
    mskt = U.at(113 * KB, "mskt", [P, 4 * P], BF16)
    hq_fm = U.at(0, "hq_fm", [P, 16 * T], BF16)
    dwq = U.at(64 * KB, "dwq", [P, 16 * D], BF16)
    hid_c = U.at(64 * KB, "hid_c", [P, NT * 256], F32)
    gwq = U.at(80 * KB, "gwq", [P, 8 * 256], BF16)
    uwq = U.at(84 * KB, "uwq", [P, 8 * 256], BF16)
    wstr = [U.at((96 + 4 * i) * KB, f"wstr{i}", [P, D], F32) for i in range(4)]

    PS = [E(nc.psum_tensor(f"ps{i}", [P, 512], F32)) for i in range(8)]

    p = Prog(nc, st, paranoid=paranoid)
    V = lambda: p.sem["vector"].v
    S = lambda: p.sem["scalar"].v
    TE = lambda: p.sem["tensor"].v
    DS = lambda: p.dma["sync"].v
    DG = lambda: p.dma["gpsimd"].v

    def vop(fn, w=()):
        return p.op("vector", fn, waits=w)

    def sop(fn, w=()):
        return p.op("scalar", fn, waits=w)

    def vopc(fn, w=()):
        return p.op("vector", fn, waits=list(w) + [(p.sem["vector"], V())])

    def sopc(fn, w=()):
        return p.op("scalar", fn, waits=list(w) + [(p.sem["scalar"], S())])

    def top(fn, w=()):
        return p.op("tensor", fn, waits=w)

    def load(dst, src, w=(), q="sync"):
        return p.dma_op(q, lambda e: [e.dma_start(out=dst, in_=src)], waits=w)

    def store(dst, src, w=(), q="sync"):
        return p.dma_op(q, lambda e: [e.dma_start(out=dst, in_=src)], waits=w)

    def loads(pairs, w=(), q="sync"):
        return p.dma_op(q, lambda e: [e.dma_start(out=d, in_=s) for d, s in pairs],
                        waits=w, n=len(pairs))

    pexp_state = {"ev": 0}

    def pe_xpose(pairs, ps, w=()):
        first = True
        for g0 in range(0, len(pairs), 4):
            grp = pairs[g0:g0 + 4]
            for j, (dst, srcp) in enumerate(grp):
                ww = []
                if first:
                    ww = list(w)
                    first = False
                if pexp_state["ev"]:
                    ww.append((p.sem["scalar"], pexp_state["ev"]))
                    pexp_state["ev"] = 0
                top(lambda e, j=j, srcp=srcp, ps=ps: e.transpose(
                    ps[:, j * P:(j + 1) * P], srcp, identbf[:]), w=ww)
            tv = TE()
            for j, (dst, srcp) in enumerate(grp):
                sop(lambda e, j=j, dst=dst, ps=ps: e.copy(dst, ps[:, j * P:(j + 1) * P]),
                    w=[(p.sem["tensor"], tv)] if j == 0 else ())
            pexp_state["ev"] = S()
        return TE(), S()

    def xbars(pairs, w=(), q="sync"):
        return p.dma_op(q, lambda e: [e.dma_start_transpose(d, s)
                                      for d, s in pairs], waits=w, n=len(pairs))

    def scale_pipe(amax_ap, nb, kcol, lncol, sc8_ap, g8_ap, wv):
        W = 256
        s0, s1, s2, s3 = (t[:, 0:W] for t in SCT)
        si = SCI[:, 0:W]
        amx = AMX[:, 0:W]
        vop(lambda e: e.tensor_scalar(s0, amx, COLS[kcol][:, 0:1], 448.0,
                                      ALU.mult, ALU.min), w=wv)
        vop(lambda e: e.tensor_scalar(s0, s0, 2.0 ** -20, None, ALU.max))
        sop(lambda e: e.activation(s1, s0, AF.Ln), w=[(p.sem["vector"], V())])
        vop(lambda e: e.tensor_scalar(s2, s1, INV_LN2, None, ALU.mult),
            w=[(p.sem["scalar"], S())])
        vop(lambda e: e.tensor_scalar(si, s2, 32.0, None, ALU.add))
        vop(lambda e: e.tensor_scalar(s3, si, -32.0, None, ALU.add))
        vop(lambda e: e.tensor_tensor(s1, s3, s2, ALU.is_gt))
        vop(lambda e: e.tensor_tensor(s3, s3, s1, ALU.subtract))
        sop(lambda e: e.activation(s2, s3, AF.Exp, scale=-LN2),
            w=[(p.sem["vector"], V())])
        vop(lambda e: e.tensor_tensor(s2, s0, s2, ALU.mult),
            w=[(p.sem["scalar"], S())])
        vop(lambda e: e.tensor_scalar(s0, s2, 8.0, 0.5, ALU.mult, ALU.add))
        vop(lambda e: e.tensor_copy(si, s0))
        vop(lambda e: e.tensor_copy(s2, si))
        vop(lambda e: e.tensor_tensor(s1, s2, s0, ALU.is_gt))
        vop(lambda e: e.tensor_tensor(s2, s2, s1, ALU.subtract))
        sop(lambda e: e.activation(s1, s3, AF.Exp, scale=LN2),
            w=[(p.sem["vector"], V())])
        vop(lambda e: e.scalar_tensor_tensor(SC8[:, 0:W], s2, 0.125, s1, ALU.mult,
                                             ALU.mult), w=[(p.sem["scalar"], S())])
        sop(lambda e: e.activation(s0, SC8[:, 0:W], AF.Ln),
            w=[(p.sem["vector"], V())])
        sop(lambda e: e.activation(G8[:, 0:W], s0, AF.Exp, scale=-1.0,
                                   bias=COLS[lncol][:, 0:1]))

    GQ = {"prev": 0}

    def quant_hot(src_ap, out_ap, nb, sc8_ap, g8_ap, ws, clip, out_ws=None):
        if out_ws is None:
            out_ws = [(p.dma["sync"], DS()),
                      (p.dma["scalar"], p.dma["scalar"].v)]
        F = 16 * nb
        t0, t1, t2, t3 = (t[:, 0:F] for t in TMP[0:4])
        b0, b1 = (t[:, 0:F] for t in TMPB)
        b2 = TMP[4][:, 0:F]
        ti = TI[:, 0:F]
        src3 = src_ap.rearrange("p (b k) -> p b k", k=16)
        g83 = g8_ap.broadcast_to([P, nb, 16])
        sc83 = sc8_ap.broadcast_to([P, nb, 16])
        vop(lambda e: e.tensor_tensor(t0.rearrange("p (b k) -> p b k", k=16),
                                      src3, g83, ALU.mult),
            w=list(ws) + [(p.sem["scalar"], S())])
        if clip:
            vop(lambda e: e.tensor_scalar(t0, t0, 6.0, -6.0, ALU.min, ALU.max))
        sop(lambda e: e.activation(b0, t0, AF.Square), w=[(p.sem["vector"], V())])
        vop(lambda e: e.tensor_scalar(b1, b0, 4.0, 0.5, ALU.is_ge, ALU.mult),
            w=[(p.sem["scalar"], S())])
        vop(lambda e: e.scalar_tensor_tensor(b2, b0, 16.0, b1, ALU.is_ge, ALU.add))
        vop(lambda e: e.tensor_scalar(b0, b2, -0.5, 2.0, ALU.mult, ALU.add))
        vop(lambda e: e.scalar_tensor_tensor(b0, b1, -1.5, b0, ALU.mult, ALU.add))
        vop(lambda e: e.tensor_tensor(t1, t0, b0, ALU.mult))
        vop(lambda e: e.tensor_scalar(ti, t1, 16.5, None, ALU.add))
        vop(lambda e: e.tensor_scalar(t2, ti, -16.0, None, ALU.add))
        vop(lambda e: e.scalar_tensor_tensor(t3, t1, 0.5, t2, ALU.add, ALU.is_lt),
            w=[(p.sem["gpsimd"], GQ["prev"])] if GQ["prev"] else ())
        vop(lambda e: e.tensor_tensor(t2, t2, t3, ALU.subtract))
        vop(lambda e: e.scalar_tensor_tensor(t3, b2, 0.5, t2, ALU.add, ALU.mult))
        vv = V()
        GQ["prev"] = p.op("gpsimd", lambda e: e.tensor_tensor(
            out_ap.rearrange("p (b k) -> p b k", k=16),
            t3.rearrange("p (b k) -> p b k", k=16),
            sc83, ALU.mult),
            waits=[(p.sem["vector"], vv)] + list(out_ws))
        return GQ["prev"]


    def amax_to(src_ap, nb, off, ws=()):
        return vop(lambda e: e.tensor_reduce(
            AMX[:, off:off + nb], src_ap.rearrange("p (b k) -> p b k", k=16),
            AX.X, ALU.max, apply_absolute_value=True), w=ws)

    def quantize_tile(src_ap, out_ap, nb, kcol, lncol, ws, clip):
        amax_to(src_ap, nb, 0, ws=ws)
        scale_pipe(None, nb, kcol, lncol, None, None,
                   wv=[(p.sem["vector"], V())])
        return quant_hot(src_ap, out_ap, nb, SC8[:, 0:nb], G8[:, 0:nb],
                         ws=[(p.sem["vector"], V())], clip=clip)

    loads([(tri16[:], tri16_h), (cmask[:], cmask_h), (oneh128[:], oneh128_h),
           (onecol[:], onecol_h), (gamma_bc[:], gamma_h), (beta_bc[:], beta_h),
           (COLS["sx"][:], sx_h), (COLS["sd"][:], sd_h), (identbf[:], identbf_h)])
    d_consts = DS()
    for W_, src in ((WQ, wq_h), (WK, wk_h), (WV, wv_h)):
        for kd in range(8):
            load(W_[:, kd * D:(kd + 1) * D], src[kd * P:(kd + 1) * P, :], q="gpsimd")
    for kd in range(8):
        load(wgk1_bf[:, kd * 16:(kd + 1) * 16], wgk1_h[kd * P:(kd + 1) * P, :],
             q="gpsimd")
    load(w2b_bf[:], w2b_h, q="gpsimd")
    d_attnw = DG()

    vop(lambda e: e.memset(AMX[:, :], 1.0))
    vop(lambda e: e.memset(COLS["m16"][:, 0:1], -16.0))
    vopc(lambda e: e.tensor_scalar(COLS["k_x"][:, 0:1], COLS["sx"][:, 0:1],
                                   1.0 / 6.0, None, ALU.mult),
         w=[(p.dma["sync"], d_consts)])
    vopc(lambda e: e.tensor_scalar(COLS["k_h"][:, 0:1], COLS["sd"][:, 0:1],
                                   1.0 / 6.0, None, ALU.mult))
    sop(lambda e: e.activation(COLS["lngs_x"][:, 0:1], COLS["sx"][:, 0:1], AF.Ln),
        w=[(p.sem["vector"], V())])
    sop(lambda e: e.activation(COLS["lngs_h"][:, 0:1], COLS["sd"][:, 0:1], AF.Ln))

    def gmax_stream(tiles, gm):
        vop(lambda e: e.memset(COLS[gm][:, 0:1], 0.0))
        for i, src in enumerate(tiles):
            buf = wstr[i % 2]
            w_ = src.shape[1]
            dv = load(buf[:, 0:w_], src, w=[(p.sem["vector"], V())])
            vop(lambda e, b=buf, w2=w_: e.tensor_reduce(
                COLS["c4"][:, 0:1], b[:, 0:w2], AX.X, ALU.max,
                apply_absolute_value=True), w=[(p.dma["sync"], dv)])
            vopc(lambda e, g=gm: e.tensor_tensor(COLS[g][:, 0:1], COLS[g][:, 0:1],
                                                 COLS["c4"][:, 0:1], ALU.max))
        vopc(lambda e, g=gm: e.tensor_copy(
            TMP[1][:, 0:32].rearrange("p (one k) -> p one k", k=32),
            COLS[g][:, 0:1].broadcast_to([P, 1, 32])))
        for j in range(4):
            vopc(lambda e, j=j: e.transpose(TMP[0][0:32, 32 * j:32 * (j + 1)],
                                            TMP[1][32 * j:32 * (j + 1), 0:32]))
        vopc(lambda e: e.tensor_reduce(onerow[0:1, P:P + 1], TMP[0][0:1, 0:P],
                                      AX.X, ALU.max))
        vopc(lambda e: e.reciprocal(onerow[0:1, P + 1:P + 2], onerow[0:1, P:P + 1]))
        vopc(lambda e: e.tensor_scalar(onerow[0:1, P + 2:P + 3],
                                      onerow[0:1, P + 1:P + 2], 2688.0, None,
                                      ALU.mult))

    def gs_bcast(gs, kc, lnc):
        tv = top(lambda e: e.matmul(PS[0][:, 0:1], onecol[0:1, 0:P],
                                    onerow[0:1, P + 2:P + 3], start=True, stop=True),
                 w=[(p.sem["vector"], V())])
        vopc(lambda e: e.tensor_copy(COLS[gs][:, 0:1], PS[0][:, 0:1]),
             w=[(p.sem["tensor"], tv)])
        vopc(lambda e: e.tensor_scalar(COLS[kc][:, 0:1], COLS[gs][:, 0:1],
                                       1.0 / 6.0, None, ALU.mult))
        sopc(lambda e: e.activation(COLS[lnc][:, 0:1], COLS[gs][:, 0:1], AF.Ln),
             w=[(p.sem["vector"], V())])

    gmax_stream([gatew_h[i * P:(i + 1) * P, :] for i in range(32)], "gm_g")
    gs_bcast("gs_g", "k_g", "lngs_g")
    gmax_stream([upw_h[i * P:(i + 1) * P, :] for i in range(32)], "gm_u")
    gs_bcast("gs_u", "k_u", "lngs_u")
    gmax_stream([downw_h[i * P:(i + 1) * P, j * D:(j + 1) * D]
                 for i in range(8) for j in range(4)], "gm_d")
    gs_bcast("gs_d", "k_d", "lngs_d")

    def recip_into(dst, a, b):
        vopc(lambda e: e.tensor_tensor(COLS["c4"][:, 0:1], COLS[a][:, 0:1],
                                       COLS[b][:, 0:1], ALU.mult))
        vopc(lambda e: e.reciprocal(COLS[dst][:, 0:1], COLS["c4"][:, 0:1]))

    recip_into("a_gate", "sx", "gs_g")
    recip_into("c0", "sx", "gs_u")
    vopc(lambda e: e.tensor_tensor(COLS["a_gu"][:, 0:1], COLS["a_gate"][:, 0:1],
                                   COLS["c0"][:, 0:1], ALU.mult))
    recip_into("a_d", "sd", "gs_d")

    xq_rd = [0, 0]
    xn_rd = 0
    for tt in range(NT):
        xb = TMP[1]
        dv = load(xb[:], x_h[tt * P:(tt + 1) * P, :],
                  w=[(p.sem["vector"], V()), (p.sem["scalar"], S()),
                     (p.sem["tensor"], max(xq_rd[0], xq_rd[1])),
                     (p.sem["gpsimd"], p.sem["gpsimd"].v)])
        vopc(lambda e, xb=xb: e.reduce_sum(COLS["c0"][:, 0:1], xb[:], AX.X),
             w=[(p.dma["sync"], dv)])
        sop(lambda e, xb=xb: e.activation(TMP[0][:], xb[:], AF.Square),
            w=[(p.sem["vector"], V())])
        vopc(lambda e: e.reduce_sum(COLS["c1"][:, 0:1], TMP[0][:], AX.X),
             w=[(p.sem["scalar"], S())])
        vopc(lambda e: e.tensor_scalar(COLS["c0"][:, 0:1], COLS["c0"][:, 0:1],
                                       1.0 / D, None, ALU.mult))
        vopc(lambda e: e.tensor_tensor(COLS["c2"][:, 0:1], COLS["c0"][:, 0:1],
                                       COLS["c0"][:, 0:1], ALU.mult))
        vopc(lambda e: e.scalar_tensor_tensor(COLS["c1"][:, 0:1], COLS["c1"][:, 0:1],
                                              1.0 / D, COLS["c2"][:, 0:1],
                                              ALU.mult, ALU.subtract))
        vopc(lambda e: e.tensor_scalar(COLS["c1"][:, 0:1], COLS["c1"][:, 0:1],
                                       1e-5, None, ALU.add))
        sopc(lambda e: e.activation(COLS["c2"][:, 0:1], COLS["c1"][:, 0:1], AF.Ln),
             w=[(p.sem["vector"], V())])
        sopc(lambda e: e.activation(COLS["c2"][:, 0:1], COLS["c2"][:, 0:1],
                                    AF.Exp, scale=-0.5))
        vopc(lambda e: e.tensor_tensor(COLS["c3"][:, 0:1], COLS["c0"][:, 0:1],
                                       COLS["c2"][:, 0:1], ALU.mult),
             w=[(p.sem["scalar"], S())])
        vopc(lambda e: e.tensor_scalar(COLS["c3"][:, 0:1], COLS["c3"][:, 0:1],
                                       -1.0, None, ALU.mult))
        sopc(lambda e, xb=xb: e.activation(TMP[4][:], xb[:], AF.Identity,
                                           scale=COLS["c2"][:, 0:1],
                                           bias=COLS["c3"][:, 0:1]),
             w=[(p.sem["vector"], V())])
        sv_xn = S()
        xn_rd, _ = pe_xpose([(xn_fm[:, kd * T + tt * P: kd * T + (tt + 1) * P],
                              TMP[4][:, kd * P:(kd + 1) * P]) for kd in range(8)],
                            PS[0], w=[(p.sem["scalar"], sv_xn),
                                      (p.dma["gpsimd"], d_attnw)])
        amax_to(TMP[4][:], 64, 0, ws=[(p.sem["scalar"], sv_xn)])
        scale_pipe(None, 64, "k_x", "lngs_x", None, None,
                   wv=[(p.sem["vector"], V())])
        qb = (TMP[1], TMPX)[tt % 2]
        gq = quant_hot(TMP[4][:], qb[:], 64, SC8[:, 0:64], G8[:, 0:64],
                       ws=[(p.sem["vector"], V()), (p.sem["tensor"], xn_rd)],
                       clip=True,
                       out_ws=([(p.sem["tensor"], xq_rd[tt % 2])]
                               if xq_rd[tt % 2] else []))
        xq_rd[tt % 2], _ = pe_xpose(
            [(xq_fm[:, kd * T + tt * P: kd * T + (tt + 1) * P],
              qb[:, kd * P:(kd + 1) * P]) for kd in range(8)],
            PS[1], w=[(p.sem["gpsimd"], gq)])
    s_xnfm = S()

    if stop_after == 'A':
        return _finish(nc, p, st, debug_taps, scr_sig, scr_g, scr_sg, scr_sout,
                       scr_gp, xq_fm, xn_fm, store, V, DS, DG, T, D)
    vop(lambda e: e.memset(rbf[0:32, :], 1.0))
    v_rones = V()
    for q4 in range(4):
        for kd in range(8):
            top(lambda e, q4=q4, kd=kd: e.matmul(
                PS[0][0:16, 0:512], wgk1_bf[:, kd * 16:(kd + 1) * 16],
                xn_fm[:, kd * T + q4 * 512: kd * T + (q4 + 1) * 512],
                start=(kd == 0), stop=(kd == 7)),
                w=[(p.sem["scalar"], s_xnfm), (p.sem["vector"], V())]
                if kd == 0 else ())
        vop(lambda e, q4=q4: e.tensor_copy(rbf[0:16, q4 * 512:(q4 + 1) * 512],
                                           PS[0][0:16, 0:512]),
            w=[(p.sem["tensor"], TE())])
    v_rbf = V()
    for kd in range(8):
        load(WG[:, kd * D:(kd + 1) * D], wg_h[kd * P:(kd + 1) * P, :], q="gpsimd")
    d_wg = DG()

    for tt in range(NT):
        tv0 = TE()
        for half in range(2):
            top(lambda e, tt=tt, half=half: e.matmul(
                PS[half][:, 0:512], rbf[0:17, tt * P:(tt + 1) * P],
                w2b_bf[0:17, half * 512:(half + 1) * 512], start=True, stop=True),
                w=[(p.sem["vector"], max(v_rbf, V())), (p.sem["scalar"], S())]
                if half == 0 else ())
        for half in range(2):
            for kd in range(8):
                top(lambda e, tt=tt, half=half, kd=kd: e.matmul(
                    PS[2 + half][:, 0:512], xn_fm[:, kd * T + tt * P:
                                                  kd * T + (tt + 1) * P],
                    WG[:, kd * D + half * 512: kd * D + (half + 1) * 512],
                    start=(kd == 0), stop=(kd == 7)),
                    w=[(p.dma["gpsimd"], d_wg)] if (tt == 0 and half == 0
                                                    and kd == 0) else ())
        tv = TE()
        sop(lambda e: e.activation(TMP[0][:, 0:512], PS[0][:, 0:512], AF.Sigmoid),
            w=[(p.sem["tensor"], tv), (p.dma["sync"], DS())])
        sop(lambda e: e.activation(TMP[0][:, 512:D], PS[1][:, 0:512], AF.Sigmoid))
        sop(lambda e: e.activation(TMP[1][:, 0:512], PS[2][:, 0:512], AF.Sigmoid))
        sop(lambda e: e.activation(TMP[1][:, 512:D], PS[3][:, 0:512], AF.Sigmoid))
        sv = S()
        store(scr_sig[tt * P:(tt + 1) * P, :], TMP[0][:],
              w=[(p.sem["scalar"], sv)])
        vop(lambda e: e.tensor_tensor(TMP[2][:, 0:512], PS[2][:, 0:512],
                                      TMP[1][:, 0:512], ALU.mult),
            w=[(p.sem["scalar"], sv), (p.dma["sync"], DS())])
        vop(lambda e: e.tensor_tensor(TMP[2][:, 512:D], PS[3][:, 0:512],
                                      TMP[1][:, 512:D], ALU.mult))
        vv = V()
        store(scr_sg[tt * P:(tt + 1) * P, :], TMP[2][:], w=[(p.sem["vector"], vv)])
    for tt in range(NT):
        dv = load(TMP[3][:], scr_sig[tt * P:(tt + 1) * P, :],
                  w=[(p.sem["scalar"], S())])
        sop(lambda e: e.activation(TMP[0][:], TMP[3][:], AF.Ln),
            w=[(p.dma["sync"], max(dv, DS()))])
        sv = S()
        store(scr_g[tt * P:(tt + 1) * P, :], TMP[0][:], w=[(p.sem["scalar"], sv)])
    d_gates = DS()

    if stop_after == 'B':
        return _finish(nc, p, st, debug_taps, scr_sig, scr_g, scr_sg, scr_sout,
                       scr_gp, xq_fm, xn_fm, store, V, DS, DG, T, D)
    for kd in range(8):
        load(WO[:, kd * D:(kd + 1) * D], wout_h[kd * P:(kd + 1) * P, :], q="gpsimd",
             w=[(p.sem["tensor"], TE())])
    d_wo = DG()
    vop(lambda e: e.memset(stateT[:, :], 0.0))
    v_state = V()
    d_dup = DS()
    v_wout_prev = 0
    v_oevac_prev = 0
    v_vt_prev = 0

    for c in range(NT):
        tc = c * P
        dv_g = load(gch[:], scr_g[tc:tc + P, :],
                    w=[(p.dma["sync"], d_gates), (p.sem["tensor"], TE()),
                       (p.sem["vector"], v_vt_prev if c > 0 else 0)])
        for half in range(2):
            for kd in range(8):
                top(lambda e, half=half, kd=kd, tc=tc: e.matmul(
                    PS[half][:, 0:512],
                    xn_fm[:, kd * T + tc: kd * T + tc + P],
                    WV[:, kd * D + half * 512: kd * D + (half + 1) * 512],
                    start=(kd == 0), stop=(kd == 7)),
                    w=[(p.sem["vector"], v_state), (p.dma["sync"], d_dup)]
                    if (half == 0 and kd == 0) else ())
        tv = TE()
        vop(lambda e: e.tensor_copy(vtm[:, 0:512], PS[0][:, 0:512]),
            w=[(p.sem["tensor"], tv)])
        vop(lambda e: e.tensor_copy(vtm[:, 512:D], PS[1][:, 0:512]))
        v_vtm = V()
        for oc in range(8):
            for kd in range(8):
                top(lambda e, oc=oc, kd=kd, tc=tc: e.matmul(
                    PS[4 + oc // 4][:, (oc % 4) * P:(oc % 4 + 1) * P],
                    WK[:, kd * D + oc * P: kd * D + (oc + 1) * P],
                    xn_fm[:, kd * T + tc: kd * T + tc + P],
                    start=(kd == 0), stop=(kd == 7)),
                    w=[(p.sem["vector"], v_oevac_prev)]
                    if (oc == 0 and kd == 0 and v_oevac_prev) else ())
        tv = TE()
        vop(lambda e: e.tensor_copy(kfm[:, 0:512], PS[4][:, 0:512]),
            w=[(p.sem["tensor"], tv)])
        vop(lambda e: e.tensor_copy(kfm[:, 512:D], PS[5][:, 0:512]))
        v_kfm = V()
        d_ktm = xbars([(ktm[:, oc * P:(oc + 1) * P], kfm[:, oc * P:(oc + 1) * P])
                       for oc in range(8)], w=[(p.sem["vector"], v_kfm)])
        for half in range(2):
            top(lambda e, half=half: e.matmul(
                PS[6 + half][:, 0:512], tri16[:, :],
                gch[:, half * 512:(half + 1) * 512], start=True, stop=True),
                w=[(p.dma["sync"], dv_g)] if half == 0 else ())
        tv = TE()
        sop(lambda e: e.activation(A_c[:, 0:512], PS[6][:, 0:512], AF.Exp),
            w=[(p.sem["tensor"], tv)])
        sop(lambda e: e.activation(A_c[:, 512:D], PS[7][:, 0:512], AF.Exp))
        sop(lambda e: e.activation(Ainv[:, 0:512], PS[6][:, 0:512], AF.Exp,
                                   scale=-1.0))
        sop(lambda e: e.activation(Ainv[:, 512:D], PS[7][:, 0:512], AF.Exp,
                                   scale=-1.0))
        sv_gates = S()
        for half in range(2):
            top(lambda e, half=half: e.matmul(
                PS[half][:, 0:512], oneh128[:, :],
                A_c[:, half * 512:(half + 1) * 512], start=True, stop=True),
                w=[(p.sem["scalar"], sv_gates), (p.sem["vector"], v_vtm)]
                if half == 0 else ())
        tv_acb = TE()
        vop(lambda e: e.tensor_tensor(vt_t[:, 0:512], vtm[:, 0:512],
                                      Ainv[:, 0:512], ALU.mult),
            w=[(p.sem["scalar"], sv_gates)])
        vop(lambda e: e.tensor_tensor(vt_t[:, 512:D], vtm[:, 512:D],
                                      Ainv[:, 512:D], ALU.mult))
        v_vt = V()
        v_vt_prev = v_vt
        for oc in range(8):
            for kd in range(8):
                top(lambda e, oc=oc, kd=kd, tc=tc: e.matmul(
                    PS[2 + oc // 4][:, (oc % 4) * P:(oc % 4 + 1) * P],
                    WQ[:, kd * D + oc * P: kd * D + (oc + 1) * P],
                    xn_fm[:, kd * T + tc: kd * T + tc + P],
                    start=(kd == 0), stop=(kd == 7)),
                    w=[(p.sem["vector"], v_wout_prev)]
                    if (oc == 0 and kd == 0 and v_wout_prev) else ())
        tv = TE()
        vop(lambda e: e.tensor_copy(qfm[:, 0:512], PS[2][:, 0:512]),
            w=[(p.sem["tensor"], tv)])
        vop(lambda e: e.tensor_copy(qfm[:, 512:D], PS[3][:, 0:512]))
        v_qfm = V()
        msk_done = [0] * 4
        intra_tv = [0] * 4
        for h in range(H):
            sl = h % 4
            rb = 64 * (h % 2)
            cb = (h // 2) * P
            w_s = []
            if h == 0:
                w_s.append((p.sem["vector"], max(v_kfm, v_qfm)))
            if msk_done[sl]:
                w_s.append((p.sem["vector"], msk_done[sl]))
            tv_s = top(lambda e, sl=sl, rb=rb, cb=cb: e.matmul(
                PS[2][:, sl * P:(sl + 1) * P],
                kfm[rb:rb + 64, cb:cb + P], qfm[rb:rb + 64, cb:cb + P],
                start=True, stop=True), w=w_s)
            w_m = [(p.sem["tensor"], tv_s), (p.sem["vector"], v_vt)]
            if intra_tv[sl]:
                w_m = [(p.sem["tensor"], max(tv_s, intra_tv[sl]))]
            v_m = vop(lambda e, sl=sl: e.tensor_tensor(
                mskt[:, sl * P:(sl + 1) * P], PS[2][:, sl * P:(sl + 1) * P],
                cmask[:, :], ALU.mult), w=w_m)
            msk_done[sl] = v_m
            ob = PS[4] if h < 8 else PS[5]
            oc0 = 64 * (h % 8)
            w_o = ([(p.sem["vector"], v_state)] + (
                [(p.dma["sync"], d_dup)] if d_dup else [])) if h == 0 else []
            top(lambda e, ob=ob, oc0=oc0, rb=rb, cb=cb, h=h: e.matmul(
                ob[:, oc0:oc0 + 64], qfm[rb:rb + 64, cb:cb + P],
                stateT[rb:rb + 64, 64 * h:64 * h + 64],
                start=True, stop=False), w=w_o)
            intra_tv[sl] = top(
                lambda e, ob=ob, oc0=oc0, h=h, sl=sl: e.matmul(
                    ob[:, oc0:oc0 + 64], mskt[:, sl * P:(sl + 1) * P],
                    vt_t[:, 64 * h:64 * h + 64],
                    start=False, stop=True),
                w=[(p.sem["vector"], max(v_m, v_vt))])
            sb_ = PS[6] if h < 8 else PS[7]
            top(lambda e, sb_=sb_, oc0=oc0, h=h: e.matmul(
                sb_[0:64, oc0:oc0 + 64], ktm[:, 64 * h:64 * h + 64],
                vt_t[:, 64 * h:64 * h + 64], start=True, stop=True),
                w=[(p.dma["sync"], d_ktm), (p.sem["scalar"], sv_gates)]
                if h == 0 else ())
        tv_heads = TE()
        vop(lambda e: e.tensor_tensor(TMP[0][0:64, 0:512], PS[6][0:64, 0:512],
                                      stateT[0:64, 0:512], ALU.add),
            w=[(p.sem["tensor"], tv_heads)])
        vop(lambda e: e.tensor_tensor(TMP[0][0:64, 512:D], PS[7][0:64, 0:512],
                                      stateT[0:64, 512:D], ALU.add))
        vop(lambda e: e.tensor_tensor(stateT[0:64, 0:512], TMP[0][0:64, 0:512],
                                      PS[0][0:64, 0:512], ALU.mult))
        vop(lambda e: e.tensor_tensor(stateT[0:64, 512:D], TMP[0][0:64, 512:D],
                                      PS[1][0:64, 0:512], ALU.mult))
        v_state = V()
        d_dup = p.dma_op("sync", lambda e: [e.dma_start(
            out=stateT[64:128, :], in_=stateT[0:64, :])],
            waits=[(p.sem["vector"], v_state)])
        vop(lambda e: e.tensor_tensor(TMP[0][:, 0:512], PS[4][:, 0:512],
                                      A_c[:, 0:512], ALU.mult))
        vop(lambda e: e.tensor_tensor(TMP[0][:, 512:D], PS[5][:, 0:512],
                                      A_c[:, 512:D], ALU.mult))
        v_oevac_prev = V()
        dv_sg = load(sgch[:], scr_sg[tc:tc + P, :],
                     w=[(p.sem["tensor"], tv_heads)])
        vopc(lambda e: e.reduce_sum(COLS["c0"][:, 0:1], TMP[0][:], AX.X))
        sopc(lambda e: e.activation(TMP[1][:], TMP[0][:], AF.Square),
            w=[(p.sem["vector"], V())])
        vopc(lambda e: e.reduce_sum(COLS["c1"][:, 0:1], TMP[1][:], AX.X),
            w=[(p.sem["scalar"], S())])
        vopc(lambda e: e.tensor_scalar(COLS["c0"][:, 0:1], COLS["c0"][:, 0:1],
                                      1.0 / D, None, ALU.mult))
        vopc(lambda e: e.tensor_tensor(COLS["c2"][:, 0:1], COLS["c0"][:, 0:1],
                                      COLS["c0"][:, 0:1], ALU.mult))
        vopc(lambda e: e.scalar_tensor_tensor(COLS["c1"][:, 0:1], COLS["c1"][:, 0:1],
                                             1.0 / D, COLS["c2"][:, 0:1],
                                             ALU.mult, ALU.subtract))
        vopc(lambda e: e.tensor_scalar(COLS["c1"][:, 0:1], COLS["c1"][:, 0:1],
                                      1e-5, None, ALU.add))
        sopc(lambda e: e.activation(COLS["c2"][:, 0:1], COLS["c1"][:, 0:1], AF.Ln),
            w=[(p.sem["vector"], V())])
        sopc(lambda e: e.activation(COLS["c2"][:, 0:1], COLS["c2"][:, 0:1],
                                   AF.Exp, scale=-0.5))
        vopc(lambda e: e.tensor_tensor(COLS["c3"][:, 0:1], COLS["c0"][:, 0:1],
                                      COLS["c2"][:, 0:1], ALU.mult),
            w=[(p.sem["scalar"], S())])
        vopc(lambda e: e.tensor_scalar(COLS["c3"][:, 0:1], COLS["c3"][:, 0:1],
                                      -1.0, None, ALU.mult))
        sopc(lambda e: e.activation(TMP[1][:], TMP[0][:], AF.Identity,
                                   scale=COLS["c2"][:, 0:1],
                                   bias=COLS["c3"][:, 0:1]),
            w=[(p.sem["vector"], V())])
        vop(lambda e: e.tensor_tensor(TMP[2][:], TMP[1][:], gamma_bc[:], ALU.mult),
            w=[(p.sem["scalar"], S())])
        vop(lambda e: e.tensor_tensor(TMP[1][:], TMP[2][:], beta_bc[:], ALU.add))
        vop(lambda e: e.tensor_tensor(TMP[2][:], TMP[1][:], sgch[:], ALU.mult),
            w=[(p.dma["sync"], dv_sg)])
        vop(lambda e: e.tensor_copy(xnbf[:], TMP[2][:]),
            w=[(p.dma["sync"], DS())])
        v_spre = V()
        d_sfm = xbars([(qfm[:, oc * P:(oc + 1) * P], xnbf[:, oc * P:(oc + 1) * P])
                       for oc in range(8)], w=[(p.sem["vector"], v_spre)])
        for half in range(2):
            for kd in range(8):
                top(lambda e, half=half, kd=kd: e.matmul(
                    PS[2 + half][:, 0:512], qfm[:, kd * P:(kd + 1) * P],
                    WO[:, kd * D + half * 512: kd * D + (half + 1) * 512],
                    start=(kd == 0), stop=(kd == 7)),
                    w=[(p.dma["sync"], d_sfm), (p.dma["gpsimd"], d_wo)]
                    if (half == 0 and kd == 0) else ())
        tv = TE()
        vop(lambda e: e.tensor_copy(TMP[3][:, 0:512], PS[2][:, 0:512]),
            w=[(p.sem["tensor"], tv), (p.dma["sync"], DS())])
        vop(lambda e: e.tensor_copy(TMP[3][:, 512:D], PS[3][:, 0:512]))
        v_wout_prev = V()
        vv = V()
        store(scr_sout[tc:tc + P, :], TMP[3][:], w=[(p.sem["vector"], vv)])
    d_sout_all = DS()

    if stop_after == 'C':
        return _finish(nc, p, st, debug_taps, scr_sig, scr_g, scr_sg, scr_sout,
                       scr_gp, xq_fm, xn_fm, store, V, DS, DG, T, D)
    dq_rd = [0, 0]
    for half in range(2):
        for ic in range(8):
            I0 = half * 2048 + ic * 256
            for wi, (src_h, dstq, kc, lnc) in enumerate(
                    ((gatew_h, gwq, "k_g", "lngs_g"), (upw_h, uwq, "k_u", "lngs_u"))):
                dvs = []
                for sub in range(2):
                    buf = wstr[2 * wi + sub]
                    dvs.append(load(buf[:], src_h[I0 + sub * P: I0 + (sub + 1) * P, :],
                                    w=[(p.sem["vector"], V())]))
                for sub in range(2):
                    amax_to(wstr[2 * wi + sub][:], 64, 64 * sub,
                            ws=[(p.dma["sync"], dvs[sub])])
                scale_pipe(None, 128, kc, lnc, None, None,
                           wv=[(p.sem["vector"], V())])
                for sub in range(2):
                    qb = (TMP[1], TMPX)[sub]
                    gq = quant_hot(wstr[2 * wi + sub][:], qb[:], 64,
                                   SC8[:, 64 * sub:64 * sub + 64],
                                   G8[:, 64 * sub:64 * sub + 64],
                                   ws=[(p.sem["vector"], V())], clip=False,
                                   out_ws=([(p.sem["tensor"], dq_rd[sub])]
                                           if dq_rd[sub] else []))
                    dq_rd[sub], _ = pe_xpose(
                        [(dstq[:, kd * 256 + sub * P: kd * 256 + (sub + 1) * P],
                          qb[:, kd * P:(kd + 1) * P]) for kd in range(8)],
                        PS[2], w=[(p.sem["gpsimd"], gq)])
            s_wslice = S()
            ev_done = [0, 0]
            for tt in range(NT):
                par = tt % 2
                pg, pu = (PS[0], PS[1]) if par == 0 else (PS[4], PS[5])
                w0 = [(p.sem["scalar"], s_wslice)]
                if ev_done[par]:
                    w0.append((p.sem["vector"], ev_done[par]))
                elif tt == 0:
                    w0.append((p.sem["vector"], V()))
                for kd in range(8):
                    top(lambda e, kd=kd, tt=tt, pg=pg: e.matmul(
                        pg[:, 0:256], xq_fm[:, kd * T + tt * P:
                                            kd * T + (tt + 1) * P],
                        gwq[:, kd * 256:(kd + 1) * 256],
                        start=(kd == 0), stop=(kd == 7)),
                        w=w0 if kd == 0 else ())
                for kd in range(8):
                    top(lambda e, kd=kd, tt=tt, pu=pu: e.matmul(
                        pu[:, 0:256], xq_fm[:, kd * T + tt * P:
                                            kd * T + (tt + 1) * P],
                        uwq[:, kd * 256:(kd + 1) * 256],
                        start=(kd == 0), stop=(kd == 7)))
                tv = TE()
                t0s, t1s = (TMP[0], TMP[1]) if par == 0 else (TMP[2], TMP[3])
                sop(lambda e, pg=pg, t0s=t0s: e.activation(
                    t0s[:, 0:256], pg[:, 0:256],
                    AF.Sigmoid, scale=COLS["a_gate"][:, 0:1]),
                    w=[(p.sem["tensor"], tv)])
                sv = S()
                vop(lambda e, pg=pg, t0s=t0s, t1s=t1s: e.tensor_tensor(
                    t1s[:, 0:256], pg[:, 0:256], t0s[:, 0:256], ALU.mult),
                    w=[(p.sem["scalar"], sv),
                       (p.sem["tensor"], max(dq_rd[0], dq_rd[1]))])
                vop(lambda e, tt=tt, pu=pu, t1s=t1s: e.scalar_tensor_tensor(
                    hid_c[:, tt * 256:(tt + 1) * 256], pu[:, 0:256],
                    COLS["a_gu"][:, 0:1], t1s[:, 0:256], ALU.mult, ALU.mult))
                ev_done[par] = V()
            for slab in range(4):
                amax_to(hid_c[:, slab * D:(slab + 1) * D], 64, 64 * slab,
                        ws=[(p.sem["vector"], V())] if slab == 0 else ())
            scale_pipe(None, 256, "k_h", "lngs_h", None, None,
                       wv=[(p.sem["vector"], V())])
            for slab in range(4):
                qb = (TMP[1], TMPX)[slab % 2]
                src = hid_c[:, slab * D:(slab + 1) * D]
                gq = quant_hot(src, qb[:], 64,
                               SC8[:, 64 * slab:64 * slab + 64],
                               G8[:, 64 * slab:64 * slab + 64],
                               ws=[(p.sem["vector"], V())], clip=True,
                               out_ws=([(p.sem["tensor"], dq_rd[slab % 2])]
                                       if dq_rd[slab % 2] else []))
                prs = []
                for j in range(4):
                    tt = slab * 4 + j
                    for ib in range(2):
                        ki = ic * 2 + ib
                        prs.append((hq_fm[:, ki * T + tt * P: ki * T + (tt + 1) * P],
                                    qb[:, j * 256 + ib * P:
                                        j * 256 + (ib + 1) * P]))
                dq_rd[slab % 2], _ = pe_xpose(prs, PS[2],
                                              w=[(p.sem["gpsimd"], gq)])
        s_hq = S()
        for i in range(8):
            dvs = []
            for sub in range(2):
                dvs.append(load(wstr[sub][:], downw_h[i * P:(i + 1) * P,
                                                      half * 2048 + sub * D:
                                                      half * 2048 + (sub + 1) * D],
                                w=[(p.sem["vector"], V())]))
            for sub in range(2):
                amax_to(wstr[sub][:], 64, 64 * sub, ws=[(p.dma["sync"], dvs[sub])])
            scale_pipe(None, 128, "k_d", "lngs_d", None, None,
                       wv=[(p.sem["vector"], V())])
            for sub in range(2):
                qb = (TMP[1], TMPX)[sub]
                gq = quant_hot(wstr[sub][:], qb[:], 64,
                               SC8[:, 64 * sub:64 * sub + 64],
                               G8[:, 64 * sub:64 * sub + 64],
                               ws=[(p.sem["vector"], V())], clip=False,
                               out_ws=([(p.sem["tensor"], dq_rd[sub])]
                                       if dq_rd[sub] else []))
                dq_rd[sub], _ = pe_xpose(
                    [(dwq[:, (sub * 8 + kk) * D + i * P:
                          (sub * 8 + kk) * D + (i + 1) * P],
                      qb[:, kk * P:(kk + 1) * P]) for kk in range(8)],
                    PS[2], w=[(p.sem["gpsimd"], gq)])
        s_dwq = S()
        ev2 = [0, 0]
        for tt in range(NT):
            par = tt % 2
            pa, pb = (PS[2], PS[3]) if par == 0 else (PS[6], PS[7])
            w0 = [(p.sem["scalar"], max(s_dwq, s_hq))]
            w0.append((p.sem["vector"], ev2[par] if ev2[par] else V()))
            for ki in range(16):
                for half2 in range(2):
                    top(lambda e, ki=ki, half2=half2, tt=tt, pa=pa, pb=pb: e.matmul(
                        (pa, pb)[half2][:, 0:512],
                        hq_fm[:, ki * T + tt * P: ki * T + (tt + 1) * P],
                        dwq[:, ki * D + half2 * 512: ki * D + (half2 + 1) * 512],
                        start=(ki == 0), stop=(ki == 15)),
                        w=w0 if (ki == 0 and half2 == 0) else ())
            tv = TE()
            t0s = TMP[0] if par == 0 else TMP[2]
            t1s = TMP[1] if par == 0 else TMP[3]
            if half == 0:
                vop(lambda e, pa=pa, t0s=t0s: e.tensor_copy(
                    t0s[:, 0:512], pa[:, 0:512]),
                    w=[(p.sem["tensor"], tv), (p.dma["sync"], DS())])
                vop(lambda e, pb=pb, t0s=t0s: e.tensor_copy(
                    t0s[:, 512:D], pb[:, 0:512]))
                vv = V()
                store(scr_gp[tt * P:(tt + 1) * P, :], t0s[:],
                      w=[(p.sem["vector"], vv)])
            else:
                dv = load(t1s[:], scr_gp[tt * P:(tt + 1) * P, :],
                          w=[(p.sem["vector"], ev2[par] if ev2[par] else V()),
                             (p.sem["tensor"], max(dq_rd[0], dq_rd[1]))])
                vop(lambda e, pa=pa, t0s=t0s, t1s=t1s: e.tensor_tensor(
                    t0s[:, 0:512], pa[:, 0:512], t1s[:, 0:512], ALU.add),
                    w=[(p.sem["tensor"], tv), (p.dma["sync"], max(dv, DS()))])
                vop(lambda e, pb=pb, t0s=t0s, t1s=t1s: e.tensor_tensor(
                    t0s[:, 512:D], pb[:, 0:512], t1s[:, 512:D], ALU.add))
                vv = V()
                store(scr_gp[tt * P:(tt + 1) * P, :], t0s[:],
                      w=[(p.sem["vector"], vv)])
            ev2[par] = V()
    d_gf = DS()

    for tt in range(NT):
        xb = wstr[0]
        dv1 = load(xb[:], x_h[tt * P:(tt + 1) * P, :],
                   w=[(p.sem["vector"], V()), (p.dma["sync"], d_gf)])
        dv2 = load(TMP[3][:], scr_gp[tt * P:(tt + 1) * P, :])
        dv3 = p.dma_op("gpsimd", lambda e, tt=tt: [e.dma_start(
            out=TMP[4][:], in_=scr_sout[tt * P:(tt + 1) * P, :])],
            waits=[(p.sem["vector"], V())])
        sop(lambda e: e.activation(TMP[0][:], TMP[3][:], AF.Sigmoid,
                                   scale=COLS["a_d"][:, 0:1]),
            w=[(p.dma["sync"], dv2)])
        sv = S()
        vop(lambda e: e.tensor_tensor(TMP[1][:], TMP[3][:], TMP[0][:], ALU.mult),
            w=[(p.sem["scalar"], sv)])
        vop(lambda e: e.scalar_tensor_tensor(TMP[2][:], TMP[1][:],
                                             COLS["a_d"][:, 0:1], TMP[4][:],
                                             ALU.mult, ALU.mult),
            w=[(p.dma["gpsimd"], dv3)])
        vop(lambda e, xb=xb: e.tensor_tensor(TMP[1][:], TMP[2][:], xb[:], ALU.add),
            w=[(p.dma["sync"], dv1)])
        vv = V()
        store(y_h[tt * P:(tt + 1) * P, :], TMP[1][:], w=[(p.sem["vector"], vv)])
    return _finish(nc, p, st, debug_taps, scr_sig, scr_g, scr_sg, scr_sout,
                   scr_gp, xq_fm, xn_fm, store, V, DS, DG, T, D)


def _finish(nc, p, st, debug_taps, scr_sig, scr_g, scr_sg, scr_sout, scr_gp,
            xq_fm, xn_fm, store, V, DS, DG, T, D):
    if debug_taps:
        for nm, scr in (("dbg_sig", scr_sig), ("dbg_g", scr_g), ("dbg_sg", scr_sg),
                        ("dbg_sout", scr_sout), ("dbg_gp", scr_gp)):
            dst = nc.dram_tensor(nm, [T, D], F32, kind="ExternalOutput").ap()
            store(dst, scr, w=[(p.sem["vector"], V())])
        dbx = nc.dram_tensor("dbg_xqfm", [P, 8 * T], BF16, kind="ExternalOutput").ap()
        store(dbx, xq_fm[:], w=[(p.sem["vector"], V())])
        dbn = nc.dram_tensor("dbg_xnfm", [P, 8 * T], BF16, kind="ExternalOutput").ap()
        store(dbn, xn_fm[:], w=[(p.sem["vector"], V())])
    d_final = DS()
    p.op("sync", lambda e: e.nop(), waits=[(p.dma["sync"], d_final),
                                           (p.dma["gpsimd"], DG()),
                                           (p.dma["scalar"], p.dma["scalar"].v)],
         inc=False)
    with nc.Block() as block:
        p.emit(block)
    st.close()
    return nc


_NC_CACHE = None


def _get_nc():
    global _NC_CACHE
    if _NC_CACHE is None:
        _NC_CACHE = build_program()
    return _NC_CACHE


def kernel(L0_attn_x, L0_attn_Wq, L0_attn_Wk, L0_attn_Wv, L0_attn_Wg,
           L0_attn_Wgk1, L0_attn_Wgk2, L0_attn_bgk2, L0_attn_gamma,
           L0_attn_beta, L0_attn_Wout, L0_ffn_gate_w, L0_ffn_up_w,
           L0_ffn_down_w, L0_ffn_s_in, L0_ffn_s_in_down):
    f32 = np.float32
    x = np.ascontiguousarray(L0_attn_x, dtype=f32)
    B = x.shape[0]
    w2b = np.ascontiguousarray(
        np.concatenate([np.asarray(L0_attn_Wgk2, f32),
                        np.asarray(L0_attn_bgk2, f32)[None, :]], axis=0))
    gamma_bc = np.ascontiguousarray(
        np.broadcast_to(np.asarray(L0_attn_gamma, f32)[None, :], (P, D)))
    beta_bc = np.ascontiguousarray(
        np.broadcast_to(np.asarray(L0_attn_beta, f32)[None, :], (P, D)))
    sx_col = np.full((P, 1), np.asarray(L0_ffn_s_in, f32)[0], f32)
    sd_col = np.full((P, 1), np.asarray(L0_ffn_s_in_down, f32)[0], f32)
    sidx = np.arange(P)
    tri16 = ((sidx[:, None] <= sidx[None, :]).astype(f32) / f32(16.0))
    cmask = (sidx[:, None] <= sidx[None, :]).astype(f32)
    oneh128 = np.zeros((P, P), f32); oneh128[P - 1, :] = 1.0
    onecol = np.ones((1, P), f32)
    identbf = np.eye(P, dtype=f32)

    shared = dict(
        wq=np.ascontiguousarray(L0_attn_Wq, f32),
        wk=np.ascontiguousarray(L0_attn_Wk, f32),
        wv=np.ascontiguousarray(L0_attn_Wv, f32),
        wg=np.ascontiguousarray(L0_attn_Wg, f32),
        wout=np.ascontiguousarray(L0_attn_Wout, f32),
        wgk1=np.ascontiguousarray(L0_attn_Wgk1, f32),
        w2b=w2b, gamma_bc=gamma_bc, beta_bc=beta_bc,
        sx_col=sx_col, sd_col=sd_col,
        gatew=np.ascontiguousarray(L0_ffn_gate_w, f32),
        upw=np.ascontiguousarray(L0_ffn_up_w, f32),
        downw=np.ascontiguousarray(L0_ffn_down_w, f32),
        tri16=tri16, cmask=cmask, oneh128=oneh128, onecol=onecol,
        identbf=identbf,
    )
    in_maps = [dict(shared, x=np.ascontiguousarray(x[b])) for b in range(B)]
    nc = _get_nc()
    res = run_bass_kernel_spmd(nc, in_maps, core_ids=list(range(B)))
    out = np.stack([np.asarray(res.results[b]["y"]) for b in range(B)], axis=0)
    return out.astype(np.float32)


# version 08
# speedup vs baseline: 1.0153x; 1.0065x over previous
import numpy as np
import concourse.bass as bass
import concourse.mybir as mybir
from concourse.bass_utils import run_bass_kernel_spmd
from contextlib import ExitStack

F32 = mybir.dt.float32
BF16 = mybir.dt.bfloat16
I32 = mybir.dt.int32
AF = mybir.ActivationFunctionType
ALU = mybir.AluOpType
AX = mybir.AxisListType

P = 128
T = 2048
D = 1024
NT = T // P
H = 16
DH = 64
IDIM = 4096
LN2 = 0.6931471805599453
INV_LN2 = 1.4426950408889634
KB = 1024

ENGS = ("sync", "vector", "scalar", "tensor", "gpsimd")


class Sem:
    def __init__(self, nc, st, name):
        self.h = st.enter_context(nc.semaphore(name))
        self.v = 0


class Prog:
    def __init__(self, nc, st, paranoid=False):
        self.nc = nc
        self.ops = []
        self.paranoid = paranoid
        self.sem = {e: Sem(nc, st, "sem_" + e) for e in ENGS}
        self.dma = {q: Sem(nc, st, "dma_" + q) for q in ("sync", "gpsimd", "scalar")}

    def _par_waits(self, eng):
        w = [(s, s.v) for e, s in self.sem.items() if e != eng]
        w += [(s, s.v) for s in self.dma.values()]
        return w

    def op(self, eng, fn, waits=(), inc=True):
        if self.paranoid:
            waits = list(waits) + self._par_waits(eng)
        incs = []
        if inc or self.paranoid:
            s = self.sem[eng]
            s.v += 1
            incs.append((s, 1))
        self.ops.append((eng, tuple(waits), tuple(incs), fn))
        return self.sem[eng].v

    def dma_op(self, q, fn, waits=(), n=1):
        if self.paranoid:
            waits = list(waits) + self._par_waits(None)
        s = self.dma[q]

        def wrapped(eng, fn=fn, s=s):
            insts = fn(eng)
            for inst in insts:
                inst.then_inc(s.h, 16)
            return None

        s.v += 16 * n
        self.ops.append((q, tuple(waits), (), wrapped))
        return s.v

    def emit(self, block):
        for ename in ENGS:
            ops = [o for o in self.ops if o[0] == ename]

            def mk(ops=ops):
                def f(eng):
                    for (_, waits, incs, fn) in ops:
                        best = {}
                        for (s, val) in waits:
                            if val > best.get(id(s), (None, -1))[1]:
                                best[id(s)] = (s, val)
                        for (s, val) in best.values():
                            eng.wait_ge(s.h, val)
                        inst = fn(eng)
                        for (s, n) in incs:
                            inst.then_inc(s.h, n)
                return f
            getattr(block, ename)(mk())


class Union:
    def __init__(self, nc, st, name, bytes_per_partition):
        self.nc = nc
        self.backing = st.enter_context(
            nc.sbuf_tensor(name, [P, bytes_per_partition], mybir.dt.uint8))
        self.base = nc.lookup_mloc(self.backing).addr
        self.size = bytes_per_partition

    def at(self, off, name, shape, dtype):
        per_part = int(np.prod(shape[1:])) * mybir.dt.size(dtype)
        assert off % 32 == 0 and off + per_part <= self.size, (name, off, per_part)
        return self.nc.alloc_sbuf_tensor_at(name, shape, dtype,
                                            offset=self.base + off)


def build_program(paranoid=False, debug_taps=False, stop_after='E'):
    nc = bass.Bass("TRN2", target_bir_lowering=False, debug=False,
                   detect_race_conditions=False)
    st = ExitStack()
    E = st.enter_context

    def din(name, shape, dt=F32):
        return nc.dram_tensor(name, shape, dt, kind="ExternalInput").ap()

    x_h = din("x", [T, D])
    wq_h = din("wq", [D, D]); wk_h = din("wk", [D, D]); wv_h = din("wv", [D, D])
    wg_h = din("wg", [D, D]); wout_h = din("wout", [D, D])
    wgk1_h = din("wgk1", [D, 16])
    w2b_h = din("w2b", [17, D])
    gatew_h = din("gatew", [IDIM, D]); upw_h = din("upw", [IDIM, D])
    downw_h = din("downw", [D, IDIM])
    gamma_h = din("gamma_bc", [P, D]); beta_h = din("beta_bc", [P, D])
    sx_h = din("sx_col", [P, 1]); sd_h = din("sd_col", [P, 1])
    tri16_h = din("tri16", [P, P]); cmask_h = din("cmask", [P, P])
    oneh128_h = din("oneh128", [P, P]); onecol_h = din("onecol", [1, P])
    identbf_h = din("identbf", [P, P])

    y_h = nc.dram_tensor("y", [T, D], F32, kind="ExternalOutput").ap()
    scr_sig = nc.dram_tensor("scr_sig", [T, D], F32, kind="Internal").ap()
    scr_g = nc.dram_tensor("scr_g", [T, D], F32, kind="Internal").ap()
    scr_sg = nc.dram_tensor("scr_sg", [T, D], F32, kind="Internal").ap()
    scr_sout = nc.dram_tensor("scr_sout", [T, D], F32, kind="Internal").ap()
    scr_gp = nc.dram_tensor("scr_gp", [T, D], F32, kind="Internal").ap()

    TMP = [E(nc.sbuf_tensor(f"tmp{i}", [P, D], F32)) for i in range(5)]
    TMPB = [E(nc.sbuf_tensor(f"tmpb{i}", [P, D], BF16)) for i in range(2)]
    TMPX = E(nc.sbuf_tensor("tmpx", [P, D], F32))
    xq_fm = E(nc.sbuf_tensor("xq_fm", [P, 8 * T], BF16))
    tri16 = E(nc.sbuf_tensor("tri16_sb", [P, P], F32))
    cmask = E(nc.sbuf_tensor("cmask_sb", [P, P], F32))
    oneh128 = E(nc.sbuf_tensor("oneh128_sb", [P, P], F32))
    identbf = E(nc.sbuf_tensor("identbf_sb", [P, P], F32))
    onecol = E(nc.sbuf_tensor("onecol_sb", [1, P], F32))
    gamma_bc = E(nc.sbuf_tensor("gamma_bc_sb", [P, D], F32))
    beta_bc = E(nc.sbuf_tensor("beta_bc_sb", [P, D], F32))
    wgk1_bf = E(nc.sbuf_tensor("wgk1_bf", [P, 8 * 16], BF16))
    w2b_bf = E(nc.sbuf_tensor("w2b_bf", [17, D], BF16))
    rbf = E(nc.sbuf_tensor("rbf", [32, T], BF16))
    stateT = E(nc.sbuf_tensor("stateT", [P, H * DH], BF16))
    xnbf = E(nc.sbuf_tensor("xnbf", [P, D], BF16))

    onerow = E(nc.sbuf_tensor("onerow", [1, P + 4], F32))
    SC8 = E(nc.sbuf_tensor("sc8t", [P, 256], F32))
    AMX = E(nc.sbuf_tensor("amxt", [P, 256], F32))
    G8 = E(nc.sbuf_tensor("g8t", [P, 256], F32))
    SCT = [E(nc.sbuf_tensor(f"sct{i}", [P, 256], F32)) for i in range(4)]
    SCI = E(nc.sbuf_tensor("sci", [P, D], I32))
    TI = SCI
    COLS = {}
    for nm in ["sx", "sd", "k_x", "k_h", "lngs_x", "lngs_h",
               "gm_g", "gm_u", "gm_d", "gs_g", "gs_u", "gs_d",
               "k_g", "k_u", "k_d", "lngs_g", "lngs_u", "lngs_d",
               "a_gate", "a_gu", "a_d", "c0", "c1", "c2", "c3", "c4", "m16"]:
        COLS[nm] = E(nc.sbuf_tensor("col_" + nm, [P, 1], F32))

    U = Union(nc, st, "union", 114 * KB)
    xn_fm = U.at(0, "xn_fm", [P, 8 * T], BF16)
    WQ = U.at(32 * KB, "WQ", [P, 8 * D], BF16)
    WK = U.at(48 * KB, "WK", [P, 8 * D], BF16)
    WV = U.at(64 * KB, "WV", [P, 8 * D], BF16)
    WO = U.at(80 * KB, "WO", [P, 8 * D], BF16)
    WG = U.at(80 * KB, "WG", [P, 8 * D], BF16)
    qfm = U.at(96 * KB, "qfm", [P, 8 * P], BF16)
    kfm = U.at(98 * KB, "kfm", [P, 8 * P], BF16)
    ktm = U.at(100 * KB, "ktm", [P, D], BF16)
    vtm = U.at(102 * KB, "vtm", [P, D], BF16)
    vt_t = U.at(104 * KB, "vt_t", [P, D], BF16)
    A_c = U.at(106 * KB, "A_c", [P, D], F32)
    gch = U.at(110 * KB, "gch", [P, D], F32)
    Ainv = U.at(110 * KB, "Ainv", [P, D], F32)
    sgch = U.at(100 * KB, "sgch", [P, D], F32)
    mskt = U.at(113 * KB, "mskt", [P, 4 * P], BF16)
    hq_fm = U.at(0, "hq_fm", [P, 16 * T], BF16)
    dwq = U.at(64 * KB, "dwq", [P, 16 * D], BF16)
    hid_c = U.at(64 * KB, "hid_c", [P, NT * 256], F32)
    gwq = U.at(80 * KB, "gwq", [P, 8 * 256], BF16)
    uwq = U.at(84 * KB, "uwq", [P, 8 * 256], BF16)
    wstr = [U.at((96 + 4 * i) * KB, f"wstr{i}", [P, D], F32) for i in range(4)]

    PS = [E(nc.psum_tensor(f"ps{i}", [P, 512], F32)) for i in range(8)]

    p = Prog(nc, st, paranoid=paranoid)
    V = lambda: p.sem["vector"].v
    S = lambda: p.sem["scalar"].v
    TE = lambda: p.sem["tensor"].v
    DS = lambda: p.dma["sync"].v
    DG = lambda: p.dma["gpsimd"].v

    def vop(fn, w=()):
        return p.op("vector", fn, waits=w)

    def sop(fn, w=()):
        return p.op("scalar", fn, waits=w)

    def vopc(fn, w=()):
        return p.op("vector", fn, waits=list(w) + [(p.sem["vector"], V())])

    def sopc(fn, w=()):
        return p.op("scalar", fn, waits=list(w) + [(p.sem["scalar"], S())])

    def top(fn, w=()):
        return p.op("tensor", fn, waits=w)

    def load(dst, src, w=(), q="sync"):
        return p.dma_op(q, lambda e: [e.dma_start(out=dst, in_=src)], waits=w)

    def store(dst, src, w=(), q="sync"):
        return p.dma_op(q, lambda e: [e.dma_start(out=dst, in_=src)], waits=w)

    def loads(pairs, w=(), q="sync"):
        return p.dma_op(q, lambda e: [e.dma_start(out=d, in_=s) for d, s in pairs],
                        waits=w, n=len(pairs))

    pexp_state = {"ev": 0}

    def pe_xpose(pairs, ps, w=()):
        first = True
        for g0 in range(0, len(pairs), 4):
            grp = pairs[g0:g0 + 4]
            for j, (dst, srcp) in enumerate(grp):
                ww = []
                if first:
                    ww = list(w)
                    first = False
                if pexp_state["ev"]:
                    ww.append((p.sem["scalar"], pexp_state["ev"]))
                    pexp_state["ev"] = 0
                top(lambda e, j=j, srcp=srcp, ps=ps: e.transpose(
                    ps[:, j * P:(j + 1) * P], srcp, identbf[:]), w=ww)
            tv = TE()
            for j, (dst, srcp) in enumerate(grp):
                sop(lambda e, j=j, dst=dst, ps=ps: e.copy(dst, ps[:, j * P:(j + 1) * P]),
                    w=[(p.sem["tensor"], tv)] if j == 0 else ())
            pexp_state["ev"] = S()
        return TE(), S()

    def xbars(pairs, w=(), q="sync"):
        return p.dma_op(q, lambda e: [e.dma_start_transpose(d, s)
                                      for d, s in pairs], waits=w, n=len(pairs))

    def scale_pipe(amax_ap, nb, kcol, lncol, sc8_ap, g8_ap, wv):
        W = 256
        s0, s1, s2, s3 = (t[:, 0:W] for t in SCT)
        si = SCI[:, 0:W]
        amx = AMX[:, 0:W]
        vop(lambda e: e.tensor_scalar(s0, amx, COLS[kcol][:, 0:1], 448.0,
                                      ALU.mult, ALU.min), w=wv)
        vop(lambda e: e.tensor_scalar(s0, s0, 2.0 ** -20, None, ALU.max))
        sop(lambda e: e.activation(s1, s0, AF.Ln), w=[(p.sem["vector"], V())])
        vop(lambda e: e.tensor_scalar(s2, s1, INV_LN2, None, ALU.mult),
            w=[(p.sem["scalar"], S())])
        vop(lambda e: e.tensor_scalar(si, s2, 32.0, None, ALU.add))
        vop(lambda e: e.tensor_scalar(s3, si, -32.0, None, ALU.add))
        vop(lambda e: e.tensor_tensor(s1, s3, s2, ALU.is_gt))
        vop(lambda e: e.tensor_tensor(s3, s3, s1, ALU.subtract))
        sop(lambda e: e.activation(s2, s3, AF.Exp, scale=-LN2),
            w=[(p.sem["vector"], V())])
        vop(lambda e: e.tensor_tensor(s2, s0, s2, ALU.mult),
            w=[(p.sem["scalar"], S())])
        vop(lambda e: e.tensor_scalar(s0, s2, 8.0, 0.5, ALU.mult, ALU.add))
        vop(lambda e: e.tensor_copy(si, s0))
        vop(lambda e: e.tensor_copy(s2, si))
        vop(lambda e: e.tensor_tensor(s1, s2, s0, ALU.is_gt))
        vop(lambda e: e.tensor_tensor(s2, s2, s1, ALU.subtract))
        sop(lambda e: e.activation(s1, s3, AF.Exp, scale=LN2),
            w=[(p.sem["vector"], V())])
        vop(lambda e: e.scalar_tensor_tensor(SC8[:, 0:W], s2, 0.125, s1, ALU.mult,
                                             ALU.mult), w=[(p.sem["scalar"], S())])
        sop(lambda e: e.activation(s0, SC8[:, 0:W], AF.Ln),
            w=[(p.sem["vector"], V())])
        sop(lambda e: e.activation(G8[:, 0:W], s0, AF.Exp, scale=-1.0,
                                   bias=COLS[lncol][:, 0:1]))

    GQ = {"prev": 0}

    def quant_hot(src_ap, out_ap, nb, sc8_ap, g8_ap, ws, clip, out_ws=None):
        if out_ws is None:
            out_ws = [(p.dma["sync"], DS()),
                      (p.dma["scalar"], p.dma["scalar"].v)]
        F = 16 * nb
        t0, t1, t2, t3 = (t[:, 0:F] for t in TMP[0:4])
        b0, b1 = (t[:, 0:F] for t in TMPB)
        b2 = TMP[4][:, 0:F]
        ti = TI[:, 0:F]
        src3 = src_ap.rearrange("p (b k) -> p b k", k=16)
        g83 = g8_ap.broadcast_to([P, nb, 16])
        sc83 = sc8_ap.broadcast_to([P, nb, 16])
        vop(lambda e: e.tensor_tensor(t0.rearrange("p (b k) -> p b k", k=16),
                                      src3, g83, ALU.mult),
            w=list(ws) + [(p.sem["scalar"], S())])
        if clip:
            vop(lambda e: e.tensor_scalar(t0, t0, 6.0, -6.0, ALU.min, ALU.max))
        vop(lambda e: e.tensor_tensor(b0, t0, t0, ALU.mult))
        vop(lambda e: e.tensor_scalar(b1, b0, 4.0, 0.5, ALU.is_ge, ALU.mult))
        vop(lambda e: e.scalar_tensor_tensor(b2, b0, 16.0, b1, ALU.is_ge, ALU.add))
        vop(lambda e: e.tensor_scalar(b0, b2, -0.5, 2.0, ALU.mult, ALU.add))
        vop(lambda e: e.scalar_tensor_tensor(b0, b1, -1.5, b0, ALU.mult, ALU.add))
        vop(lambda e: e.tensor_tensor(t1, t0, b0, ALU.mult))
        vop(lambda e: e.tensor_scalar(ti, t1, 16.5, None, ALU.add))
        vop(lambda e: e.tensor_scalar(t2, ti, -16.0, None, ALU.add))
        vop(lambda e: e.scalar_tensor_tensor(t3, t1, 0.5, t2, ALU.add, ALU.is_lt),
            w=[(p.sem["gpsimd"], GQ["prev"])] if GQ["prev"] else ())
        vop(lambda e: e.tensor_tensor(t2, t2, t3, ALU.subtract))
        vop(lambda e: e.scalar_tensor_tensor(t3, b2, 0.5, t2, ALU.add, ALU.mult))
        vv = V()
        GQ["prev"] = p.op("gpsimd", lambda e: e.tensor_tensor(
            out_ap.rearrange("p (b k) -> p b k", k=16),
            t3.rearrange("p (b k) -> p b k", k=16),
            sc83, ALU.mult),
            waits=[(p.sem["vector"], vv)] + list(out_ws))
        return GQ["prev"]


    def amax_to(src_ap, nb, off, ws=()):
        return vop(lambda e: e.tensor_reduce(
            AMX[:, off:off + nb], src_ap.rearrange("p (b k) -> p b k", k=16),
            AX.X, ALU.max, apply_absolute_value=True), w=ws)

    def quantize_tile(src_ap, out_ap, nb, kcol, lncol, ws, clip):
        amax_to(src_ap, nb, 0, ws=ws)
        scale_pipe(None, nb, kcol, lncol, None, None,
                   wv=[(p.sem["vector"], V())])
        return quant_hot(src_ap, out_ap, nb, SC8[:, 0:nb], G8[:, 0:nb],
                         ws=[(p.sem["vector"], V())], clip=clip)

    loads([(tri16[:], tri16_h), (cmask[:], cmask_h), (oneh128[:], oneh128_h),
           (onecol[:], onecol_h), (gamma_bc[:], gamma_h), (beta_bc[:], beta_h),
           (COLS["sx"][:], sx_h), (COLS["sd"][:], sd_h), (identbf[:], identbf_h)])
    d_consts = DS()
    for W_, src in ((WQ, wq_h), (WK, wk_h), (WV, wv_h)):
        for kd in range(8):
            load(W_[:, kd * D:(kd + 1) * D], src[kd * P:(kd + 1) * P, :], q="gpsimd")
    for kd in range(8):
        load(wgk1_bf[:, kd * 16:(kd + 1) * 16], wgk1_h[kd * P:(kd + 1) * P, :],
             q="gpsimd")
    load(w2b_bf[:], w2b_h, q="gpsimd")
    d_attnw = DG()

    vop(lambda e: e.memset(AMX[:, :], 1.0))
    vop(lambda e: e.memset(COLS["m16"][:, 0:1], -16.0))
    vopc(lambda e: e.tensor_scalar(COLS["k_x"][:, 0:1], COLS["sx"][:, 0:1],
                                   1.0 / 6.0, None, ALU.mult),
         w=[(p.dma["sync"], d_consts)])
    vopc(lambda e: e.tensor_scalar(COLS["k_h"][:, 0:1], COLS["sd"][:, 0:1],
                                   1.0 / 6.0, None, ALU.mult))
    sop(lambda e: e.activation(COLS["lngs_x"][:, 0:1], COLS["sx"][:, 0:1], AF.Ln),
        w=[(p.sem["vector"], V())])
    sop(lambda e: e.activation(COLS["lngs_h"][:, 0:1], COLS["sd"][:, 0:1], AF.Ln))

    def gmax_stream(tiles, gm):
        vop(lambda e: e.memset(COLS[gm][:, 0:1], 0.0))
        for i, src in enumerate(tiles):
            buf = wstr[i % 2]
            w_ = src.shape[1]
            dv = load(buf[:, 0:w_], src, w=[(p.sem["vector"], V())])
            vop(lambda e, b=buf, w2=w_: e.tensor_reduce(
                COLS["c4"][:, 0:1], b[:, 0:w2], AX.X, ALU.max,
                apply_absolute_value=True), w=[(p.dma["sync"], dv)])
            vopc(lambda e, g=gm: e.tensor_tensor(COLS[g][:, 0:1], COLS[g][:, 0:1],
                                                 COLS["c4"][:, 0:1], ALU.max))
        vopc(lambda e, g=gm: e.tensor_copy(
            TMP[1][:, 0:32].rearrange("p (one k) -> p one k", k=32),
            COLS[g][:, 0:1].broadcast_to([P, 1, 32])))
        for j in range(4):
            vopc(lambda e, j=j: e.transpose(TMP[0][0:32, 32 * j:32 * (j + 1)],
                                            TMP[1][32 * j:32 * (j + 1), 0:32]))
        vopc(lambda e: e.tensor_reduce(onerow[0:1, P:P + 1], TMP[0][0:1, 0:P],
                                      AX.X, ALU.max))
        vopc(lambda e: e.reciprocal(onerow[0:1, P + 1:P + 2], onerow[0:1, P:P + 1]))
        vopc(lambda e: e.tensor_scalar(onerow[0:1, P + 2:P + 3],
                                      onerow[0:1, P + 1:P + 2], 2688.0, None,
                                      ALU.mult))

    def gs_bcast(gs, kc, lnc):
        tv = top(lambda e: e.matmul(PS[0][:, 0:1], onecol[0:1, 0:P],
                                    onerow[0:1, P + 2:P + 3], start=True, stop=True),
                 w=[(p.sem["vector"], V())])
        vopc(lambda e: e.tensor_copy(COLS[gs][:, 0:1], PS[0][:, 0:1]),
             w=[(p.sem["tensor"], tv)])
        vopc(lambda e: e.tensor_scalar(COLS[kc][:, 0:1], COLS[gs][:, 0:1],
                                       1.0 / 6.0, None, ALU.mult))
        sopc(lambda e: e.activation(COLS[lnc][:, 0:1], COLS[gs][:, 0:1], AF.Ln),
             w=[(p.sem["vector"], V())])

    gmax_stream([gatew_h[i * P:(i + 1) * P, :] for i in range(32)], "gm_g")
    gs_bcast("gs_g", "k_g", "lngs_g")
    gmax_stream([upw_h[i * P:(i + 1) * P, :] for i in range(32)], "gm_u")
    gs_bcast("gs_u", "k_u", "lngs_u")
    gmax_stream([downw_h[i * P:(i + 1) * P, j * D:(j + 1) * D]
                 for i in range(8) for j in range(4)], "gm_d")
    gs_bcast("gs_d", "k_d", "lngs_d")

    def recip_into(dst, a, b):
        vopc(lambda e: e.tensor_tensor(COLS["c4"][:, 0:1], COLS[a][:, 0:1],
                                       COLS[b][:, 0:1], ALU.mult))
        vopc(lambda e: e.reciprocal(COLS[dst][:, 0:1], COLS["c4"][:, 0:1]))

    recip_into("a_gate", "sx", "gs_g")
    recip_into("c0", "sx", "gs_u")
    vopc(lambda e: e.tensor_tensor(COLS["a_gu"][:, 0:1], COLS["a_gate"][:, 0:1],
                                   COLS["c0"][:, 0:1], ALU.mult))
    recip_into("a_d", "sd", "gs_d")

    xq_rd = [0, 0]
    xn_rd = 0
    for tt in range(NT):
        xb = TMP[1]
        dv = load(xb[:], x_h[tt * P:(tt + 1) * P, :],
                  w=[(p.sem["vector"], V()), (p.sem["scalar"], S()),
                     (p.sem["tensor"], max(xq_rd[0], xq_rd[1])),
                     (p.sem["gpsimd"], p.sem["gpsimd"].v)])
        vopc(lambda e, xb=xb: e.reduce_sum(COLS["c0"][:, 0:1], xb[:], AX.X),
             w=[(p.dma["sync"], dv)])
        sop(lambda e, xb=xb: e.activation(TMP[0][:], xb[:], AF.Square),
            w=[(p.sem["vector"], V())])
        vopc(lambda e: e.reduce_sum(COLS["c1"][:, 0:1], TMP[0][:], AX.X),
             w=[(p.sem["scalar"], S())])
        vopc(lambda e: e.tensor_scalar(COLS["c0"][:, 0:1], COLS["c0"][:, 0:1],
                                       1.0 / D, None, ALU.mult))
        vopc(lambda e: e.tensor_tensor(COLS["c2"][:, 0:1], COLS["c0"][:, 0:1],
                                       COLS["c0"][:, 0:1], ALU.mult))
        vopc(lambda e: e.scalar_tensor_tensor(COLS["c1"][:, 0:1], COLS["c1"][:, 0:1],
                                              1.0 / D, COLS["c2"][:, 0:1],
                                              ALU.mult, ALU.subtract))
        vopc(lambda e: e.tensor_scalar(COLS["c1"][:, 0:1], COLS["c1"][:, 0:1],
                                       1e-5, None, ALU.add))
        sopc(lambda e: e.activation(COLS["c2"][:, 0:1], COLS["c1"][:, 0:1], AF.Ln),
             w=[(p.sem["vector"], V())])
        sopc(lambda e: e.activation(COLS["c2"][:, 0:1], COLS["c2"][:, 0:1],
                                    AF.Exp, scale=-0.5))
        vopc(lambda e: e.tensor_tensor(COLS["c3"][:, 0:1], COLS["c0"][:, 0:1],
                                       COLS["c2"][:, 0:1], ALU.mult),
             w=[(p.sem["scalar"], S())])
        vopc(lambda e: e.tensor_scalar(COLS["c3"][:, 0:1], COLS["c3"][:, 0:1],
                                       -1.0, None, ALU.mult))
        sopc(lambda e, xb=xb: e.activation(TMP[4][:], xb[:], AF.Identity,
                                           scale=COLS["c2"][:, 0:1],
                                           bias=COLS["c3"][:, 0:1]),
             w=[(p.sem["vector"], V())])
        sv_xn = S()
        xn_rd, _ = pe_xpose([(xn_fm[:, kd * T + tt * P: kd * T + (tt + 1) * P],
                              TMP[4][:, kd * P:(kd + 1) * P]) for kd in range(8)],
                            PS[0], w=[(p.sem["scalar"], sv_xn),
                                      (p.dma["gpsimd"], d_attnw)])
        amax_to(TMP[4][:], 64, 0, ws=[(p.sem["scalar"], sv_xn)])
        scale_pipe(None, 64, "k_x", "lngs_x", None, None,
                   wv=[(p.sem["vector"], V())])
        qb = (TMP[1], TMPX)[tt % 2]
        gq = quant_hot(TMP[4][:], qb[:], 64, SC8[:, 0:64], G8[:, 0:64],
                       ws=[(p.sem["vector"], V()), (p.sem["tensor"], xn_rd)],
                       clip=True,
                       out_ws=([(p.sem["tensor"], xq_rd[tt % 2])]
                               if xq_rd[tt % 2] else []))
        xq_rd[tt % 2], _ = pe_xpose(
            [(xq_fm[:, kd * T + tt * P: kd * T + (tt + 1) * P],
              qb[:, kd * P:(kd + 1) * P]) for kd in range(8)],
            PS[1], w=[(p.sem["gpsimd"], gq)])
    s_xnfm = S()

    if stop_after == 'A':
        return _finish(nc, p, st, debug_taps, scr_sig, scr_g, scr_sg, scr_sout,
                       scr_gp, xq_fm, xn_fm, store, V, DS, DG, T, D)
    vop(lambda e: e.memset(rbf[0:32, :], 1.0))
    v_rones = V()
    for q4 in range(4):
        for kd in range(8):
            top(lambda e, q4=q4, kd=kd: e.matmul(
                PS[0][0:16, 0:512], wgk1_bf[:, kd * 16:(kd + 1) * 16],
                xn_fm[:, kd * T + q4 * 512: kd * T + (q4 + 1) * 512],
                start=(kd == 0), stop=(kd == 7)),
                w=[(p.sem["scalar"], s_xnfm), (p.sem["vector"], V())]
                if kd == 0 else ())
        vop(lambda e, q4=q4: e.tensor_copy(rbf[0:16, q4 * 512:(q4 + 1) * 512],
                                           PS[0][0:16, 0:512]),
            w=[(p.sem["tensor"], TE())])
    v_rbf = V()
    for kd in range(8):
        load(WG[:, kd * D:(kd + 1) * D], wg_h[kd * P:(kd + 1) * P, :], q="gpsimd")
    d_wg = DG()

    for tt in range(NT):
        tv0 = TE()
        for half in range(2):
            top(lambda e, tt=tt, half=half: e.matmul(
                PS[half][:, 0:512], rbf[0:17, tt * P:(tt + 1) * P],
                w2b_bf[0:17, half * 512:(half + 1) * 512], start=True, stop=True),
                w=[(p.sem["vector"], max(v_rbf, V())), (p.sem["scalar"], S())]
                if half == 0 else ())
        for half in range(2):
            for kd in range(8):
                top(lambda e, tt=tt, half=half, kd=kd: e.matmul(
                    PS[2 + half][:, 0:512], xn_fm[:, kd * T + tt * P:
                                                  kd * T + (tt + 1) * P],
                    WG[:, kd * D + half * 512: kd * D + (half + 1) * 512],
                    start=(kd == 0), stop=(kd == 7)),
                    w=[(p.dma["gpsimd"], d_wg)] if (tt == 0 and half == 0
                                                    and kd == 0) else ())
        tv = TE()
        sop(lambda e: e.activation(TMP[0][:, 0:512], PS[0][:, 0:512], AF.Sigmoid),
            w=[(p.sem["tensor"], tv), (p.dma["sync"], DS())])
        sop(lambda e: e.activation(TMP[0][:, 512:D], PS[1][:, 0:512], AF.Sigmoid))
        sop(lambda e: e.activation(TMP[1][:, 0:512], PS[2][:, 0:512], AF.Sigmoid))
        sop(lambda e: e.activation(TMP[1][:, 512:D], PS[3][:, 0:512], AF.Sigmoid))
        sv = S()
        store(scr_sig[tt * P:(tt + 1) * P, :], TMP[0][:],
              w=[(p.sem["scalar"], sv)])
        vop(lambda e: e.tensor_tensor(TMP[2][:, 0:512], PS[2][:, 0:512],
                                      TMP[1][:, 0:512], ALU.mult),
            w=[(p.sem["scalar"], sv), (p.dma["sync"], DS())])
        vop(lambda e: e.tensor_tensor(TMP[2][:, 512:D], PS[3][:, 0:512],
                                      TMP[1][:, 512:D], ALU.mult))
        vv = V()
        store(scr_sg[tt * P:(tt + 1) * P, :], TMP[2][:], w=[(p.sem["vector"], vv)])
    for tt in range(NT):
        dv = load(TMP[3][:], scr_sig[tt * P:(tt + 1) * P, :],
                  w=[(p.sem["scalar"], S())])
        sop(lambda e: e.activation(TMP[0][:], TMP[3][:], AF.Ln),
            w=[(p.dma["sync"], max(dv, DS()))])
        sv = S()
        store(scr_g[tt * P:(tt + 1) * P, :], TMP[0][:], w=[(p.sem["scalar"], sv)])
    d_gates = DS()

    if stop_after == 'B':
        return _finish(nc, p, st, debug_taps, scr_sig, scr_g, scr_sg, scr_sout,
                       scr_gp, xq_fm, xn_fm, store, V, DS, DG, T, D)
    for kd in range(8):
        load(WO[:, kd * D:(kd + 1) * D], wout_h[kd * P:(kd + 1) * P, :], q="gpsimd",
             w=[(p.sem["tensor"], TE())])
    d_wo = DG()
    vop(lambda e: e.memset(stateT[:, :], 0.0))
    v_state = V()
    d_dup = DS()
    v_wout_prev = 0
    v_oevac_prev = 0
    v_vt_prev = 0

    for c in range(NT):
        tc = c * P
        dv_g = load(gch[:], scr_g[tc:tc + P, :],
                    w=[(p.dma["sync"], d_gates), (p.sem["tensor"], TE()),
                       (p.sem["vector"], v_vt_prev if c > 0 else 0)])
        for half in range(2):
            for kd in range(8):
                top(lambda e, half=half, kd=kd, tc=tc: e.matmul(
                    PS[half][:, 0:512],
                    xn_fm[:, kd * T + tc: kd * T + tc + P],
                    WV[:, kd * D + half * 512: kd * D + (half + 1) * 512],
                    start=(kd == 0), stop=(kd == 7)),
                    w=[(p.sem["vector"], v_state), (p.dma["sync"], d_dup)]
                    if (half == 0 and kd == 0) else ())
        tv = TE()
        vop(lambda e: e.tensor_copy(vtm[:, 0:512], PS[0][:, 0:512]),
            w=[(p.sem["tensor"], tv)])
        vop(lambda e: e.tensor_copy(vtm[:, 512:D], PS[1][:, 0:512]))
        v_vtm = V()
        for oc in range(8):
            for kd in range(8):
                top(lambda e, oc=oc, kd=kd, tc=tc: e.matmul(
                    PS[4 + oc // 4][:, (oc % 4) * P:(oc % 4 + 1) * P],
                    WK[:, kd * D + oc * P: kd * D + (oc + 1) * P],
                    xn_fm[:, kd * T + tc: kd * T + tc + P],
                    start=(kd == 0), stop=(kd == 7)),
                    w=[(p.sem["vector"], v_oevac_prev)]
                    if (oc == 0 and kd == 0 and v_oevac_prev) else ())
        tv = TE()
        vop(lambda e: e.tensor_copy(kfm[:, 0:512], PS[4][:, 0:512]),
            w=[(p.sem["tensor"], tv)])
        vop(lambda e: e.tensor_copy(kfm[:, 512:D], PS[5][:, 0:512]))
        v_kfm = V()
        d_ktm = xbars([(ktm[:, oc * P:(oc + 1) * P], kfm[:, oc * P:(oc + 1) * P])
                       for oc in range(8)], w=[(p.sem["vector"], v_kfm)])
        for half in range(2):
            top(lambda e, half=half: e.matmul(
                PS[6 + half][:, 0:512], tri16[:, :],
                gch[:, half * 512:(half + 1) * 512], start=True, stop=True),
                w=[(p.dma["sync"], dv_g)] if half == 0 else ())
        tv = TE()
        sop(lambda e: e.activation(A_c[:, 0:512], PS[6][:, 0:512], AF.Exp),
            w=[(p.sem["tensor"], tv)])
        sop(lambda e: e.activation(A_c[:, 512:D], PS[7][:, 0:512], AF.Exp))
        sop(lambda e: e.activation(Ainv[:, 0:512], PS[6][:, 0:512], AF.Exp,
                                   scale=-1.0))
        sop(lambda e: e.activation(Ainv[:, 512:D], PS[7][:, 0:512], AF.Exp,
                                   scale=-1.0))
        sv_gates = S()
        for half in range(2):
            top(lambda e, half=half: e.matmul(
                PS[half][:, 0:512], oneh128[:, :],
                A_c[:, half * 512:(half + 1) * 512], start=True, stop=True),
                w=[(p.sem["scalar"], sv_gates), (p.sem["vector"], v_vtm)]
                if half == 0 else ())
        tv_acb = TE()
        vop(lambda e: e.tensor_tensor(vt_t[:, 0:512], vtm[:, 0:512],
                                      Ainv[:, 0:512], ALU.mult),
            w=[(p.sem["scalar"], sv_gates)])
        vop(lambda e: e.tensor_tensor(vt_t[:, 512:D], vtm[:, 512:D],
                                      Ainv[:, 512:D], ALU.mult))
        v_vt = V()
        v_vt_prev = v_vt
        for oc in range(8):
            for kd in range(8):
                top(lambda e, oc=oc, kd=kd, tc=tc: e.matmul(
                    PS[2 + oc // 4][:, (oc % 4) * P:(oc % 4 + 1) * P],
                    WQ[:, kd * D + oc * P: kd * D + (oc + 1) * P],
                    xn_fm[:, kd * T + tc: kd * T + tc + P],
                    start=(kd == 0), stop=(kd == 7)),
                    w=[(p.sem["vector"], v_wout_prev)]
                    if (oc == 0 and kd == 0 and v_wout_prev) else ())
        tv = TE()
        vop(lambda e: e.tensor_copy(qfm[:, 0:512], PS[2][:, 0:512]),
            w=[(p.sem["tensor"], tv)])
        vop(lambda e: e.tensor_copy(qfm[:, 512:D], PS[3][:, 0:512]))
        v_qfm = V()
        msk_done = [0] * 4
        intra_tv = [0] * 4
        for h in range(H):
            sl = h % 4
            rb = 64 * (h % 2)
            cb = (h // 2) * P
            w_s = []
            if h == 0:
                w_s.append((p.sem["vector"], max(v_kfm, v_qfm)))
            if msk_done[sl]:
                w_s.append((p.sem["vector"], msk_done[sl]))
            tv_s = top(lambda e, sl=sl, rb=rb, cb=cb: e.matmul(
                PS[2][:, sl * P:(sl + 1) * P],
                kfm[rb:rb + 64, cb:cb + P], qfm[rb:rb + 64, cb:cb + P],
                start=True, stop=True), w=w_s)
            w_m = [(p.sem["tensor"], tv_s), (p.sem["vector"], v_vt)]
            if intra_tv[sl]:
                w_m = [(p.sem["tensor"], max(tv_s, intra_tv[sl]))]
            v_m = vop(lambda e, sl=sl: e.tensor_tensor(
                mskt[:, sl * P:(sl + 1) * P], PS[2][:, sl * P:(sl + 1) * P],
                cmask[:, :], ALU.mult), w=w_m)
            msk_done[sl] = v_m
            ob = PS[4] if h < 8 else PS[5]
            oc0 = 64 * (h % 8)
            w_o = ([(p.sem["vector"], v_state)] + (
                [(p.dma["sync"], d_dup)] if d_dup else [])) if h == 0 else []
            top(lambda e, ob=ob, oc0=oc0, rb=rb, cb=cb, h=h: e.matmul(
                ob[:, oc0:oc0 + 64], qfm[rb:rb + 64, cb:cb + P],
                stateT[rb:rb + 64, 64 * h:64 * h + 64],
                start=True, stop=False), w=w_o)
            intra_tv[sl] = top(
                lambda e, ob=ob, oc0=oc0, h=h, sl=sl: e.matmul(
                    ob[:, oc0:oc0 + 64], mskt[:, sl * P:(sl + 1) * P],
                    vt_t[:, 64 * h:64 * h + 64],
                    start=False, stop=True),
                w=[(p.sem["vector"], max(v_m, v_vt))])
            sb_ = PS[6] if h < 8 else PS[7]
            top(lambda e, sb_=sb_, oc0=oc0, h=h: e.matmul(
                sb_[0:64, oc0:oc0 + 64], ktm[:, 64 * h:64 * h + 64],
                vt_t[:, 64 * h:64 * h + 64], start=True, stop=True),
                w=[(p.dma["sync"], d_ktm), (p.sem["scalar"], sv_gates)]
                if h == 0 else ())
        tv_heads = TE()
        vop(lambda e: e.tensor_tensor(TMP[0][0:64, 0:512], PS[6][0:64, 0:512],
                                      stateT[0:64, 0:512], ALU.add),
            w=[(p.sem["tensor"], tv_heads)])
        vop(lambda e: e.tensor_tensor(TMP[0][0:64, 512:D], PS[7][0:64, 0:512],
                                      stateT[0:64, 512:D], ALU.add))
        vop(lambda e: e.tensor_tensor(stateT[0:64, 0:512], TMP[0][0:64, 0:512],
                                      PS[0][0:64, 0:512], ALU.mult))
        vop(lambda e: e.tensor_tensor(stateT[0:64, 512:D], TMP[0][0:64, 512:D],
                                      PS[1][0:64, 0:512], ALU.mult))
        v_state = V()
        d_dup = p.dma_op("sync", lambda e: [e.dma_start(
            out=stateT[64:128, :], in_=stateT[0:64, :])],
            waits=[(p.sem["vector"], v_state)])
        vop(lambda e: e.tensor_tensor(TMP[0][:, 0:512], PS[4][:, 0:512],
                                      A_c[:, 0:512], ALU.mult))
        vop(lambda e: e.tensor_tensor(TMP[0][:, 512:D], PS[5][:, 0:512],
                                      A_c[:, 512:D], ALU.mult))
        v_oevac_prev = V()
        dv_sg = load(sgch[:], scr_sg[tc:tc + P, :],
                     w=[(p.sem["tensor"], tv_heads)])
        vopc(lambda e: e.reduce_sum(COLS["c0"][:, 0:1], TMP[0][:], AX.X))
        sopc(lambda e: e.activation(TMP[1][:], TMP[0][:], AF.Square),
            w=[(p.sem["vector"], V())])
        vopc(lambda e: e.reduce_sum(COLS["c1"][:, 0:1], TMP[1][:], AX.X),
            w=[(p.sem["scalar"], S())])
        vopc(lambda e: e.tensor_scalar(COLS["c0"][:, 0:1], COLS["c0"][:, 0:1],
                                      1.0 / D, None, ALU.mult))
        vopc(lambda e: e.tensor_tensor(COLS["c2"][:, 0:1], COLS["c0"][:, 0:1],
                                      COLS["c0"][:, 0:1], ALU.mult))
        vopc(lambda e: e.scalar_tensor_tensor(COLS["c1"][:, 0:1], COLS["c1"][:, 0:1],
                                             1.0 / D, COLS["c2"][:, 0:1],
                                             ALU.mult, ALU.subtract))
        vopc(lambda e: e.tensor_scalar(COLS["c1"][:, 0:1], COLS["c1"][:, 0:1],
                                      1e-5, None, ALU.add))
        sopc(lambda e: e.activation(COLS["c2"][:, 0:1], COLS["c1"][:, 0:1], AF.Ln),
            w=[(p.sem["vector"], V())])
        sopc(lambda e: e.activation(COLS["c2"][:, 0:1], COLS["c2"][:, 0:1],
                                   AF.Exp, scale=-0.5))
        vopc(lambda e: e.tensor_tensor(COLS["c3"][:, 0:1], COLS["c0"][:, 0:1],
                                      COLS["c2"][:, 0:1], ALU.mult),
            w=[(p.sem["scalar"], S())])
        vopc(lambda e: e.tensor_scalar(COLS["c3"][:, 0:1], COLS["c3"][:, 0:1],
                                      -1.0, None, ALU.mult))
        sopc(lambda e: e.activation(TMP[1][:], TMP[0][:], AF.Identity,
                                   scale=COLS["c2"][:, 0:1],
                                   bias=COLS["c3"][:, 0:1]),
            w=[(p.sem["vector"], V())])
        vop(lambda e: e.tensor_tensor(TMP[2][:], TMP[1][:], gamma_bc[:], ALU.mult),
            w=[(p.sem["scalar"], S())])
        vop(lambda e: e.tensor_tensor(TMP[1][:], TMP[2][:], beta_bc[:], ALU.add))
        vop(lambda e: e.tensor_tensor(TMP[2][:], TMP[1][:], sgch[:], ALU.mult),
            w=[(p.dma["sync"], dv_sg)])
        vop(lambda e: e.tensor_copy(xnbf[:], TMP[2][:]),
            w=[(p.dma["sync"], DS())])
        v_spre = V()
        d_sfm = xbars([(qfm[:, oc * P:(oc + 1) * P], xnbf[:, oc * P:(oc + 1) * P])
                       for oc in range(8)], w=[(p.sem["vector"], v_spre)])
        for half in range(2):
            for kd in range(8):
                top(lambda e, half=half, kd=kd: e.matmul(
                    PS[2 + half][:, 0:512], qfm[:, kd * P:(kd + 1) * P],
                    WO[:, kd * D + half * 512: kd * D + (half + 1) * 512],
                    start=(kd == 0), stop=(kd == 7)),
                    w=[(p.dma["sync"], d_sfm), (p.dma["gpsimd"], d_wo)]
                    if (half == 0 and kd == 0) else ())
        tv = TE()
        vop(lambda e: e.tensor_copy(TMP[3][:, 0:512], PS[2][:, 0:512]),
            w=[(p.sem["tensor"], tv), (p.dma["sync"], DS())])
        vop(lambda e: e.tensor_copy(TMP[3][:, 512:D], PS[3][:, 0:512]))
        v_wout_prev = V()
        vv = V()
        store(scr_sout[tc:tc + P, :], TMP[3][:], w=[(p.sem["vector"], vv)])
    d_sout_all = DS()

    if stop_after == 'C':
        return _finish(nc, p, st, debug_taps, scr_sig, scr_g, scr_sg, scr_sout,
                       scr_gp, xq_fm, xn_fm, store, V, DS, DG, T, D)
    dq_rd = [0, 0]
    for half in range(2):
        for ic in range(8):
            I0 = half * 2048 + ic * 256
            for wi, (src_h, dstq, kc, lnc) in enumerate(
                    ((gatew_h, gwq, "k_g", "lngs_g"), (upw_h, uwq, "k_u", "lngs_u"))):
                dvs = []
                for sub in range(2):
                    buf = wstr[2 * wi + sub]
                    dvs.append(load(buf[:], src_h[I0 + sub * P: I0 + (sub + 1) * P, :],
                                    w=[(p.sem["vector"], V())]))
                for sub in range(2):
                    amax_to(wstr[2 * wi + sub][:], 64, 64 * sub,
                            ws=[(p.dma["sync"], dvs[sub])])
                scale_pipe(None, 128, kc, lnc, None, None,
                           wv=[(p.sem["vector"], V())])
                for sub in range(2):
                    qb = (TMP[1], TMPX)[sub]
                    gq = quant_hot(wstr[2 * wi + sub][:], qb[:], 64,
                                   SC8[:, 64 * sub:64 * sub + 64],
                                   G8[:, 64 * sub:64 * sub + 64],
                                   ws=[(p.sem["vector"], V())], clip=False,
                                   out_ws=([(p.sem["tensor"], dq_rd[sub])]
                                           if dq_rd[sub] else []))
                    dq_rd[sub], _ = pe_xpose(
                        [(dstq[:, kd * 256 + sub * P: kd * 256 + (sub + 1) * P],
                          qb[:, kd * P:(kd + 1) * P]) for kd in range(8)],
                        PS[2], w=[(p.sem["gpsimd"], gq)])
            s_wslice = S()
            ev_done = [0, 0]
            for tt in range(NT):
                par = tt % 2
                pg, pu = (PS[0], PS[1]) if par == 0 else (PS[4], PS[5])
                w0 = [(p.sem["scalar"], s_wslice)]
                if ev_done[par]:
                    w0.append((p.sem["vector"], ev_done[par]))
                elif tt == 0:
                    w0.append((p.sem["vector"], V()))
                for kd in range(8):
                    top(lambda e, kd=kd, tt=tt, pg=pg: e.matmul(
                        pg[:, 0:256], xq_fm[:, kd * T + tt * P:
                                            kd * T + (tt + 1) * P],
                        gwq[:, kd * 256:(kd + 1) * 256],
                        start=(kd == 0), stop=(kd == 7)),
                        w=w0 if kd == 0 else ())
                for kd in range(8):
                    top(lambda e, kd=kd, tt=tt, pu=pu: e.matmul(
                        pu[:, 0:256], xq_fm[:, kd * T + tt * P:
                                            kd * T + (tt + 1) * P],
                        uwq[:, kd * 256:(kd + 1) * 256],
                        start=(kd == 0), stop=(kd == 7)))
                tv = TE()
                t0s, t1s = (TMP[0], TMP[1]) if par == 0 else (TMP[2], TMP[3])
                sop(lambda e, pg=pg, t0s=t0s: e.activation(
                    t0s[:, 0:256], pg[:, 0:256],
                    AF.Sigmoid, scale=COLS["a_gate"][:, 0:1]),
                    w=[(p.sem["tensor"], tv)])
                sv = S()
                vop(lambda e, pg=pg, t0s=t0s, t1s=t1s: e.tensor_tensor(
                    t1s[:, 0:256], pg[:, 0:256], t0s[:, 0:256], ALU.mult),
                    w=[(p.sem["scalar"], sv),
                       (p.sem["tensor"], max(dq_rd[0], dq_rd[1]))])
                vop(lambda e, tt=tt, pu=pu, t1s=t1s: e.scalar_tensor_tensor(
                    hid_c[:, tt * 256:(tt + 1) * 256], pu[:, 0:256],
                    COLS["a_gu"][:, 0:1], t1s[:, 0:256], ALU.mult, ALU.mult))
                ev_done[par] = V()
            for slab in range(4):
                amax_to(hid_c[:, slab * D:(slab + 1) * D], 64, 64 * slab,
                        ws=[(p.sem["vector"], V())] if slab == 0 else ())
            scale_pipe(None, 256, "k_h", "lngs_h", None, None,
                       wv=[(p.sem["vector"], V())])
            for slab in range(4):
                qb = (TMP[1], TMPX)[slab % 2]
                src = hid_c[:, slab * D:(slab + 1) * D]
                gq = quant_hot(src, qb[:], 64,
                               SC8[:, 64 * slab:64 * slab + 64],
                               G8[:, 64 * slab:64 * slab + 64],
                               ws=[(p.sem["vector"], V())], clip=True,
                               out_ws=([(p.sem["tensor"], dq_rd[slab % 2])]
                                       if dq_rd[slab % 2] else []))
                prs = []
                for j in range(4):
                    tt = slab * 4 + j
                    for ib in range(2):
                        ki = ic * 2 + ib
                        prs.append((hq_fm[:, ki * T + tt * P: ki * T + (tt + 1) * P],
                                    qb[:, j * 256 + ib * P:
                                        j * 256 + (ib + 1) * P]))
                dq_rd[slab % 2], _ = pe_xpose(prs, PS[2],
                                              w=[(p.sem["gpsimd"], gq)])
        s_hq = S()
        for i in range(8):
            dvs = []
            for sub in range(2):
                dvs.append(load(wstr[sub][:], downw_h[i * P:(i + 1) * P,
                                                      half * 2048 + sub * D:
                                                      half * 2048 + (sub + 1) * D],
                                w=[(p.sem["vector"], V())]))
            for sub in range(2):
                amax_to(wstr[sub][:], 64, 64 * sub, ws=[(p.dma["sync"], dvs[sub])])
            scale_pipe(None, 128, "k_d", "lngs_d", None, None,
                       wv=[(p.sem["vector"], V())])
            for sub in range(2):
                qb = (TMP[1], TMPX)[sub]
                gq = quant_hot(wstr[sub][:], qb[:], 64,
                               SC8[:, 64 * sub:64 * sub + 64],
                               G8[:, 64 * sub:64 * sub + 64],
                               ws=[(p.sem["vector"], V())], clip=False,
                               out_ws=([(p.sem["tensor"], dq_rd[sub])]
                                       if dq_rd[sub] else []))
                dq_rd[sub], _ = pe_xpose(
                    [(dwq[:, (sub * 8 + kk) * D + i * P:
                          (sub * 8 + kk) * D + (i + 1) * P],
                      qb[:, kk * P:(kk + 1) * P]) for kk in range(8)],
                    PS[2], w=[(p.sem["gpsimd"], gq)])
        s_dwq = S()
        ev2 = [0, 0]
        for tt in range(NT):
            par = tt % 2
            pa, pb = (PS[2], PS[3]) if par == 0 else (PS[6], PS[7])
            w0 = [(p.sem["scalar"], max(s_dwq, s_hq))]
            w0.append((p.sem["vector"], ev2[par] if ev2[par] else V()))
            for ki in range(16):
                for half2 in range(2):
                    top(lambda e, ki=ki, half2=half2, tt=tt, pa=pa, pb=pb: e.matmul(
                        (pa, pb)[half2][:, 0:512],
                        hq_fm[:, ki * T + tt * P: ki * T + (tt + 1) * P],
                        dwq[:, ki * D + half2 * 512: ki * D + (half2 + 1) * 512],
                        start=(ki == 0), stop=(ki == 15)),
                        w=w0 if (ki == 0 and half2 == 0) else ())
            tv = TE()
            t0s = TMP[0] if par == 0 else TMP[2]
            t1s = TMP[1] if par == 0 else TMP[3]
            if half == 0:
                vop(lambda e, pa=pa, t0s=t0s: e.tensor_copy(
                    t0s[:, 0:512], pa[:, 0:512]),
                    w=[(p.sem["tensor"], tv), (p.dma["sync"], DS())])
                vop(lambda e, pb=pb, t0s=t0s: e.tensor_copy(
                    t0s[:, 512:D], pb[:, 0:512]))
                vv = V()
                store(scr_gp[tt * P:(tt + 1) * P, :], t0s[:],
                      w=[(p.sem["vector"], vv)])
            else:
                dv = load(t1s[:], scr_gp[tt * P:(tt + 1) * P, :],
                          w=[(p.sem["vector"], ev2[par] if ev2[par] else V()),
                             (p.sem["tensor"], max(dq_rd[0], dq_rd[1]))])
                vop(lambda e, pa=pa, t0s=t0s, t1s=t1s: e.tensor_tensor(
                    t0s[:, 0:512], pa[:, 0:512], t1s[:, 0:512], ALU.add),
                    w=[(p.sem["tensor"], tv), (p.dma["sync"], max(dv, DS()))])
                vop(lambda e, pb=pb, t0s=t0s, t1s=t1s: e.tensor_tensor(
                    t0s[:, 512:D], pb[:, 0:512], t1s[:, 512:D], ALU.add))
                vv = V()
                store(scr_gp[tt * P:(tt + 1) * P, :], t0s[:],
                      w=[(p.sem["vector"], vv)])
            ev2[par] = V()
    d_gf = DS()

    for tt in range(NT):
        xb = wstr[0]
        dv1 = load(xb[:], x_h[tt * P:(tt + 1) * P, :],
                   w=[(p.sem["vector"], V()), (p.dma["sync"], d_gf)])
        dv2 = load(TMP[3][:], scr_gp[tt * P:(tt + 1) * P, :])
        dv3 = p.dma_op("gpsimd", lambda e, tt=tt: [e.dma_start(
            out=TMP[4][:], in_=scr_sout[tt * P:(tt + 1) * P, :])],
            waits=[(p.sem["vector"], V())])
        sop(lambda e: e.activation(TMP[0][:], TMP[3][:], AF.Sigmoid,
                                   scale=COLS["a_d"][:, 0:1]),
            w=[(p.dma["sync"], dv2)])
        sv = S()
        vop(lambda e: e.tensor_tensor(TMP[1][:], TMP[3][:], TMP[0][:], ALU.mult),
            w=[(p.sem["scalar"], sv)])
        vop(lambda e: e.scalar_tensor_tensor(TMP[2][:], TMP[1][:],
                                             COLS["a_d"][:, 0:1], TMP[4][:],
                                             ALU.mult, ALU.mult),
            w=[(p.dma["gpsimd"], dv3)])
        vop(lambda e, xb=xb: e.tensor_tensor(TMP[1][:], TMP[2][:], xb[:], ALU.add),
            w=[(p.dma["sync"], dv1)])
        vv = V()
        store(y_h[tt * P:(tt + 1) * P, :], TMP[1][:], w=[(p.sem["vector"], vv)])
    return _finish(nc, p, st, debug_taps, scr_sig, scr_g, scr_sg, scr_sout,
                   scr_gp, xq_fm, xn_fm, store, V, DS, DG, T, D)


def _finish(nc, p, st, debug_taps, scr_sig, scr_g, scr_sg, scr_sout, scr_gp,
            xq_fm, xn_fm, store, V, DS, DG, T, D):
    if debug_taps:
        for nm, scr in (("dbg_sig", scr_sig), ("dbg_g", scr_g), ("dbg_sg", scr_sg),
                        ("dbg_sout", scr_sout), ("dbg_gp", scr_gp)):
            dst = nc.dram_tensor(nm, [T, D], F32, kind="ExternalOutput").ap()
            store(dst, scr, w=[(p.sem["vector"], V())])
        dbx = nc.dram_tensor("dbg_xqfm", [P, 8 * T], BF16, kind="ExternalOutput").ap()
        store(dbx, xq_fm[:], w=[(p.sem["vector"], V())])
        dbn = nc.dram_tensor("dbg_xnfm", [P, 8 * T], BF16, kind="ExternalOutput").ap()
        store(dbn, xn_fm[:], w=[(p.sem["vector"], V())])
    d_final = DS()
    p.op("sync", lambda e: e.nop(), waits=[(p.dma["sync"], d_final),
                                           (p.dma["gpsimd"], DG()),
                                           (p.dma["scalar"], p.dma["scalar"].v)],
         inc=False)
    with nc.Block() as block:
        p.emit(block)
    st.close()
    return nc


_NC_CACHE = None


def _get_nc():
    global _NC_CACHE
    if _NC_CACHE is None:
        _NC_CACHE = build_program()
    return _NC_CACHE


def kernel(L0_attn_x, L0_attn_Wq, L0_attn_Wk, L0_attn_Wv, L0_attn_Wg,
           L0_attn_Wgk1, L0_attn_Wgk2, L0_attn_bgk2, L0_attn_gamma,
           L0_attn_beta, L0_attn_Wout, L0_ffn_gate_w, L0_ffn_up_w,
           L0_ffn_down_w, L0_ffn_s_in, L0_ffn_s_in_down):
    f32 = np.float32
    x = np.ascontiguousarray(L0_attn_x, dtype=f32)
    B = x.shape[0]
    w2b = np.ascontiguousarray(
        np.concatenate([np.asarray(L0_attn_Wgk2, f32),
                        np.asarray(L0_attn_bgk2, f32)[None, :]], axis=0))
    gamma_bc = np.ascontiguousarray(
        np.broadcast_to(np.asarray(L0_attn_gamma, f32)[None, :], (P, D)))
    beta_bc = np.ascontiguousarray(
        np.broadcast_to(np.asarray(L0_attn_beta, f32)[None, :], (P, D)))
    sx_col = np.full((P, 1), np.asarray(L0_ffn_s_in, f32)[0], f32)
    sd_col = np.full((P, 1), np.asarray(L0_ffn_s_in_down, f32)[0], f32)
    sidx = np.arange(P)
    tri16 = ((sidx[:, None] <= sidx[None, :]).astype(f32) / f32(16.0))
    cmask = (sidx[:, None] <= sidx[None, :]).astype(f32)
    oneh128 = np.zeros((P, P), f32); oneh128[P - 1, :] = 1.0
    onecol = np.ones((1, P), f32)
    identbf = np.eye(P, dtype=f32)

    shared = dict(
        wq=np.ascontiguousarray(L0_attn_Wq, f32),
        wk=np.ascontiguousarray(L0_attn_Wk, f32),
        wv=np.ascontiguousarray(L0_attn_Wv, f32),
        wg=np.ascontiguousarray(L0_attn_Wg, f32),
        wout=np.ascontiguousarray(L0_attn_Wout, f32),
        wgk1=np.ascontiguousarray(L0_attn_Wgk1, f32),
        w2b=w2b, gamma_bc=gamma_bc, beta_bc=beta_bc,
        sx_col=sx_col, sd_col=sd_col,
        gatew=np.ascontiguousarray(L0_ffn_gate_w, f32),
        upw=np.ascontiguousarray(L0_ffn_up_w, f32),
        downw=np.ascontiguousarray(L0_ffn_down_w, f32),
        tri16=tri16, cmask=cmask, oneh128=oneh128, onecol=onecol,
        identbf=identbf,
    )
    in_maps = [dict(shared, x=np.ascontiguousarray(x[b])) for b in range(B)]
    nc = _get_nc()
    res = run_bass_kernel_spmd(nc, in_maps, core_ids=list(range(B)))
    out = np.stack([np.asarray(res.results[b]["y"]) for b in range(B)], axis=0)
    return out.astype(np.float32)


# version 09
# speedup vs baseline: 1.0949x; 1.0784x over previous
import numpy as np
import concourse.bass as bass
import concourse.mybir as mybir
from concourse.bass_utils import run_bass_kernel_spmd
from contextlib import ExitStack

F32 = mybir.dt.float32
BF16 = mybir.dt.bfloat16
I32 = mybir.dt.int32
AF = mybir.ActivationFunctionType
ALU = mybir.AluOpType
AX = mybir.AxisListType

P = 128
T = 2048
D = 1024
NT = T // P
H = 16
DH = 64
IDIM = 4096
LN2 = 0.6931471805599453
INV_LN2 = 1.4426950408889634
KB = 1024

ENGS = ("sync", "vector", "scalar", "tensor", "gpsimd")


class Sem:
    def __init__(self, nc, st, name):
        self.h = st.enter_context(nc.semaphore(name))
        self.v = 0


class Prog:
    def __init__(self, nc, st, paranoid=False):
        self.nc = nc
        self.ops = []
        self.paranoid = paranoid
        self.sem = {e: Sem(nc, st, "sem_" + e) for e in ENGS}
        self.dma = {q: Sem(nc, st, "dma_" + q) for q in ("sync", "gpsimd", "scalar")}

    def _par_waits(self, eng):
        w = [(s, s.v) for e, s in self.sem.items() if e != eng]
        w += [(s, s.v) for s in self.dma.values()]
        return w

    def op(self, eng, fn, waits=(), inc=True):
        if self.paranoid:
            waits = list(waits) + self._par_waits(eng)
        incs = []
        if inc or self.paranoid:
            s = self.sem[eng]
            s.v += 1
            incs.append((s, 1))
        self.ops.append((eng, tuple(waits), tuple(incs), fn))
        return self.sem[eng].v

    def dma_op(self, q, fn, waits=(), n=1):
        if self.paranoid:
            waits = list(waits) + self._par_waits(None)
        s = self.dma[q]

        def wrapped(eng, fn=fn, s=s):
            insts = fn(eng)
            for inst in insts:
                inst.then_inc(s.h, 16)
            return None

        s.v += 16 * n
        self.ops.append((q, tuple(waits), (), wrapped))
        return s.v

    def emit(self, block):
        for ename in ENGS:
            ops = [o for o in self.ops if o[0] == ename]

            def mk(ops=ops):
                def f(eng):
                    for (_, waits, incs, fn) in ops:
                        best = {}
                        for (s, val) in waits:
                            if val > best.get(id(s), (None, -1))[1]:
                                best[id(s)] = (s, val)
                        for (s, val) in best.values():
                            eng.wait_ge(s.h, val)
                        inst = fn(eng)
                        for (s, n) in incs:
                            inst.then_inc(s.h, n)
                return f
            getattr(block, ename)(mk())


class Union:
    def __init__(self, nc, st, name, bytes_per_partition):
        self.nc = nc
        self.backing = st.enter_context(
            nc.sbuf_tensor(name, [P, bytes_per_partition], mybir.dt.uint8))
        self.base = nc.lookup_mloc(self.backing).addr
        self.size = bytes_per_partition

    def at(self, off, name, shape, dtype):
        per_part = int(np.prod(shape[1:])) * mybir.dt.size(dtype)
        assert off % 32 == 0 and off + per_part <= self.size, (name, off, per_part)
        return self.nc.alloc_sbuf_tensor_at(name, shape, dtype,
                                            offset=self.base + off)


def build_program(paranoid=False, debug_taps=False, stop_after='E'):
    nc = bass.Bass("TRN2", target_bir_lowering=False, debug=False,
                   detect_race_conditions=False)
    st = ExitStack()
    E = st.enter_context

    def din(name, shape, dt=F32):
        return nc.dram_tensor(name, shape, dt, kind="ExternalInput").ap()

    x_h = din("x", [T, D])
    wq_h = din("wq", [D, D]); wk_h = din("wk", [D, D]); wv_h = din("wv", [D, D])
    wg_h = din("wg", [D, D]); wout_h = din("wout", [D, D])
    wgk1_h = din("wgk1", [D, 16])
    w2b_h = din("w2b", [17, D])
    gatew_h = din("gatew", [IDIM, D]); upw_h = din("upw", [IDIM, D])
    downw_h = din("downw", [D, IDIM])
    gamma_h = din("gamma_bc", [P, D]); beta_h = din("beta_bc", [P, D])
    sx_h = din("sx_col", [P, 1]); sd_h = din("sd_col", [P, 1])
    tri16_h = din("tri16", [P, P]); cmask_h = din("cmask", [P, P])
    oneh128_h = din("oneh128", [P, P]); onecol_h = din("onecol", [1, P])
    identbf_h = din("identbf", [P, P])

    y_h = nc.dram_tensor("y", [T, D], F32, kind="ExternalOutput").ap()
    scr_sig = nc.dram_tensor("scr_sig", [T, D], F32, kind="Internal").ap()
    scr_g = nc.dram_tensor("scr_g", [T, D], F32, kind="Internal").ap()
    scr_sg = nc.dram_tensor("scr_sg", [T, D], F32, kind="Internal").ap()
    scr_sout = nc.dram_tensor("scr_sout", [T, D], F32, kind="Internal").ap()
    scr_gp = nc.dram_tensor("scr_gp", [T, D], F32, kind="Internal").ap()

    TMP = [E(nc.sbuf_tensor(f"tmp{i}", [P, D], F32)) for i in range(5)]
    TMPB = [E(nc.sbuf_tensor(f"tmpb{i}", [P, D], BF16)) for i in range(2)]
    TMPX = E(nc.sbuf_tensor("tmpx", [P, D], F32))
    xq_fm = E(nc.sbuf_tensor("xq_fm", [P, 8 * T], BF16))
    tri16 = E(nc.sbuf_tensor("tri16_sb", [P, P], F32))
    cmask = E(nc.sbuf_tensor("cmask_sb", [P, P], F32))
    oneh128 = E(nc.sbuf_tensor("oneh128_sb", [P, P], F32))
    identbf = E(nc.sbuf_tensor("identbf_sb", [P, P], F32))
    onecol = E(nc.sbuf_tensor("onecol_sb", [1, P], F32))
    gamma_bc = E(nc.sbuf_tensor("gamma_bc_sb", [P, D], F32))
    beta_bc = E(nc.sbuf_tensor("beta_bc_sb", [P, D], F32))
    wgk1_bf = E(nc.sbuf_tensor("wgk1_bf", [P, 8 * 16], BF16))
    w2b_bf = E(nc.sbuf_tensor("w2b_bf", [17, D], BF16))
    rbf = E(nc.sbuf_tensor("rbf", [32, T], BF16))
    stateT = E(nc.sbuf_tensor("stateT", [P, H * DH], BF16))
    xnbf = E(nc.sbuf_tensor("xnbf", [P, D], BF16))

    onerow = E(nc.sbuf_tensor("onerow", [1, P + 4], F32))
    SC8 = E(nc.sbuf_tensor("sc8t", [P, 256], F32))
    AMX = E(nc.sbuf_tensor("amxt", [P, 256], F32))
    G8 = E(nc.sbuf_tensor("g8t", [P, 256], F32))
    SCT = [E(nc.sbuf_tensor(f"sct{i}", [P, 256], F32)) for i in range(4)]
    SCI = E(nc.sbuf_tensor("sci", [P, D], I32))
    TI = SCI
    COLS = {}
    for nm in ["sx", "sd", "k_x", "k_h", "lngs_x", "lngs_h",
               "gm_g", "gm_u", "gm_d", "gs_g", "gs_u", "gs_d",
               "k_g", "k_u", "k_d", "lngs_g", "lngs_u", "lngs_d",
               "a_gate", "a_gu", "a_d", "c0", "c1", "c2", "c3", "c4", "m16"]:
        COLS[nm] = E(nc.sbuf_tensor("col_" + nm, [P, 1], F32))

    U = Union(nc, st, "union", 114 * KB)
    xn_fm = U.at(0, "xn_fm", [P, 8 * T], BF16)
    WQ = U.at(32 * KB, "WQ", [P, 8 * D], BF16)
    WK = U.at(48 * KB, "WK", [P, 8 * D], BF16)
    WV = U.at(64 * KB, "WV", [P, 8 * D], BF16)
    WO = U.at(80 * KB, "WO", [P, 8 * D], BF16)
    WG = U.at(80 * KB, "WG", [P, 8 * D], BF16)
    qfm = U.at(96 * KB, "qfm", [P, 8 * P], BF16)
    kfm = U.at(98 * KB, "kfm", [P, 8 * P], BF16)
    ktm = U.at(100 * KB, "ktm", [P, D], BF16)
    vtm = U.at(102 * KB, "vtm", [P, D], BF16)
    vt_t = U.at(104 * KB, "vt_t", [P, D], BF16)
    A_c = U.at(106 * KB, "A_c", [P, D], F32)
    gch = U.at(110 * KB, "gch", [P, D], F32)
    Ainv = U.at(110 * KB, "Ainv", [P, D], F32)
    sgch = U.at(100 * KB, "sgch", [P, D], F32)
    mskt = U.at(113 * KB, "mskt", [P, 4 * P], BF16)
    hq_fm = U.at(0, "hq_fm", [P, 16 * T], BF16)
    dwq = U.at(64 * KB, "dwq", [P, 16 * D], BF16)
    hid_c = U.at(64 * KB, "hid_c", [P, NT * 256], F32)
    gwq = U.at(80 * KB, "gwq", [P, 8 * 256], BF16)
    uwq = U.at(84 * KB, "uwq", [P, 8 * 256], BF16)
    wstr = [U.at((96 + 4 * i) * KB, f"wstr{i}", [P, D], F32) for i in range(4)]

    PS = [E(nc.psum_tensor(f"ps{i}", [P, 512], F32)) for i in range(8)]

    p = Prog(nc, st, paranoid=paranoid)
    V = lambda: p.sem["vector"].v
    S = lambda: p.sem["scalar"].v
    TE = lambda: p.sem["tensor"].v
    DS = lambda: p.dma["sync"].v
    DG = lambda: p.dma["gpsimd"].v

    def vop(fn, w=()):
        return p.op("vector", fn, waits=w)

    def sop(fn, w=()):
        return p.op("scalar", fn, waits=w)

    def vopc(fn, w=()):
        return p.op("vector", fn, waits=list(w) + [(p.sem["vector"], V())])

    def sopc(fn, w=()):
        return p.op("scalar", fn, waits=list(w) + [(p.sem["scalar"], S())])

    def top(fn, w=()):
        return p.op("tensor", fn, waits=w)

    def load(dst, src, w=(), q="sync"):
        return p.dma_op(q, lambda e: [e.dma_start(out=dst, in_=src)], waits=w)

    def store(dst, src, w=(), q="sync"):
        return p.dma_op(q, lambda e: [e.dma_start(out=dst, in_=src)], waits=w)

    def loads(pairs, w=(), q="sync"):
        return p.dma_op(q, lambda e: [e.dma_start(out=d, in_=s) for d, s in pairs],
                        waits=w, n=len(pairs))

    pexp_state = {"ev": 0}

    def pe_xpose(pairs, ps, w=()):
        first = True
        for g0 in range(0, len(pairs), 4):
            grp = pairs[g0:g0 + 4]
            for j, (dst, srcp) in enumerate(grp):
                ww = []
                if first:
                    ww = list(w)
                    first = False
                if pexp_state["ev"]:
                    ww.append((p.sem["scalar"], pexp_state["ev"]))
                    pexp_state["ev"] = 0
                top(lambda e, j=j, srcp=srcp, ps=ps: e.transpose(
                    ps[:, j * P:(j + 1) * P], srcp, identbf[:]), w=ww)
            tv = TE()
            for j, (dst, srcp) in enumerate(grp):
                sop(lambda e, j=j, dst=dst, ps=ps: e.copy(dst, ps[:, j * P:(j + 1) * P]),
                    w=[(p.sem["tensor"], tv)] if j == 0 else ())
            pexp_state["ev"] = S()
        return TE(), S()

    def xbars(pairs, w=(), q="sync"):
        return p.dma_op(q, lambda e: [e.dma_start_transpose(d, s)
                                      for d, s in pairs], waits=w, n=len(pairs))

    def scale_pipe(amax_ap, nb, kcol, lncol, sc8_ap, g8_ap, wv, off=0, W=256):
        lo = min(off, 256 - 256) if False else 0
        s0, s1, s2, s3 = (t[:, 0:256] for t in SCT)
        si = SCI[:, 0:256]
        amx = AMX[:, 0:256]
        vop(lambda e: e.tensor_scalar(s0, amx, COLS[kcol][:, 0:1], 448.0,
                                      ALU.mult, ALU.min), w=wv)
        vop(lambda e: e.tensor_scalar(s0, s0, 2.0 ** -20, None, ALU.max))
        sop(lambda e: e.activation(s1, s0, AF.Ln), w=[(p.sem["vector"], V())])
        vop(lambda e: e.tensor_scalar(s2, s1, INV_LN2, None, ALU.mult),
            w=[(p.sem["scalar"], S())])
        vop(lambda e: e.tensor_scalar(si, s2, 32.0, None, ALU.add))
        vop(lambda e: e.tensor_scalar(s3, si, -32.0, None, ALU.add))
        vop(lambda e: e.tensor_tensor(s1, s3, s2, ALU.is_gt))
        vop(lambda e: e.tensor_tensor(s3, s3, s1, ALU.subtract))
        sop(lambda e: e.activation(s2, s3, AF.Exp, scale=-LN2),
            w=[(p.sem["vector"], V())])
        vop(lambda e: e.tensor_tensor(s2, s0, s2, ALU.mult),
            w=[(p.sem["scalar"], S())])
        vop(lambda e: e.tensor_scalar(s0, s2, 8.0, 0.5, ALU.mult, ALU.add))
        vop(lambda e: e.tensor_copy(si, s0))
        vop(lambda e: e.tensor_copy(s2, si))
        vop(lambda e: e.tensor_tensor(s1, s2, s0, ALU.is_gt))
        vop(lambda e: e.tensor_tensor(s2, s2, s1, ALU.subtract))
        sop(lambda e: e.activation(s1, s3, AF.Exp, scale=LN2),
            w=[(p.sem["vector"], V())])
        vop(lambda e: e.scalar_tensor_tensor(
            SC8[:, off:off + W], s2[:, off:off + W], 0.125, s1[:, off:off + W],
            ALU.mult, ALU.mult), w=[(p.sem["scalar"], S())])
        sop(lambda e: e.activation(s0[:, off:off + W], SC8[:, off:off + W], AF.Ln),
            w=[(p.sem["vector"], V())])
        sop(lambda e: e.activation(G8[:, off:off + W], s0[:, off:off + W],
                                   AF.Exp, scale=-1.0,
                                   bias=COLS[lncol][:, 0:1]))
        return S()

    GQ = {"prev": 0}

    def quant_hot(src_ap, out_ap, nb, sc8_ap, g8_ap, ws, clip, out_ws=None):
        if out_ws is None:
            out_ws = [(p.dma["sync"], DS()),
                      (p.dma["scalar"], p.dma["scalar"].v)]
        F = 16 * nb
        t0, t1, t2, t3 = (t[:, 0:F] for t in TMP[0:4])
        b0, b1 = (t[:, 0:F] for t in TMPB)
        b2 = TMP[4][:, 0:F]
        ti = TI[:, 0:F]
        src3 = src_ap.rearrange("p (b k) -> p b k", k=16)
        g83 = g8_ap.broadcast_to([P, nb, 16])
        sc83 = sc8_ap.broadcast_to([P, nb, 16])
        vop(lambda e: e.tensor_tensor(t0.rearrange("p (b k) -> p b k", k=16),
                                      src3, g83, ALU.mult), w=list(ws))
        if clip:
            vop(lambda e: e.tensor_scalar(t0, t0, 6.0, -6.0, ALU.min, ALU.max))
        vop(lambda e: e.tensor_tensor(b0, t0, t0, ALU.mult))
        vop(lambda e: e.tensor_scalar(b1, b0, 4.0, 0.5, ALU.is_ge, ALU.mult))
        vop(lambda e: e.scalar_tensor_tensor(b2, b0, 16.0, b1, ALU.is_ge, ALU.add))
        vop(lambda e: e.tensor_scalar(b0, b2, -0.5, 2.0, ALU.mult, ALU.add))
        vop(lambda e: e.scalar_tensor_tensor(b0, b1, -1.5, b0, ALU.mult, ALU.add))
        vop(lambda e: e.tensor_tensor(t1, t0, b0, ALU.mult))
        vop(lambda e: e.tensor_scalar(ti, t1, 16.5, None, ALU.add))
        vop(lambda e: e.tensor_scalar(t2, ti, -16.0, None, ALU.add))
        vop(lambda e: e.scalar_tensor_tensor(t3, t1, 0.5, t2, ALU.add, ALU.is_lt),
            w=[(p.sem["gpsimd"], GQ["prev"])] if GQ["prev"] else ())
        vop(lambda e: e.tensor_tensor(t2, t2, t3, ALU.subtract))
        vop(lambda e: e.scalar_tensor_tensor(t3, b2, 0.5, t2, ALU.add, ALU.mult))
        vv = V()
        GQ["prev"] = p.op("gpsimd", lambda e: e.tensor_tensor(
            out_ap.rearrange("p (b k) -> p b k", k=16),
            t3.rearrange("p (b k) -> p b k", k=16),
            sc83, ALU.mult),
            waits=[(p.sem["vector"], vv)] + list(out_ws))
        return GQ["prev"]


    def amax_to(src_ap, nb, off, ws=()):
        return vop(lambda e: e.tensor_reduce(
            AMX[:, off:off + nb], src_ap.rearrange("p (b k) -> p b k", k=16),
            AX.X, ALU.max, apply_absolute_value=True), w=ws)

    def quantize_tile(src_ap, out_ap, nb, kcol, lncol, ws, clip):
        amax_to(src_ap, nb, 0, ws=ws)
        scale_pipe(None, nb, kcol, lncol, None, None,
                   wv=[(p.sem["vector"], V())])
        return quant_hot(src_ap, out_ap, nb, SC8[:, 0:nb], G8[:, 0:nb],
                         ws=[(p.sem["vector"], V())], clip=clip)

    loads([(tri16[:], tri16_h), (cmask[:], cmask_h), (oneh128[:], oneh128_h),
           (onecol[:], onecol_h), (gamma_bc[:], gamma_h), (beta_bc[:], beta_h),
           (COLS["sx"][:], sx_h), (COLS["sd"][:], sd_h), (identbf[:], identbf_h)])
    d_consts = DS()
    for W_, src in ((WQ, wq_h), (WK, wk_h), (WV, wv_h)):
        for kd in range(8):
            load(W_[:, kd * D:(kd + 1) * D], src[kd * P:(kd + 1) * P, :], q="gpsimd")
    for kd in range(8):
        load(wgk1_bf[:, kd * 16:(kd + 1) * 16], wgk1_h[kd * P:(kd + 1) * P, :],
             q="gpsimd")
    load(w2b_bf[:], w2b_h, q="gpsimd")
    d_attnw = DG()

    vop(lambda e: e.memset(AMX[:, :], 1.0))
    vop(lambda e: e.memset(COLS["m16"][:, 0:1], -16.0))
    vopc(lambda e: e.tensor_scalar(COLS["k_x"][:, 0:1], COLS["sx"][:, 0:1],
                                   1.0 / 6.0, None, ALU.mult),
         w=[(p.dma["sync"], d_consts)])
    vopc(lambda e: e.tensor_scalar(COLS["k_h"][:, 0:1], COLS["sd"][:, 0:1],
                                   1.0 / 6.0, None, ALU.mult))
    sop(lambda e: e.activation(COLS["lngs_x"][:, 0:1], COLS["sx"][:, 0:1], AF.Ln),
        w=[(p.sem["vector"], V())])
    sop(lambda e: e.activation(COLS["lngs_h"][:, 0:1], COLS["sd"][:, 0:1], AF.Ln))

    def gmax_stream(tiles, gm):
        vop(lambda e: e.memset(COLS[gm][:, 0:1], 0.0))
        for i, src in enumerate(tiles):
            buf = wstr[i % 2]
            w_ = src.shape[1]
            dv = load(buf[:, 0:w_], src, w=[(p.sem["vector"], V())])
            vop(lambda e, b=buf, w2=w_: e.tensor_reduce(
                COLS["c4"][:, 0:1], b[:, 0:w2], AX.X, ALU.max,
                apply_absolute_value=True), w=[(p.dma["sync"], dv)])
            vopc(lambda e, g=gm: e.tensor_tensor(COLS[g][:, 0:1], COLS[g][:, 0:1],
                                                 COLS["c4"][:, 0:1], ALU.max))
        vopc(lambda e, g=gm: e.tensor_copy(
            TMP[1][:, 0:32].rearrange("p (one k) -> p one k", k=32),
            COLS[g][:, 0:1].broadcast_to([P, 1, 32])))
        for j in range(4):
            vopc(lambda e, j=j: e.transpose(TMP[0][0:32, 32 * j:32 * (j + 1)],
                                            TMP[1][32 * j:32 * (j + 1), 0:32]))
        vopc(lambda e: e.tensor_reduce(onerow[0:1, P:P + 1], TMP[0][0:1, 0:P],
                                      AX.X, ALU.max))
        vopc(lambda e: e.reciprocal(onerow[0:1, P + 1:P + 2], onerow[0:1, P:P + 1]))
        vopc(lambda e: e.tensor_scalar(onerow[0:1, P + 2:P + 3],
                                      onerow[0:1, P + 1:P + 2], 2688.0, None,
                                      ALU.mult))

    def gs_bcast(gs, kc, lnc):
        tv = top(lambda e: e.matmul(PS[0][:, 0:1], onecol[0:1, 0:P],
                                    onerow[0:1, P + 2:P + 3], start=True, stop=True),
                 w=[(p.sem["vector"], V())])
        vopc(lambda e: e.tensor_copy(COLS[gs][:, 0:1], PS[0][:, 0:1]),
             w=[(p.sem["tensor"], tv)])
        vopc(lambda e: e.tensor_scalar(COLS[kc][:, 0:1], COLS[gs][:, 0:1],
                                       1.0 / 6.0, None, ALU.mult))
        sopc(lambda e: e.activation(COLS[lnc][:, 0:1], COLS[gs][:, 0:1], AF.Ln),
             w=[(p.sem["vector"], V())])

    gmax_stream([gatew_h[i * P:(i + 1) * P, :] for i in range(32)], "gm_g")
    gs_bcast("gs_g", "k_g", "lngs_g")
    gmax_stream([upw_h[i * P:(i + 1) * P, :] for i in range(32)], "gm_u")
    gs_bcast("gs_u", "k_u", "lngs_u")
    gmax_stream([downw_h[i * P:(i + 1) * P, j * D:(j + 1) * D]
                 for i in range(8) for j in range(4)], "gm_d")
    gs_bcast("gs_d", "k_d", "lngs_d")

    def recip_into(dst, a, b):
        vopc(lambda e: e.tensor_tensor(COLS["c4"][:, 0:1], COLS[a][:, 0:1],
                                       COLS[b][:, 0:1], ALU.mult))
        vopc(lambda e: e.reciprocal(COLS[dst][:, 0:1], COLS["c4"][:, 0:1]))

    recip_into("a_gate", "sx", "gs_g")
    recip_into("c0", "sx", "gs_u")
    vopc(lambda e: e.tensor_tensor(COLS["a_gu"][:, 0:1], COLS["a_gate"][:, 0:1],
                                   COLS["c0"][:, 0:1], ALU.mult))
    recip_into("a_d", "sd", "gs_d")

    xq_rd = [0, 0]
    xn_rd = 0
    for tt in range(NT):
        xb = TMP[1]
        dv = load(xb[:], x_h[tt * P:(tt + 1) * P, :],
                  w=[(p.sem["vector"], V()), (p.sem["scalar"], S()),
                     (p.sem["tensor"], max(xq_rd[0], xq_rd[1])),
                     (p.sem["gpsimd"], p.sem["gpsimd"].v)])
        vopc(lambda e, xb=xb: e.reduce_sum(COLS["c0"][:, 0:1], xb[:], AX.X),
             w=[(p.dma["sync"], dv)])
        sop(lambda e, xb=xb: e.activation(TMP[0][:], xb[:], AF.Square),
            w=[(p.sem["vector"], V())])
        vopc(lambda e: e.reduce_sum(COLS["c1"][:, 0:1], TMP[0][:], AX.X),
             w=[(p.sem["scalar"], S())])
        vopc(lambda e: e.tensor_scalar(COLS["c0"][:, 0:1], COLS["c0"][:, 0:1],
                                       1.0 / D, None, ALU.mult))
        vopc(lambda e: e.tensor_tensor(COLS["c2"][:, 0:1], COLS["c0"][:, 0:1],
                                       COLS["c0"][:, 0:1], ALU.mult))
        vopc(lambda e: e.scalar_tensor_tensor(COLS["c1"][:, 0:1], COLS["c1"][:, 0:1],
                                              1.0 / D, COLS["c2"][:, 0:1],
                                              ALU.mult, ALU.subtract))
        vopc(lambda e: e.tensor_scalar(COLS["c1"][:, 0:1], COLS["c1"][:, 0:1],
                                       1e-5, None, ALU.add))
        sopc(lambda e: e.activation(COLS["c2"][:, 0:1], COLS["c1"][:, 0:1], AF.Ln),
             w=[(p.sem["vector"], V())])
        sopc(lambda e: e.activation(COLS["c2"][:, 0:1], COLS["c2"][:, 0:1],
                                    AF.Exp, scale=-0.5))
        vopc(lambda e: e.tensor_tensor(COLS["c3"][:, 0:1], COLS["c0"][:, 0:1],
                                       COLS["c2"][:, 0:1], ALU.mult),
             w=[(p.sem["scalar"], S())])
        vopc(lambda e: e.tensor_scalar(COLS["c3"][:, 0:1], COLS["c3"][:, 0:1],
                                       -1.0, None, ALU.mult))
        sopc(lambda e, xb=xb: e.activation(TMP[4][:], xb[:], AF.Identity,
                                           scale=COLS["c2"][:, 0:1],
                                           bias=COLS["c3"][:, 0:1]),
             w=[(p.sem["vector"], V())])
        sv_xn = S()
        xn_rd, _ = pe_xpose([(xn_fm[:, kd * T + tt * P: kd * T + (tt + 1) * P],
                              TMP[4][:, kd * P:(kd + 1) * P]) for kd in range(8)],
                            PS[0], w=[(p.sem["scalar"], sv_xn),
                                      (p.dma["gpsimd"], d_attnw)])
        amax_to(TMP[4][:], 64, 0, ws=[(p.sem["scalar"], sv_xn)])
        scale_pipe(None, 64, "k_x", "lngs_x", None, None,
                   wv=[(p.sem["vector"], V())])
        qb = (TMP[1], TMPX)[tt % 2]
        gq = quant_hot(TMP[4][:], qb[:], 64, SC8[:, 0:64], G8[:, 0:64],
                       ws=[(p.sem["vector"], V()), (p.sem["tensor"], xn_rd),
                           (p.sem["scalar"], S())],
                       clip=True,
                       out_ws=([(p.sem["tensor"], xq_rd[tt % 2])]
                               if xq_rd[tt % 2] else []))
        xq_rd[tt % 2], _ = pe_xpose(
            [(xq_fm[:, kd * T + tt * P: kd * T + (tt + 1) * P],
              qb[:, kd * P:(kd + 1) * P]) for kd in range(8)],
            PS[1], w=[(p.sem["gpsimd"], gq)])
    s_xnfm = S()

    if stop_after == 'A':
        return _finish(nc, p, st, debug_taps, scr_sig, scr_g, scr_sg, scr_sout,
                       scr_gp, xq_fm, xn_fm, store, V, DS, DG, T, D)
    vop(lambda e: e.memset(rbf[0:32, :], 1.0))
    v_rones = V()
    for q4 in range(4):
        for kd in range(8):
            top(lambda e, q4=q4, kd=kd: e.matmul(
                PS[0][0:16, 0:512], wgk1_bf[:, kd * 16:(kd + 1) * 16],
                xn_fm[:, kd * T + q4 * 512: kd * T + (q4 + 1) * 512],
                start=(kd == 0), stop=(kd == 7)),
                w=[(p.sem["scalar"], s_xnfm), (p.sem["vector"], V())]
                if kd == 0 else ())
        vop(lambda e, q4=q4: e.tensor_copy(rbf[0:16, q4 * 512:(q4 + 1) * 512],
                                           PS[0][0:16, 0:512]),
            w=[(p.sem["tensor"], TE())])
    v_rbf = V()
    for kd in range(8):
        load(WG[:, kd * D:(kd + 1) * D], wg_h[kd * P:(kd + 1) * P, :], q="gpsimd")
    d_wg = DG()

    for tt in range(NT):
        tv0 = TE()
        for half in range(2):
            top(lambda e, tt=tt, half=half: e.matmul(
                PS[half][:, 0:512], rbf[0:17, tt * P:(tt + 1) * P],
                w2b_bf[0:17, half * 512:(half + 1) * 512], start=True, stop=True),
                w=[(p.sem["vector"], max(v_rbf, V())), (p.sem["scalar"], S())]
                if half == 0 else ())
        for half in range(2):
            for kd in range(8):
                top(lambda e, tt=tt, half=half, kd=kd: e.matmul(
                    PS[2 + half][:, 0:512], xn_fm[:, kd * T + tt * P:
                                                  kd * T + (tt + 1) * P],
                    WG[:, kd * D + half * 512: kd * D + (half + 1) * 512],
                    start=(kd == 0), stop=(kd == 7)),
                    w=[(p.dma["gpsimd"], d_wg)] if (tt == 0 and half == 0
                                                    and kd == 0) else ())
        tv = TE()
        sop(lambda e: e.activation(TMP[0][:, 0:512], PS[0][:, 0:512], AF.Sigmoid),
            w=[(p.sem["tensor"], tv), (p.dma["sync"], DS())])
        sop(lambda e: e.activation(TMP[0][:, 512:D], PS[1][:, 0:512], AF.Sigmoid))
        sop(lambda e: e.activation(TMP[1][:, 0:512], PS[2][:, 0:512], AF.Sigmoid))
        sop(lambda e: e.activation(TMP[1][:, 512:D], PS[3][:, 0:512], AF.Sigmoid))
        sv = S()
        store(scr_sig[tt * P:(tt + 1) * P, :], TMP[0][:],
              w=[(p.sem["scalar"], sv)])
        vop(lambda e: e.tensor_tensor(TMP[2][:, 0:512], PS[2][:, 0:512],
                                      TMP[1][:, 0:512], ALU.mult),
            w=[(p.sem["scalar"], sv), (p.dma["sync"], DS())])
        vop(lambda e: e.tensor_tensor(TMP[2][:, 512:D], PS[3][:, 0:512],
                                      TMP[1][:, 512:D], ALU.mult))
        vv = V()
        store(scr_sg[tt * P:(tt + 1) * P, :], TMP[2][:], w=[(p.sem["vector"], vv)])
    for tt in range(NT):
        dv = load(TMP[3][:], scr_sig[tt * P:(tt + 1) * P, :],
                  w=[(p.sem["scalar"], S())])
        sop(lambda e: e.activation(TMP[0][:], TMP[3][:], AF.Ln),
            w=[(p.dma["sync"], max(dv, DS()))])
        sv = S()
        store(scr_g[tt * P:(tt + 1) * P, :], TMP[0][:], w=[(p.sem["scalar"], sv)])
    d_gates = DS()

    if stop_after == 'B':
        return _finish(nc, p, st, debug_taps, scr_sig, scr_g, scr_sg, scr_sout,
                       scr_gp, xq_fm, xn_fm, store, V, DS, DG, T, D)
    for kd in range(8):
        load(WO[:, kd * D:(kd + 1) * D], wout_h[kd * P:(kd + 1) * P, :], q="gpsimd",
             w=[(p.sem["tensor"], TE())])
    d_wo = DG()
    vop(lambda e: e.memset(stateT[:, :], 0.0))
    v_state = V()
    d_dup = DS()
    v_wout_prev = 0
    v_oevac_prev = 0
    v_vt_prev = 0

    for c in range(NT):
        tc = c * P
        dv_g = load(gch[:], scr_g[tc:tc + P, :],
                    w=[(p.dma["sync"], d_gates), (p.sem["tensor"], TE()),
                       (p.sem["vector"], v_vt_prev if c > 0 else 0)])
        for half in range(2):
            for kd in range(8):
                top(lambda e, half=half, kd=kd, tc=tc: e.matmul(
                    PS[half][:, 0:512],
                    xn_fm[:, kd * T + tc: kd * T + tc + P],
                    WV[:, kd * D + half * 512: kd * D + (half + 1) * 512],
                    start=(kd == 0), stop=(kd == 7)),
                    w=[(p.sem["vector"], v_state), (p.dma["sync"], d_dup)]
                    if (half == 0 and kd == 0) else ())
        tv = TE()
        vop(lambda e: e.tensor_copy(vtm[:, 0:512], PS[0][:, 0:512]),
            w=[(p.sem["tensor"], tv)])
        vop(lambda e: e.tensor_copy(vtm[:, 512:D], PS[1][:, 0:512]))
        v_vtm = V()
        for oc in range(8):
            for kd in range(8):
                top(lambda e, oc=oc, kd=kd, tc=tc: e.matmul(
                    PS[4 + oc // 4][:, (oc % 4) * P:(oc % 4 + 1) * P],
                    WK[:, kd * D + oc * P: kd * D + (oc + 1) * P],
                    xn_fm[:, kd * T + tc: kd * T + tc + P],
                    start=(kd == 0), stop=(kd == 7)),
                    w=[(p.sem["vector"], v_oevac_prev)]
                    if (oc == 0 and kd == 0 and v_oevac_prev) else ())
        tv = TE()
        vop(lambda e: e.tensor_copy(kfm[:, 0:512], PS[4][:, 0:512]),
            w=[(p.sem["tensor"], tv)])
        vop(lambda e: e.tensor_copy(kfm[:, 512:D], PS[5][:, 0:512]))
        v_kfm = V()
        d_ktm = xbars([(ktm[:, oc * P:(oc + 1) * P], kfm[:, oc * P:(oc + 1) * P])
                       for oc in range(8)], w=[(p.sem["vector"], v_kfm)])
        for half in range(2):
            top(lambda e, half=half: e.matmul(
                PS[6 + half][:, 0:512], tri16[:, :],
                gch[:, half * 512:(half + 1) * 512], start=True, stop=True),
                w=[(p.dma["sync"], dv_g)] if half == 0 else ())
        tv = TE()
        sop(lambda e: e.activation(A_c[:, 0:512], PS[6][:, 0:512], AF.Exp),
            w=[(p.sem["tensor"], tv)])
        sop(lambda e: e.activation(A_c[:, 512:D], PS[7][:, 0:512], AF.Exp))
        sop(lambda e: e.activation(Ainv[:, 0:512], PS[6][:, 0:512], AF.Exp,
                                   scale=-1.0))
        sop(lambda e: e.activation(Ainv[:, 512:D], PS[7][:, 0:512], AF.Exp,
                                   scale=-1.0))
        sv_gates = S()
        for half in range(2):
            top(lambda e, half=half: e.matmul(
                PS[half][:, 0:512], oneh128[:, :],
                A_c[:, half * 512:(half + 1) * 512], start=True, stop=True),
                w=[(p.sem["scalar"], sv_gates), (p.sem["vector"], v_vtm)]
                if half == 0 else ())
        tv_acb = TE()
        vop(lambda e: e.tensor_tensor(vt_t[:, 0:512], vtm[:, 0:512],
                                      Ainv[:, 0:512], ALU.mult),
            w=[(p.sem["scalar"], sv_gates)])
        vop(lambda e: e.tensor_tensor(vt_t[:, 512:D], vtm[:, 512:D],
                                      Ainv[:, 512:D], ALU.mult))
        v_vt = V()
        v_vt_prev = v_vt
        for oc in range(8):
            for kd in range(8):
                top(lambda e, oc=oc, kd=kd, tc=tc: e.matmul(
                    PS[2 + oc // 4][:, (oc % 4) * P:(oc % 4 + 1) * P],
                    WQ[:, kd * D + oc * P: kd * D + (oc + 1) * P],
                    xn_fm[:, kd * T + tc: kd * T + tc + P],
                    start=(kd == 0), stop=(kd == 7)),
                    w=[(p.sem["vector"], v_wout_prev)]
                    if (oc == 0 and kd == 0 and v_wout_prev) else ())
        tv = TE()
        vop(lambda e: e.tensor_copy(qfm[:, 0:512], PS[2][:, 0:512]),
            w=[(p.sem["tensor"], tv)])
        vop(lambda e: e.tensor_copy(qfm[:, 512:D], PS[3][:, 0:512]))
        v_qfm = V()
        msk_done = [0] * 4
        intra_tv = [0] * 4
        for h in range(H):
            sl = h % 4
            rb = 64 * (h % 2)
            cb = (h // 2) * P
            w_s = []
            if h == 0:
                w_s.append((p.sem["vector"], max(v_kfm, v_qfm)))
            if msk_done[sl]:
                w_s.append((p.sem["vector"], msk_done[sl]))
            tv_s = top(lambda e, sl=sl, rb=rb, cb=cb: e.matmul(
                PS[2][:, sl * P:(sl + 1) * P],
                kfm[rb:rb + 64, cb:cb + P], qfm[rb:rb + 64, cb:cb + P],
                start=True, stop=True), w=w_s)
            w_m = [(p.sem["tensor"], tv_s), (p.sem["vector"], v_vt)]
            if intra_tv[sl]:
                w_m = [(p.sem["tensor"], max(tv_s, intra_tv[sl]))]
            v_m = vop(lambda e, sl=sl: e.tensor_tensor(
                mskt[:, sl * P:(sl + 1) * P], PS[2][:, sl * P:(sl + 1) * P],
                cmask[:, :], ALU.mult), w=w_m)
            msk_done[sl] = v_m
            ob = PS[4] if h < 8 else PS[5]
            oc0 = 64 * (h % 8)
            w_o = ([(p.sem["vector"], v_state)] + (
                [(p.dma["sync"], d_dup)] if d_dup else [])) if h == 0 else []
            top(lambda e, ob=ob, oc0=oc0, rb=rb, cb=cb, h=h: e.matmul(
                ob[:, oc0:oc0 + 64], qfm[rb:rb + 64, cb:cb + P],
                stateT[rb:rb + 64, 64 * h:64 * h + 64],
                start=True, stop=False), w=w_o)
            intra_tv[sl] = top(
                lambda e, ob=ob, oc0=oc0, h=h, sl=sl: e.matmul(
                    ob[:, oc0:oc0 + 64], mskt[:, sl * P:(sl + 1) * P],
                    vt_t[:, 64 * h:64 * h + 64],
                    start=False, stop=True),
                w=[(p.sem["vector"], max(v_m, v_vt))])
            sb_ = PS[6] if h < 8 else PS[7]
            top(lambda e, sb_=sb_, oc0=oc0, h=h: e.matmul(
                sb_[0:64, oc0:oc0 + 64], ktm[:, 64 * h:64 * h + 64],
                vt_t[:, 64 * h:64 * h + 64], start=True, stop=True),
                w=[(p.dma["sync"], d_ktm), (p.sem["scalar"], sv_gates)]
                if h == 0 else ())
        tv_heads = TE()
        vop(lambda e: e.tensor_tensor(TMP[0][0:64, 0:512], PS[6][0:64, 0:512],
                                      stateT[0:64, 0:512], ALU.add),
            w=[(p.sem["tensor"], tv_heads)])
        vop(lambda e: e.tensor_tensor(TMP[0][0:64, 512:D], PS[7][0:64, 0:512],
                                      stateT[0:64, 512:D], ALU.add))
        vop(lambda e: e.tensor_tensor(stateT[0:64, 0:512], TMP[0][0:64, 0:512],
                                      PS[0][0:64, 0:512], ALU.mult))
        vop(lambda e: e.tensor_tensor(stateT[0:64, 512:D], TMP[0][0:64, 512:D],
                                      PS[1][0:64, 0:512], ALU.mult))
        v_state = V()
        d_dup = p.dma_op("sync", lambda e: [e.dma_start(
            out=stateT[64:128, :], in_=stateT[0:64, :])],
            waits=[(p.sem["vector"], v_state)])
        vop(lambda e: e.tensor_tensor(TMP[0][:, 0:512], PS[4][:, 0:512],
                                      A_c[:, 0:512], ALU.mult))
        vop(lambda e: e.tensor_tensor(TMP[0][:, 512:D], PS[5][:, 0:512],
                                      A_c[:, 512:D], ALU.mult))
        v_oevac_prev = V()
        dv_sg = load(sgch[:], scr_sg[tc:tc + P, :],
                     w=[(p.sem["tensor"], tv_heads)])
        vopc(lambda e: e.reduce_sum(COLS["c0"][:, 0:1], TMP[0][:], AX.X))
        sopc(lambda e: e.activation(TMP[1][:], TMP[0][:], AF.Square),
            w=[(p.sem["vector"], V())])
        vopc(lambda e: e.reduce_sum(COLS["c1"][:, 0:1], TMP[1][:], AX.X),
            w=[(p.sem["scalar"], S())])
        vopc(lambda e: e.tensor_scalar(COLS["c0"][:, 0:1], COLS["c0"][:, 0:1],
                                      1.0 / D, None, ALU.mult))
        vopc(lambda e: e.tensor_tensor(COLS["c2"][:, 0:1], COLS["c0"][:, 0:1],
                                      COLS["c0"][:, 0:1], ALU.mult))
        vopc(lambda e: e.scalar_tensor_tensor(COLS["c1"][:, 0:1], COLS["c1"][:, 0:1],
                                             1.0 / D, COLS["c2"][:, 0:1],
                                             ALU.mult, ALU.subtract))
        vopc(lambda e: e.tensor_scalar(COLS["c1"][:, 0:1], COLS["c1"][:, 0:1],
                                      1e-5, None, ALU.add))
        sopc(lambda e: e.activation(COLS["c2"][:, 0:1], COLS["c1"][:, 0:1], AF.Ln),
            w=[(p.sem["vector"], V())])
        sopc(lambda e: e.activation(COLS["c2"][:, 0:1], COLS["c2"][:, 0:1],
                                   AF.Exp, scale=-0.5))
        vopc(lambda e: e.tensor_tensor(COLS["c3"][:, 0:1], COLS["c0"][:, 0:1],
                                      COLS["c2"][:, 0:1], ALU.mult),
            w=[(p.sem["scalar"], S())])
        vopc(lambda e: e.tensor_scalar(COLS["c3"][:, 0:1], COLS["c3"][:, 0:1],
                                      -1.0, None, ALU.mult))
        sopc(lambda e: e.activation(TMP[1][:], TMP[0][:], AF.Identity,
                                   scale=COLS["c2"][:, 0:1],
                                   bias=COLS["c3"][:, 0:1]),
            w=[(p.sem["vector"], V())])
        vop(lambda e: e.tensor_tensor(TMP[2][:], TMP[1][:], gamma_bc[:], ALU.mult),
            w=[(p.sem["scalar"], S())])
        vop(lambda e: e.tensor_tensor(TMP[1][:], TMP[2][:], beta_bc[:], ALU.add))
        vop(lambda e: e.tensor_tensor(TMP[2][:], TMP[1][:], sgch[:], ALU.mult),
            w=[(p.dma["sync"], dv_sg)])
        vop(lambda e: e.tensor_copy(xnbf[:], TMP[2][:]),
            w=[(p.dma["sync"], DS())])
        v_spre = V()
        d_sfm = xbars([(qfm[:, oc * P:(oc + 1) * P], xnbf[:, oc * P:(oc + 1) * P])
                       for oc in range(8)], w=[(p.sem["vector"], v_spre)])
        for half in range(2):
            for kd in range(8):
                top(lambda e, half=half, kd=kd: e.matmul(
                    PS[2 + half][:, 0:512], qfm[:, kd * P:(kd + 1) * P],
                    WO[:, kd * D + half * 512: kd * D + (half + 1) * 512],
                    start=(kd == 0), stop=(kd == 7)),
                    w=[(p.dma["sync"], d_sfm), (p.dma["gpsimd"], d_wo)]
                    if (half == 0 and kd == 0) else ())
        tv = TE()
        vop(lambda e: e.tensor_copy(TMP[3][:, 0:512], PS[2][:, 0:512]),
            w=[(p.sem["tensor"], tv), (p.dma["sync"], DS())])
        vop(lambda e: e.tensor_copy(TMP[3][:, 512:D], PS[3][:, 0:512]))
        v_wout_prev = V()
        vv = V()
        store(scr_sout[tc:tc + P, :], TMP[3][:], w=[(p.sem["vector"], vv)])
    d_sout_all = DS()

    if stop_after == 'C':
        return _finish(nc, p, st, debug_taps, scr_sig, scr_g, scr_sg, scr_sout,
                       scr_gp, xq_fm, xn_fm, store, V, DS, DG, T, D)
    dq_rd = [0, 0]
    for half in range(2):
        for ic in range(8):
            I0 = half * 2048 + ic * 256
            dvs = []
            for wi, src_h in enumerate((gatew_h, upw_h)):
                for sub in range(2):
                    buf = wstr[2 * wi + sub]
                    dvs.append(load(buf[:], src_h[I0 + sub * P: I0 + (sub + 1) * P, :],
                                    w=[(p.sem["vector"], V())]))
            for j in range(4):
                amax_to(wstr[j][:], 64, 64 * j, ws=[(p.dma["sync"], dvs[j])])
            s_g = scale_pipe(None, 128, "k_g", "lngs_g", None, None,
                             wv=[(p.sem["vector"], V())], off=0, W=128)
            s_u = scale_pipe(None, 128, "k_u", "lngs_u", None, None,
                             wv=[(p.sem["scalar"], s_g)], off=128, W=128)
            s_pair = (s_g, s_u)
            for wi, dstq in enumerate((gwq, uwq)):
                for sub in range(2):
                    j = 2 * wi + sub
                    qb = (TMP[1], TMPX)[j % 2]
                    gq = quant_hot(wstr[j][:], qb[:], 64,
                                   SC8[:, 64 * j:64 * j + 64],
                                   G8[:, 64 * j:64 * j + 64],
                                   ws=[(p.sem["vector"], V()),
                                       (p.sem["scalar"], s_pair[wi])],
                                   clip=False,
                                   out_ws=([(p.sem["tensor"], dq_rd[j % 2])]
                                           if dq_rd[j % 2] else []))
                    dq_rd[j % 2], _ = pe_xpose(
                        [(dstq[:, kd * 256 + sub * P: kd * 256 + (sub + 1) * P],
                          qb[:, kd * P:(kd + 1) * P]) for kd in range(8)],
                        PS[2], w=[(p.sem["gpsimd"], gq)])
            s_wslice = S()
            ev_done = [0, 0]
            for tt in range(NT):
                par = tt % 2
                pg, pu = (PS[0], PS[1]) if par == 0 else (PS[4], PS[5])
                w0 = [(p.sem["scalar"], s_wslice)]
                if ev_done[par]:
                    w0.append((p.sem["vector"], ev_done[par]))
                elif tt == 0:
                    w0.append((p.sem["vector"], V()))
                for kd in range(8):
                    top(lambda e, kd=kd, tt=tt, pg=pg: e.matmul(
                        pg[:, 0:256], xq_fm[:, kd * T + tt * P:
                                            kd * T + (tt + 1) * P],
                        gwq[:, kd * 256:(kd + 1) * 256],
                        start=(kd == 0), stop=(kd == 7)),
                        w=w0 if kd == 0 else ())
                for kd in range(8):
                    top(lambda e, kd=kd, tt=tt, pu=pu: e.matmul(
                        pu[:, 0:256], xq_fm[:, kd * T + tt * P:
                                            kd * T + (tt + 1) * P],
                        uwq[:, kd * 256:(kd + 1) * 256],
                        start=(kd == 0), stop=(kd == 7)))
                tv = TE()
                t0s, t1s = (TMP[0], TMP[1]) if par == 0 else (TMP[2], TMP[3])
                sop(lambda e, pg=pg, t0s=t0s: e.activation(
                    t0s[:, 0:256], pg[:, 0:256],
                    AF.Sigmoid, scale=COLS["a_gate"][:, 0:1]),
                    w=[(p.sem["tensor"], tv)])
                sv = S()
                vop(lambda e, pg=pg, t0s=t0s, t1s=t1s: e.tensor_tensor(
                    t1s[:, 0:256], pg[:, 0:256], t0s[:, 0:256], ALU.mult),
                    w=[(p.sem["scalar"], sv),
                       (p.sem["tensor"], max(dq_rd[0], dq_rd[1]))])
                vop(lambda e, tt=tt, pu=pu, t1s=t1s: e.scalar_tensor_tensor(
                    hid_c[:, tt * 256:(tt + 1) * 256], pu[:, 0:256],
                    COLS["a_gu"][:, 0:1], t1s[:, 0:256], ALU.mult, ALU.mult))
                ev_done[par] = V()
            for slab in range(4):
                amax_to(hid_c[:, slab * D:(slab + 1) * D], 64, 64 * slab,
                        ws=[(p.sem["vector"], V())] if slab == 0 else ())
            s_hid = scale_pipe(None, 256, "k_h", "lngs_h", None, None,
                               wv=[(p.sem["vector"], V())])
            for slab in range(4):
                qb = (TMP[1], TMPX)[slab % 2]
                src = hid_c[:, slab * D:(slab + 1) * D]
                gq = quant_hot(src, qb[:], 64,
                               SC8[:, 64 * slab:64 * slab + 64],
                               G8[:, 64 * slab:64 * slab + 64],
                               ws=[(p.sem["vector"], V()),
                                   (p.sem["scalar"], s_hid)], clip=True,
                               out_ws=([(p.sem["tensor"], dq_rd[slab % 2])]
                                       if dq_rd[slab % 2] else []))
                prs = []
                for j in range(4):
                    tt = slab * 4 + j
                    for ib in range(2):
                        ki = ic * 2 + ib
                        prs.append((hq_fm[:, ki * T + tt * P: ki * T + (tt + 1) * P],
                                    qb[:, j * 256 + ib * P:
                                        j * 256 + (ib + 1) * P]))
                dq_rd[slab % 2], _ = pe_xpose(prs, PS[2],
                                              w=[(p.sem["gpsimd"], gq)])
        s_hq = S()
        for i in range(8):
            dvs = []
            for sub in range(2):
                dvs.append(load(wstr[sub][:], downw_h[i * P:(i + 1) * P,
                                                      half * 2048 + sub * D:
                                                      half * 2048 + (sub + 1) * D],
                                w=[(p.sem["vector"], V())]))
            for sub in range(2):
                amax_to(wstr[sub][:], 64, 64 * sub, ws=[(p.dma["sync"], dvs[sub])])
            s_dn = scale_pipe(None, 128, "k_d", "lngs_d", None, None,
                              wv=[(p.sem["vector"], V())], off=0, W=128)
            for sub in range(2):
                qb = (TMP[1], TMPX)[sub]
                gq = quant_hot(wstr[sub][:], qb[:], 64,
                               SC8[:, 64 * sub:64 * sub + 64],
                               G8[:, 64 * sub:64 * sub + 64],
                               ws=[(p.sem["vector"], V()),
                                   (p.sem["scalar"], s_dn)], clip=False,
                               out_ws=([(p.sem["tensor"], dq_rd[sub])]
                                       if dq_rd[sub] else []))
                dq_rd[sub], _ = pe_xpose(
                    [(dwq[:, (sub * 8 + kk) * D + i * P:
                          (sub * 8 + kk) * D + (i + 1) * P],
                      qb[:, kk * P:(kk + 1) * P]) for kk in range(8)],
                    PS[2], w=[(p.sem["gpsimd"], gq)])
        s_dwq = S()
        ev2 = [0, 0]
        for tt in range(NT):
            par = tt % 2
            pa, pb = (PS[2], PS[3]) if par == 0 else (PS[6], PS[7])
            w0 = [(p.sem["scalar"], max(s_dwq, s_hq))]
            w0.append((p.sem["vector"], ev2[par] if ev2[par] else V()))
            for ki in range(16):
                for half2 in range(2):
                    top(lambda e, ki=ki, half2=half2, tt=tt, pa=pa, pb=pb: e.matmul(
                        (pa, pb)[half2][:, 0:512],
                        hq_fm[:, ki * T + tt * P: ki * T + (tt + 1) * P],
                        dwq[:, ki * D + half2 * 512: ki * D + (half2 + 1) * 512],
                        start=(ki == 0), stop=(ki == 15)),
                        w=w0 if (ki == 0 and half2 == 0) else ())
            tv = TE()
            t0s = TMP[0] if par == 0 else TMP[2]
            t1s = TMP[1] if par == 0 else TMP[3]
            if half == 0:
                vop(lambda e, pa=pa, t0s=t0s: e.tensor_copy(
                    t0s[:, 0:512], pa[:, 0:512]),
                    w=[(p.sem["tensor"], tv), (p.dma["sync"], DS())])
                vop(lambda e, pb=pb, t0s=t0s: e.tensor_copy(
                    t0s[:, 512:D], pb[:, 0:512]))
                vv = V()
                store(scr_gp[tt * P:(tt + 1) * P, :], t0s[:],
                      w=[(p.sem["vector"], vv)])
            else:
                dv = load(t1s[:], scr_gp[tt * P:(tt + 1) * P, :],
                          w=[(p.sem["vector"], ev2[par] if ev2[par] else V()),
                             (p.sem["tensor"], max(dq_rd[0], dq_rd[1]))])
                vop(lambda e, pa=pa, t0s=t0s, t1s=t1s: e.tensor_tensor(
                    t0s[:, 0:512], pa[:, 0:512], t1s[:, 0:512], ALU.add),
                    w=[(p.sem["tensor"], tv), (p.dma["sync"], max(dv, DS()))])
                vop(lambda e, pb=pb, t0s=t0s, t1s=t1s: e.tensor_tensor(
                    t0s[:, 512:D], pb[:, 0:512], t1s[:, 512:D], ALU.add))
                vv = V()
                store(scr_gp[tt * P:(tt + 1) * P, :], t0s[:],
                      w=[(p.sem["vector"], vv)])
            ev2[par] = V()
    d_gf = DS()

    for tt in range(NT):
        xb = wstr[0]
        dv1 = load(xb[:], x_h[tt * P:(tt + 1) * P, :],
                   w=[(p.sem["vector"], V()), (p.dma["sync"], d_gf)])
        dv2 = load(TMP[3][:], scr_gp[tt * P:(tt + 1) * P, :])
        dv3 = p.dma_op("gpsimd", lambda e, tt=tt: [e.dma_start(
            out=TMP[4][:], in_=scr_sout[tt * P:(tt + 1) * P, :])],
            waits=[(p.sem["vector"], V())])
        sop(lambda e: e.activation(TMP[0][:], TMP[3][:], AF.Sigmoid,
                                   scale=COLS["a_d"][:, 0:1]),
            w=[(p.dma["sync"], dv2)])
        sv = S()
        vop(lambda e: e.tensor_tensor(TMP[1][:], TMP[3][:], TMP[0][:], ALU.mult),
            w=[(p.sem["scalar"], sv)])
        vop(lambda e: e.scalar_tensor_tensor(TMP[2][:], TMP[1][:],
                                             COLS["a_d"][:, 0:1], TMP[4][:],
                                             ALU.mult, ALU.mult),
            w=[(p.dma["gpsimd"], dv3)])
        vop(lambda e, xb=xb: e.tensor_tensor(TMP[1][:], TMP[2][:], xb[:], ALU.add),
            w=[(p.dma["sync"], dv1)])
        vv = V()
        store(y_h[tt * P:(tt + 1) * P, :], TMP[1][:], w=[(p.sem["vector"], vv)])
    return _finish(nc, p, st, debug_taps, scr_sig, scr_g, scr_sg, scr_sout,
                   scr_gp, xq_fm, xn_fm, store, V, DS, DG, T, D)


def _finish(nc, p, st, debug_taps, scr_sig, scr_g, scr_sg, scr_sout, scr_gp,
            xq_fm, xn_fm, store, V, DS, DG, T, D):
    if debug_taps:
        for nm, scr in (("dbg_sig", scr_sig), ("dbg_g", scr_g), ("dbg_sg", scr_sg),
                        ("dbg_sout", scr_sout), ("dbg_gp", scr_gp)):
            dst = nc.dram_tensor(nm, [T, D], F32, kind="ExternalOutput").ap()
            store(dst, scr, w=[(p.sem["vector"], V())])
        dbx = nc.dram_tensor("dbg_xqfm", [P, 8 * T], BF16, kind="ExternalOutput").ap()
        store(dbx, xq_fm[:], w=[(p.sem["vector"], V())])
        dbn = nc.dram_tensor("dbg_xnfm", [P, 8 * T], BF16, kind="ExternalOutput").ap()
        store(dbn, xn_fm[:], w=[(p.sem["vector"], V())])
    d_final = DS()
    p.op("sync", lambda e: e.nop(), waits=[(p.dma["sync"], d_final),
                                           (p.dma["gpsimd"], DG()),
                                           (p.dma["scalar"], p.dma["scalar"].v)],
         inc=False)
    with nc.Block() as block:
        p.emit(block)
    st.close()
    return nc


_NC_CACHE = None


def _get_nc():
    global _NC_CACHE
    if _NC_CACHE is None:
        _NC_CACHE = build_program()
    return _NC_CACHE


def kernel(L0_attn_x, L0_attn_Wq, L0_attn_Wk, L0_attn_Wv, L0_attn_Wg,
           L0_attn_Wgk1, L0_attn_Wgk2, L0_attn_bgk2, L0_attn_gamma,
           L0_attn_beta, L0_attn_Wout, L0_ffn_gate_w, L0_ffn_up_w,
           L0_ffn_down_w, L0_ffn_s_in, L0_ffn_s_in_down):
    f32 = np.float32
    x = np.ascontiguousarray(L0_attn_x, dtype=f32)
    B = x.shape[0]
    w2b = np.ascontiguousarray(
        np.concatenate([np.asarray(L0_attn_Wgk2, f32),
                        np.asarray(L0_attn_bgk2, f32)[None, :]], axis=0))
    gamma_bc = np.ascontiguousarray(
        np.broadcast_to(np.asarray(L0_attn_gamma, f32)[None, :], (P, D)))
    beta_bc = np.ascontiguousarray(
        np.broadcast_to(np.asarray(L0_attn_beta, f32)[None, :], (P, D)))
    sx_col = np.full((P, 1), np.asarray(L0_ffn_s_in, f32)[0], f32)
    sd_col = np.full((P, 1), np.asarray(L0_ffn_s_in_down, f32)[0], f32)
    sidx = np.arange(P)
    tri16 = ((sidx[:, None] <= sidx[None, :]).astype(f32) / f32(16.0))
    cmask = (sidx[:, None] <= sidx[None, :]).astype(f32)
    oneh128 = np.zeros((P, P), f32); oneh128[P - 1, :] = 1.0
    onecol = np.ones((1, P), f32)
    identbf = np.eye(P, dtype=f32)

    shared = dict(
        wq=np.ascontiguousarray(L0_attn_Wq, f32),
        wk=np.ascontiguousarray(L0_attn_Wk, f32),
        wv=np.ascontiguousarray(L0_attn_Wv, f32),
        wg=np.ascontiguousarray(L0_attn_Wg, f32),
        wout=np.ascontiguousarray(L0_attn_Wout, f32),
        wgk1=np.ascontiguousarray(L0_attn_Wgk1, f32),
        w2b=w2b, gamma_bc=gamma_bc, beta_bc=beta_bc,
        sx_col=sx_col, sd_col=sd_col,
        gatew=np.ascontiguousarray(L0_ffn_gate_w, f32),
        upw=np.ascontiguousarray(L0_ffn_up_w, f32),
        downw=np.ascontiguousarray(L0_ffn_down_w, f32),
        tri16=tri16, cmask=cmask, oneh128=oneh128, onecol=onecol,
        identbf=identbf,
    )
    in_maps = [dict(shared, x=np.ascontiguousarray(x[b])) for b in range(B)]
    nc = _get_nc()
    res = run_bass_kernel_spmd(nc, in_maps, core_ids=list(range(B)))
    out = np.stack([np.asarray(res.results[b]["y"]) for b in range(B)], axis=0)
    return out.astype(np.float32)
